# version 01
# baseline (speedup 1.0000x reference)
import math
import numpy as np
from contextlib import ExitStack

import concourse.bass as bass
import concourse.mybir as mybir
from concourse.bass_utils import run_bass_kernel_spmd

F32 = mybir.dt.float32
BF16 = mybir.dt.bfloat16
I32 = mybir.dt.int32
AF = mybir.ActivationFunctionType
ALU = mybir.AluOpType
AX = mybir.AxisListType.X

D = 2048
KC = D // 128
NMLP = 16384
NH = 16
HD = 128
ROPE = 64
LAT = 512
LC = LAT // 128
B, S = 2, 4096
T = B * S
NCORES = 8
TC = T // NCORES
TT = TC // 128
SCALE = float(HD) ** -0.5
GRP = NCORES // B
SEQ = S
KB = SEQ // TC

GATE_SH = NMLP // NCORES
DOWN_SH = D // NCORES
ATTN_ROWS = D + LAT + D
ATTN_SH = ATTN_ROWS // NCORES
KVB_SH = (2 * D) // NCORES
A_BLK = 2 * GATE_SH + ATTN_SH
A_ATTN_OFF = 2 * GATE_SH
C_DOWN_ROWS = DOWN_SH
C_KVB_ROWS = (KVB_SH * LAT) // NMLP
C_BLK = C_DOWN_ROWS + C_KVB_ROWS

_DEV_STOP = "full"
_DEV_DUMP = False


def _yarn_cos_sin():
    ROPE_DIM = 64
    MAX_POS = 131072
    ORIG_POS = 8192
    BASE = 500000.0
    BETA_FAST, BETA_SLOW = (32.0, 1.0)
    scaling = MAX_POS / ORIG_POS
    pos_freqs = BASE ** (np.arange(0, ROPE_DIM, 2, dtype=np.float64) / ROPE_DIM)
    extrap = 1.0 / pos_freqs
    interp = 1.0 / (scaling * pos_freqs)
    low = max(math.floor(ROPE_DIM * math.log(ORIG_POS / (BETA_FAST * 2 * math.pi)) / (2 * math.log(BASE))), 0)
    high = min(math.ceil(ROPE_DIM * math.log(ORIG_POS / (BETA_SLOW * 2 * math.pi)) / (2 * math.log(BASE))), ROPE_DIM - 1)
    i = np.arange(ROPE_DIM // 2, dtype=np.float64)
    smooth = np.clip((i - low) / max(high - low, 1), 0.0, 1.0)
    inv_freq = np.asarray((1.0 - smooth) * interp + smooth * extrap, dtype=np.float32)
    pos = np.arange(S, dtype=np.float32)
    freqs = pos[:, None] * inv_freq[None, :]
    emb = np.concatenate([freqs, freqs], axis=-1)
    mscale = np.float32(0.1 * math.log(scaling) + 1.0)
    cos = (np.cos(emb) * mscale).astype(np.float32)
    sin = (np.sin(emb) * mscale).astype(np.float32)
    return cos, sin


def _perm(n):
    return np.concatenate([np.arange(0, n, 2), np.arange(1, n, 2)])


class Sem:
    def __init__(self, st, nc, name):
        self.h = st.enter_context(nc.semaphore(name))
        self.n = 0

    def inc(self, instr, k=1):
        instr.then_inc(self.h, k)
        self.n += k
        return self.n


def APr(handle, offset, dims):
    return bass.AP(handle, int(offset), [[int(s), int(n)] for s, n in dims])


class Prog:

    def __init__(self):
        self.nc = bass.Bass("TRN2", target_bir_lowering=False, debug=False)
        self.st = ExitStack()

    def sem(self, name):
        return Sem(self.st, self.nc, name)

    def sb(self, name, shape, dtype, stack=None, **kw):
        return (stack or self.st).enter_context(self.nc.sbuf_tensor(name, shape, dtype, **kw))

    def ps(self, name, shape, dtype=F32, stack=None):
        return (stack or self.st).enter_context(self.nc.psum_tensor(name, shape, dtype))


def _build(stop=None):
    stop = stop or _DEV_STOP
    P = Prog()
    nc = P.nc

    x_in = nc.dram_tensor("x_in", [TC, D], F32, kind="ExternalInput")
    gpk = nc.dram_tensor("gpk", [GATE_SH, D // 2], I32, kind="ExternalInput")
    gsc = nc.dram_tensor("gsc", [GATE_SH, D // 32], F32, kind="ExternalInput")
    upk = nc.dram_tensor("upk", [GATE_SH, D // 2], I32, kind="ExternalInput")
    usc = nc.dram_tensor("usc", [GATE_SH, D // 32], F32, kind="ExternalInput")
    dpk = nc.dram_tensor("dpk", [DOWN_SH, NMLP // 2], I32, kind="ExternalInput")
    dsc = nc.dram_tensor("dsc", [DOWN_SH, NMLP // 32], F32, kind="ExternalInput")
    attnA_in = nc.dram_tensor("attnA", [ATTN_SH, D], F32, kind="ExternalInput")
    kvbW_in = nc.dram_tensor("kvbW", [KVB_SH, LAT], F32, kind="ExternalInput")
    kvnorm_in = nc.dram_tensor("kvnorm", [128, LC], F32, kind="ExternalInput")
    cosq_in = nc.dram_tensor("cosq", [ROPE, TC], F32, kind="ExternalInput")
    sinq_in = nc.dram_tensor("sinq", [ROPE, TC], F32, kind="ExternalInput")
    cosk_in = nc.dram_tensor("cosk", [ROPE, SEQ], F32, kind="ExternalInput")
    sink_in = nc.dram_tensor("sink", [ROPE, SEQ], F32, kind="ExternalInput")
    identb_in = nc.dram_tensor("identb", [128, 128], BF16, kind="ExternalInput")
    identf_in = nc.dram_tensor("identf", [128, 128], F32, kind="ExternalInput")
    y_out = nc.dram_tensor("y", [TC, D], F32, kind="ExternalOutput")

    bncA = nc.dram_tensor("bncA", [A_BLK, D], BF16)
    gathA = nc.dram_tensor("gathA", [NCORES * A_BLK, D], BF16)
    bncC = nc.dram_tensor("bncC", [C_BLK, NMLP], BF16)
    gathC = nc.dram_tensor("gathC", [NCORES * C_BLK, NMLP], BF16)
    dbg = "ExternalOutput" if _DEV_DUMP else "Internal"
    latn_my = nc.dram_tensor("latn_my", [LAT, TC], BF16)
    latn_all = nc.dram_tensor("latn_all", [GRP * LAT, TC], BF16)
    dbg_latn = nc.dram_tensor("dbg_latn", [GRP * LAT, TC], BF16, kind="ExternalOutput") if _DEV_DUMP else None
    qT_dram = nc.dram_tensor("qT_dram", [NH * 128, TC], BF16, kind=dbg)
    heads_dram = nc.dram_tensor("heads_dram", [NH * 128, TC], BF16, kind=dbg)
    dbg_stats = nc.dram_tensor("dbg_stats", [NH, 8], F32, kind=dbg) if _DEV_DUMP else None
    dbg_latf = nc.dram_tensor("dbg_latf", [128, LC * TC], F32, kind=dbg) if _DEV_DUMP else None
    dbg_row = nc.dram_tensor("dbg_row", [1, TC], F32, kind=dbg) if _DEV_DUMP else None
    dbg_chain = nc.dram_tensor("dbg_chain", [1, 4 * TC], F32, kind=dbg) if _DEV_DUMP else None
    dbg_kT = nc.dram_tensor("dbg_kT", [128, SEQ], BF16, kind=dbg) if _DEV_DUMP else None
    dbg_vB = nc.dram_tensor("dbg_vB", [128, SEQ], BF16, kind=dbg) if _DEV_DUMP else None
    dbg_vB2 = nc.dram_tensor("dbg_vB2", [128, SEQ], BF16, kind=dbg) if _DEV_DUMP else None
    dbg_kvb = nc.dram_tensor("dbg_kvb", [128, LC * 256], BF16, kind=dbg) if _DEV_DUMP else None
    dbg_bncC = nc.dram_tensor("dbg_bncC", [16, NMLP], BF16, kind=dbg) if _DEV_DUMP else None
    dbg_gathC = nc.dram_tensor("dbg_gathC", [32, NMLP], BF16, kind=dbg) if _DEV_DUMP else None
    dbg_xtok = nc.dram_tensor("dbg_xtok", [128, 512], F32, kind=dbg) if _DEV_DUMP else None
    dbg_deq = nc.dram_tensor("dbg_deq", [128, 512], BF16, kind=dbg) if _DEV_DUMP else None
    dbg_bnc2 = nc.dram_tensor("dbg_bnc2", [4, NMLP], BF16, kind=dbg) if _DEV_DUMP else None
    dbg_pT = nc.dram_tensor("dbg_pT", [128, TC], BF16, kind=dbg) if _DEV_DUMP else None

    qd = P.sem("qdma")
    ad = P.sem("adma")
    gd = P.sem("gdma")
    pe = P.sem("pe")
    ve = P.sem("ve")
    se = P.sem("se")
    po = P.sem("po")
    cc = P.sem("cc")

    xT = P.sb("xT", [128, KC * TC], F32)
    identb = P.sb("identb_sb", [128, 128], BF16)
    identf = P.sb("identf_sb", [128, 128], F32)
    ones_col_f = P.sb("ones_col_f", [128, 1], F32)
    ones_row_f = P.sb("ones_row_f", [1, 128], F32)
    ones_col_b = P.sb("ones_col_b", [128, 1], BF16)
    kvn_sb = P.sb("kvn_sb", [128, LC], F32)

    block = P.st.enter_context(nc.Block())

    def SYNC(f):
        block.sync(f)

    def TEN(f):
        block.tensor(f)

    def VEC(f):
        block.vector(f)

    def SCA(f):
        block.scalar(f)

    def GPS(f):
        block.gpsimd(f)

    phA = ExitStack()
    x_tok = [P.sb(f"x_tok{i}", [128, D], F32, phA) for i in range(2)]
    ps_tr = [P.ps(f"ps_tr{i}", [128, 512], F32, phA) for i in range(2)]

    def _(q):
        q.dma_start(out=identb[:], in_=identb_in.ap()).then_inc(qd.h, 16)
        q.dma_start(out=identf[:], in_=identf_in.ap()).then_inc(qd.h, 16)
        q.dma_start(out=kvn_sb[:], in_=kvnorm_in.ap()).then_inc(qd.h, 16)
        qd.n += 48
    SYNC(_)

    def _(g):
        g.memset(ones_col_f[:], 1.0)
        g.memset(ones_row_f[:], 1.0)
        po.inc(g.memset(ones_col_b[:], 1.0), 3)
    GPS(_)

    xtok_ready = [None] * TT
    xtok_done = [None] * TT
    evac_cnt = {}

    def _emit_xdma(i):
        def _(q, i=i):
            if i >= 2:
                q.wait_ge(pe.h, xtok_done[i - 2])
            qd.inc(q.dma_start(out=x_tok[i % 2][:], in_=x_in.ap()[i * 128:(i + 1) * 128, :]), 16)
            xtok_ready[i] = qd.n
        SYNC(_)

    _emit_xdma(0)
    _emit_xdma(1)
    for i in range(TT):
        for kg in range(4):
            bb = (i * 4 + kg) % 2
            def _(t, i=i, kg=kg, bb=bb):
                if kg == 0:
                    t.wait_ge(qd.h, xtok_ready[i])
                    t.wait_ge(po.h, po.n)
                key = i * 4 + kg
                if key >= 2:
                    t.wait_ge(ve.h, evac_cnt[key - 2])
                last = None
                for j in range(4):
                    kc = kg * 4 + j
                    last = t.transpose(ps_tr[bb][:, j * 128:(j + 1) * 128],
                                       x_tok[i % 2][:, kc * 128:(kc + 1) * 128], identf[:])
                pe.inc(last)
            TEN(_)
            def _(v, i=i, kg=kg, bb=bb):
                v.wait_ge(pe.h, pe.n)
                src = ps_tr[bb][:].rearrange("p (j c) -> p j c", c=128)
                dst = APr(xT, (kg * 4) * TC + i * 128, [[KC * TC, 128], [TC, 4], [1, 128]])
                ve.inc(v.tensor_copy(dst, src))
                evac_cnt[i * 4 + kg] = ve.n
            VEC(_)
        xtok_done[i] = pe.n
        if i + 2 < TT:
            _emit_xdma(i + 2)

    dq = ExitStack()
    w_sb = P.sb("w_sb", [128, 2048], I32, dq)
    sc_sb = P.sb("sc_sb", [128, 128], F32, dq)
    bc_ = P.sb("dq_c", [128, 2048], I32, dq)
    bc8 = P.sb("dq_c8", [128, 2048], I32, dq)
    bs31 = P.sb("dq_s31", [128, 2048], I32, dq)
    bt1 = P.sb("dq_t1", [128, 2048], I32, dq)
    bu = P.sb("dq_u", [128, 2048], I32, dq)
    deq_sb = P.sb("deq_sb", [128, 4096], BF16, dq)
    wT_sb = P.sb("wT_sb", [128, 4096], BF16, dq)
    ps_wq = [P.ps(f"ps_wq{i}", [128, 512], BF16, dq) for i in range(2)]

    def emit_dq_unit(wait_dma_cnt, Fk, sc_cols, out_planes, fold05):
        NB = Fk // 16
        for plane in range(2):
            def _(v, plane=plane):
                c = bc_[:, 0:Fk]
                c8 = bc8[:, 0:Fk]
                s31 = bs31[:, 0:Fk]
                t1 = bt1[:, 0:Fk]
                u = bu[:, 0:Fk]
                v.wait_ge(qd.h, wait_dma_cnt)
                v.wait_ge(ve.h, ve.n)
                if plane == 0:
                    ve.inc(v.tensor_scalar(c, w_sb[:, 0:Fk], 15, None, ALU.bitwise_and))
                else:
                    ve.inc(v.tensor_scalar(c, w_sb[:, 0:Fk], 4, None, ALU.logical_shift_right))
                v.wait_ge(ve.h, ve.n)
                ve.inc(v.tensor_scalar(c8, c, 7, None, ALU.bitwise_and))
                v.wait_ge(ve.h, ve.n)
                ve.inc(v.tensor_scalar(s31, c, 3, 31, ALU.logical_shift_right, ALU.logical_shift_left))
                v.wait_ge(ve.h, ve.n)
                ve.inc(v.tensor_scalar(t1, c8, -251, 252, ALU.mult, ALU.add))
                v.wait_ge(ve.h, ve.n)
                ve.inc(v.tensor_scalar(t1, t1, 0, None, ALU.max))
                v.wait_ge(ve.h, ve.n)
                ve.inc(v.tensor_scalar(u, c8, 1 << 22, 0x3F000000, ALU.mult, ALU.add))
                v.wait_ge(ve.h, ve.n)
                ve.inc(v.scalar_tensor_tensor(c8, t1, -(1 << 22), u, ALU.mult, ALU.add))
                v.wait_ge(ve.h, ve.n)
                ve.inc(v.tensor_tensor(c8, c8, s31, ALU.bitwise_or))
                v.wait_ge(ve.h, ve.n)
                bits_f = c8.bitcast(F32).rearrange("p (b j) -> p b j", j=16)
                sc_view = sc_sb[:, 0:NB, None].broadcast_to([128, NB, 16])
                out = out_planes[plane]
                if fold05:
                    ve.inc(v.scalar_tensor_tensor(out, bits_f, 0.5, sc_view, ALU.mult, ALU.mult))
                else:
                    ve.inc(v.tensor_tensor(out, bits_f, sc_view, ALU.mult))
            VEC(_)

    def emit_shard_transpose_and_store(n_pieces, dst_fn):
        for grp4 in range((n_pieces + 3) // 4):
            bb = grp4 % 2
            def _(t, grp4=grp4, bb=bb):
                t.wait_ge(ve.h, ve.n)
                last = None
                for j in range(4):
                    pc = grp4 * 4 + j
                    last = t.transpose(ps_wq[bb][:, j * 128:(j + 1) * 128],
                                       deq_sb[:, pc * 128:(pc + 1) * 128], identb[:])
                pe.inc(last)
            TEN(_)
            def _(v, grp4=grp4, bb=bb):
                v.wait_ge(pe.h, pe.n)
                v.wait_ge(gd.h, gd.n)
                ve.inc(v.tensor_copy(wT_sb[:, grp4 * 512:(grp4 + 1) * 512], ps_wq[bb][:]))
            VEC(_)
        for gi in range(n_pieces // 16):
            def _(g, gi=gi):
                g.wait_ge(ve.h, ve.n)
                dst, src = dst_fn(gi)
                gd.inc(g.dma_start(out=dst, in_=src), 16)
            GPS(_)

    for seg, (pk_t, sc_t, blob_row0) in enumerate([(gpk, gsc, 0), (upk, usc, GATE_SH)]):
        for r in range(GATE_SH // 128):
            def _(q, pk_t=pk_t, sc_t=sc_t, r=r):
                q.wait_ge(ve.h, ve.n)
                q.dma_start(out=w_sb[:, 0:1024], in_=pk_t.ap()[r * 128:(r + 1) * 128, :]).then_inc(qd.h, 16)
                q.dma_start(out=sc_sb[:, 0:64], in_=sc_t.ap()[r * 128:(r + 1) * 128, :]).then_inc(qd.h, 16)
                qd.n += 32
            SYNC(_)
            lo = deq_sb[:, 0:1024].rearrange("p (b j) -> p b j", j=16)
            hi = deq_sb[:, 1024:2048].rearrange("p (b j) -> p b j", j=16)
            def _(v):
                v.wait_ge(pe.h, pe.n)
            VEC(_)
            emit_dq_unit(qd.n, 1024, 64, (lo, hi), False)

            def dst_fn(gi, r=r, blob_row0=blob_row0):
                dst = APr(bncA, blob_row0 * D + r * 128,
                          [[D, 128], [128 * D, 16], [1, 128]])
                src = wT_sb[:, 0:2048].rearrange("p (kc n) -> p kc n", n=128)
                return dst, src
            emit_shard_transpose_and_store(16, dst_fn)

    for hb in range(DOWN_SH // 128):
        for kq in range(4):
            def _(q, hb=hb, kq=kq):
                q.wait_ge(ve.h, ve.n)
                q.dma_start(out=w_sb[:, 0:2048],
                            in_=dpk.ap()[hb * 128:(hb + 1) * 128, kq * 2048:(kq + 1) * 2048]).then_inc(qd.h, 16)
                q.dma_start(out=sc_sb[:, 0:128],
                            in_=dsc.ap()[hb * 128:(hb + 1) * 128, kq * 128:(kq + 1) * 128]).then_inc(qd.h, 16)
                qd.n += 32
            SYNC(_)
            lo = deq_sb[:, 0:2048].rearrange("p (b j) -> p b j", j=16)
            hi = deq_sb[:, 2048:4096].rearrange("p (b j) -> p b j", j=16)
            def _(v):
                v.wait_ge(pe.h, pe.n)
            VEC(_)
            emit_dq_unit(qd.n, 2048, 128, (lo, hi), True)

            def dst_fn(gi, hb=hb, kq=kq):
                n0 = kq * 2048 + (0 if gi == 0 else NMLP // 2)
                dst = APr(bncC, n0 * DOWN_SH + hb * 128,
                          [[DOWN_SH, 128], [128 * DOWN_SH, 16], [1, 128]])
                src = wT_sb[:, gi * 2048:(gi + 1) * 2048].rearrange("p (kc n) -> p kc n", n=128)
                return dst, src
            emit_shard_transpose_and_store(32, dst_fn)

    att_tiles = [(0, 128), (128, 128), (256, 128), (384, 128), (512, 64)]
    for (r0, nr) in att_tiles:
        def _(q, r0=r0, nr=nr):
            q.wait_ge(ve.h, ve.n)
            qd.inc(q.dma_start(out=x_tok[0][0:nr, :], in_=attnA_in.ap()[r0:r0 + nr, :]), 16)
        SYNC(_)
        def _(v, r0=r0, nr=nr):
            v.wait_ge(qd.h, qd.n)
            v.wait_ge(gd.h, gd.n)
            ve.inc(v.tensor_copy(deq_sb[0:nr, 0:2048], x_tok[0][0:nr, :]))
        VEC(_)
        def _(g, r0=r0, nr=nr):
            g.wait_ge(ve.h, ve.n)
            gd.inc(g.dma_start(out=bncA.ap()[A_ATTN_OFF + r0:A_ATTN_OFF + r0 + nr, :],
                               in_=deq_sb[0:nr, 0:2048]), 16)
        GPS(_)
    for r in range(KVB_SH // 128):
        def _(q, r=r):
            q.wait_ge(ve.h, ve.n)
            qd.inc(q.dma_start(out=x_tok[0][:, 0:512], in_=kvbW_in.ap()[r * 128:(r + 1) * 128, :]), 16)
        SYNC(_)
        def _(v, r=r):
            v.wait_ge(qd.h, qd.n)
            v.wait_ge(gd.h, gd.n)
            ve.inc(v.tensor_copy(deq_sb[:, 0:512], x_tok[0][:, 0:512]))
        VEC(_)
        def _(g, r=r):
            g.wait_ge(ve.h, ve.n)
            gd.inc(g.dma_start(out=APr(bncC, C_DOWN_ROWS * NMLP + r * 128 * 512,
                                       [[512, 128], [1, 512]]),
                               in_=deq_sb[:, 0:512]), 16)
            if _DEV_DUMP and r == 3:
                gd.inc(g.dma_start(out=dbg_xtok.ap(), in_=x_tok[0][:, 0:512]), 16)
                gd.inc(g.dma_start(out=dbg_deq.ap(), in_=deq_sb[:, 0:512]), 16)
                g.wait_ge(gd.h, gd.n)
                gd.inc(g.dma_start(out=dbg_bnc2.ap(),
                                   in_=bncC.ap()[C_DOWN_ROWS + 12:C_DOWN_ROWS + 16, :]), 16)
        GPS(_)

    def _(g):
        g.wait_ge(gd.h, gd.n)
        g.wait_ge(ve.h, ve.n)
        cc.inc(g.collective_compute(
            "AllGather", ALU.bypass,
            replica_groups=[list(range(NCORES))],
            ins=[bncA.ap().opt()], outs=[gathA.ap().opt()]))
        cc.inc(g.collective_compute(
            "AllGather", ALU.bypass,
            replica_groups=[list(range(NCORES))],
            ins=[bncC.ap().opt()], outs=[gathC.ap().opt()]))
    GPS(_)

    dq.close()
    phA.close()

    def emit_ln(dst, scr, rstd_t, ps_st):
        ps_sum, ps_sq, ps_mb, ps_rb, rows = ps_st
        for kc in range(KC):
            def _(s, kc=kc):
                if kc == 0:
                    s.wait_ge(ve.h, ve.n)
                    s.wait_ge(pe.h, pe.n)
                    s.wait_ge(gd.h, gd.n)
                    s.wait_ge(ad.h, ad.n)
                else:
                    s.wait_ge(pe.h, pe.n)
                se.inc(s.activation(scr[kc % 2][:], xT[:, kc * TC:kc * TC + TC], AF.Square))
            SCA(_)
            def _(t, kc=kc):
                t.wait_ge(se.h, se.n)
                for h in range(2):
                    t.matmul(ps_sum[0:1, h * 512:(h + 1) * 512], ones_col_f[:],
                             xT[:, kc * TC + h * 512:kc * TC + (h + 1) * 512],
                             start=(kc == 0), stop=(kc == KC - 1))
                last = None
                for h in range(2):
                    last = t.matmul(ps_sq[0:1, h * 512:(h + 1) * 512], ones_col_f[:],
                                    scr[kc % 2][:, h * 512:(h + 1) * 512],
                                    start=(kc == 0), stop=(kc == KC - 1))
                pe.inc(last)
            TEN(_)
        def _(v):
            v.wait_ge(pe.h, pe.n)
            ve.inc(v.tensor_scalar(rows[0][:], ps_sum[0:1, :], 1.0 / D, None, ALU.mult))
            v.wait_ge(ve.h, ve.n)
            ve.inc(v.tensor_scalar(rows[1][:], ps_sq[0:1, :], 1.0 / D, None, ALU.mult))
            v.wait_ge(ve.h, ve.n)
            ve.inc(v.tensor_tensor(rows[2][:], rows[0][:], rows[0][:], ALU.mult))
            v.wait_ge(ve.h, ve.n)
            ve.inc(v.scalar_tensor_tensor(rows[1][:], rows[1][:], 1e-5, rows[2][:], ALU.add, ALU.subtract))
        VEC(_)
        def _(t):
            t.wait_ge(ve.h, ve.n)
            for h in range(2):
                t.matmul(ps_mb[:, h * 512:(h + 1) * 512], ones_row_f[:],
                         rows[0][0:1, h * 512:(h + 1) * 512], start=True, stop=True)
            last = None
            for h in range(2):
                last = t.matmul(ps_rb[:, h * 512:(h + 1) * 512], ones_row_f[:],
                                rows[1][0:1, h * 512:(h + 1) * 512], start=True, stop=True)
            pe.inc(last)
        TEN(_)
        def _(s):
            s.wait_ge(pe.h, pe.n)
            s.wait_ge(ve.h, ve.n)
            se.inc(s.activation(scr[0][:], ps_rb[:], AF.Sqrt))
        SCA(_)
        def _(v):
            v.wait_ge(se.h, se.n)
            ve.inc(v.reciprocal(rstd_t[:], scr[0][:]))
        VEC(_)
        for kc in range(KC):
            def _(v, kc=kc):
                v.wait_ge(pe.h, pe.n)
                v.wait_ge(ve.h, ve.n)
                ve.inc(v.tensor_tensor(scr[kc % 2][:], xT[:, kc * TC:kc * TC + TC], ps_mb[:], ALU.subtract))
                v.wait_ge(ve.h, ve.n)
                ve.inc(v.tensor_tensor(dst[:, kc * TC:kc * TC + TC], scr[kc % 2][:], rstd_t[:], ALU.mult))
            VEC(_)

    def emit_ffn(tag):
        ph = ExitStack()
        xnT = P.sb(f"{tag}_xnT", [128, KC * TC], BF16, ph)
        interT = P.sb(f"{tag}_inter", [128, 16 * TC], BF16, ph)
        scr = [P.sb(f"{tag}_scr{i}", [128, TC], F32, ph) for i in range(2)]
        rows = [P.sb(f"{tag}_row{i}", [1, TC], F32, ph) for i in range(3)]
        wg_sl = [P.sb(f"{tag}_wg{i}", [128, 2048], BF16, ph) for i in range(2)]
        wu_sl = [P.sb(f"{tag}_wu{i}", [128, 2048], BF16, ph) for i in range(2)]
        wd_sl = [P.sb(f"{tag}_wd{i}", [128, 8192], BF16, ph) for i in range(2)]
        rstd_t = P.sb(f"{tag}_rstd", [128, TC], F32, ph)

        lnps = ExitStack()
        ps_st = (P.ps(f"{tag}_ln_sum", [1, TC], F32, lnps),
                 P.ps(f"{tag}_ln_sq", [1, TC], F32, lnps),
                 P.ps(f"{tag}_ln_mb", [128, TC], F32, lnps),
                 P.ps(f"{tag}_ln_rb", [128, TC], F32, lnps),
                 rows)
        emit_ln(xnT, scr, rstd_t, ps_st)
        lnps.close()

        ps_g = P.ps(f"{tag}_ps_g", [128, TC], F32, ph)
        ps_u = [P.ps(f"{tag}_ps_u{i}", [128, TC], F32, ph) for i in range(2)]
        ps_d = [P.ps(f"{tag}_ps_d{i}", [128, 512], F32, ph) for i in range(2)]

        ln_done = ve.n
        g_mm = {}
        gu_mm = {}
        gelu_done = {}
        mult_done = {}
        fetch_cnt = {}
        fetchd_cnt = {}
        d_read = {}
        slabd_mm = {}

        for g in range(8):
            for nci in range(16):
                nabs = g * 16 + nci
                bw = nabs % 2

                def _(q, g=g, nci=nci, nabs=nabs, bw=bw):
                    if nabs >= 2:
                        q.wait_ge(pe.h, gu_mm[nabs - 2])
                    if nabs == 0:
                        q.wait_ge(cc.h, cc.n)
                        q.wait_ge(gd.h, gd.n)
                        q.wait_ge(ad.h, ad.n)
                    if nabs < 2:
                        q.wait_ge(ve.h, ln_done)
                    src_g = APr(gathA, (g * A_BLK) * D + nci * 128,
                                [[D, 128], [128 * D, 16], [1, 128]])
                    src_u = APr(gathA, (g * A_BLK + GATE_SH) * D + nci * 128,
                                [[D, 128], [128 * D, 16], [1, 128]])
                    q.dma_start(out=wg_sl[bw][:].rearrange("p (kc n) -> p kc n", n=128),
                                in_=src_g).then_inc(qd.h, 16)
                    q.dma_start(out=wu_sl[bw][:].rearrange("p (kc n) -> p kc n", n=128),
                                in_=src_u).then_inc(qd.h, 16)
                    qd.n += 32
                    fetch_cnt[nabs] = qd.n
                SYNC(_)

                def _(t, nabs=nabs, bw=bw):
                    t.wait_ge(qd.h, fetch_cnt[nabs])
                    if nabs == 0:
                        t.wait_ge(ve.h, ln_done)
                    if nabs >= 1:
                        t.wait_ge(se.h, gelu_done[nabs - 1])
                    if nabs >= 2:
                        t.wait_ge(ve.h, mult_done[nabs - 2])
                    last = None
                    for kc in range(KC):
                        for h in range(2):
                            last = t.matmul(ps_g[:, h * 512:(h + 1) * 512],
                                            wg_sl[bw][:, kc * 128:(kc + 1) * 128],
                                            xnT[:, kc * TC + h * 512:kc * TC + (h + 1) * 512],
                                            start=(kc == 0), stop=(kc == KC - 1))
                    pe.inc(last)
                    g_mm[nabs] = pe.n
                    for kc in range(KC):
                        for h in range(2):
                            last = t.matmul(ps_u[bw][:, h * 512:(h + 1) * 512],
                                            wu_sl[bw][:, kc * 128:(kc + 1) * 128],
                                            xnT[:, kc * TC + h * 512:kc * TC + (h + 1) * 512],
                                            start=(kc == 0), stop=(kc == KC - 1))
                    pe.inc(last)
                    gu_mm[nabs] = pe.n
                TEN(_)

                def _(s, nabs=nabs, bw=bw):
                    s.wait_ge(pe.h, g_mm[nabs])
                    if nabs >= 2:
                        s.wait_ge(ve.h, mult_done[nabs - 2])
                    else:
                        s.wait_ge(ve.h, ln_done)
                    se.inc(s.activation(scr[bw][:], ps_g[:], AF.Gelu_apprx_tanh))
                    gelu_done[nabs] = se.n
                SCA(_)

                def _(v, nabs=nabs, bw=bw, nci=nci):
                    v.wait_ge(se.h, gelu_done[nabs])
                    v.wait_ge(pe.h, gu_mm[nabs])
                    ve.inc(v.tensor_tensor(interT[:, nci * TC:(nci + 1) * TC],
                                           scr[bw][:], ps_u[bw][:], ALU.mult))
                    mult_done[nabs] = ve.n
                VEC(_)

            for hq in range(4):
                dseq = g * 4 + hq
                bws = dseq % 2

                def _(q, g=g, hq=hq, dseq=dseq, bws=bws):
                    if dseq >= 2:
                        q.wait_ge(pe.h, slabd_mm[dseq - 2])
                    if dseq < 2:
                        q.wait_ge(ve.h, ln_done)
                    for r in range(2):
                        src = APr(gathC,
                                  (hq * 2 + r) * C_BLK * NMLP + (g * 2048) * DOWN_SH,
                                  [[DOWN_SH, 128], [128 * DOWN_SH, 16], [1, 256]])
                        dst = APr(wd_sl[bws], r * 256, [[8192, 128], [512, 16], [1, 256]])
                        q.dma_start(out=dst, in_=src).then_inc(qd.h, 16)
                        qd.n += 16
                    fetchd_cnt[dseq] = qd.n
                SYNC(_)

                for hc in range(4):
                    habs = hq * 4 + hc
                    for th in range(2):
                        sseq = (dseq * 8) + hc * 2 + th
                        bd = sseq % 2

                        def _(t, g=g, hq=hq, hc=hc, th=th, habs=habs, dseq=dseq, bws=bws, bd=bd, sseq=sseq):
                            t.wait_ge(qd.h, fetchd_cnt[dseq])
                            if hc == 0 and th == 0:
                                t.wait_ge(ve.h, mult_done[g * 16 + 15])
                            if sseq >= 2 and (sseq - 2) in d_read:
                                t.wait_ge(ve.h, d_read[sseq - 2])
                            last = None
                            for nci in range(16):
                                last = t.matmul(ps_d[bd][:],
                                                wd_sl[bws][:, nci * 512 + hc * 128:nci * 512 + (hc + 1) * 128],
                                                interT[:, nci * TC + th * 512:nci * TC + (th + 1) * 512],
                                                start=(nci == 0), stop=(nci == 15))
                            pe.inc(last)
                            if hc == 3 and th == 1:
                                slabd_mm[dseq] = pe.n
                        TEN(_)

                        def _(v, habs=habs, th=th, bd=bd, sseq=sseq):
                            v.wait_ge(pe.h, pe.n)
                            ve.inc(v.tensor_tensor(
                                xT[:, habs * TC + th * 512:habs * TC + (th + 1) * 512],
                                xT[:, habs * TC + th * 512:habs * TC + (th + 1) * 512],
                                ps_d[bd][:], ALU.add))
                            d_read[sseq] = ve.n
                        VEC(_)

        ph.close()

    def tdma_attnA(q, dst_tile, dst_col0, row0, nrows, kc):
        done = 0
        n = 0
        while done < nrows:
            r = row0 + done
            blk = r // ATTN_SH
            within = r % ATTN_SH
            glen = min(ATTN_SH - within, nrows - done)
            src = APr(gathA, (blk * A_BLK + A_ATTN_OFF + within) * D + kc * 128,
                      [[D, glen], [1, 128]])
            q.dma_start(out=dst_tile[:, dst_col0 + done:dst_col0 + done + glen],
                        in_=src, transpose=True).then_inc(qd.h, 16)
            qd.n += 16
            done += glen
            n += 1
        return n

    def emit_attn():
        ph = ExitStack()
        cosk = P.sb("at_cosk", [ROPE, SEQ], F32, ph)
        sink = P.sb("at_sink", [ROPE, SEQ], F32, ph)

        s1 = ExitStack()
        cosq = P.sb("at_cosq", [ROPE, TC], F32, s1)
        sinq = P.sb("at_sinq", [ROPE, TC], F32, s1)
        xn2 = P.sb("at_xn", [128, KC * TC], BF16, s1)
        scr = [P.sb(f"at_scr{i}", [128, TC], F32, s1) for i in range(2)]
        rows = [P.sb(f"at_row{i}", [1, TC], F32, s1) for i in range(3)]
        qsl = P.sb("at_qsl", [128, 16 * 512], BF16, s1, is_transpose_dest=True, align_bytes=4096)
        rstd_t = P.sb("at_rstd", [128, TC], F32, s1)
        qt_tile = P.sb("at_qt", [128, TC], BF16, s1)
        tmpa = P.sb("at_tmpa", [32, TC], F32, s1)
        tmpb = P.sb("at_tmpb", [32, TC], F32, s1)
        lat_f = P.sb("at_latf", [128, LC * TC], F32, s1)
        lat_b = qsl

        def _(q):
            q.wait_ge(pe.h, pe.n)
            q.wait_ge(ve.h, ve.n)
            q.wait_ge(se.h, se.n)
            q.wait_ge(gd.h, gd.n)
            q.wait_ge(ad.h, ad.n)
            for dst, srct in ((cosk, cosk_in), (sink, sink_in), (cosq, cosq_in), (sinq, sinq_in)):
                q.dma_start(out=dst[:], in_=srct.ap()).then_inc(qd.h, 16)
                qd.n += 16
        SYNC(_)
        def _(a):
            a.wait_ge(pe.h, pe.n)
            a.wait_ge(ve.h, ve.n)
            a.wait_ge(gd.h, gd.n)
        SCA(_)

        lnps = ExitStack()
        ps_st_ln = (P.ps("at_ln_sum", [1, TC], F32, lnps),
                    P.ps("at_ln_sq", [1, TC], F32, lnps),
                    P.ps("at_ln_mb", [128, TC], F32, lnps),
                    P.ps("at_ln_rb", [128, TC], F32, lnps),
                    rows)
        emit_ln(xn2, scr, rstd_t, ps_st_ln)
        lnps.close()

        qps = ExitStack()
        ps_q = [P.ps(f"at_ps_q{i}", [128, TC], F32, qps) for i in range(2)]
        qfetch = {}
        qmm = {}
        qev = {}
        for qg in range(4):
            def _(q, qg=qg):
                if qg >= 1:
                    q.wait_ge(pe.h, qmm[qg - 1])
                for kc in range(KC):
                    tdma_attnA(q, qsl, kc * 512, qg * 512, 512, kc)
                qfetch[qg] = qd.n
            SYNC(_)
            for qc4 in range(4):
                hidx = qg * 4 + qc4
                bq = hidx % 2
                def _(t, qg=qg, qc4=qc4, hidx=hidx, bq=bq):
                    t.wait_ge(qd.h, qfetch[qg])
                    if qc4 == 0 and qg == 0:
                        t.wait_ge(ve.h, ve.n)
                    if hidx >= 2:
                        t.wait_ge(ve.h, qev[hidx - 2])
                    last = None
                    for kc in range(KC):
                        for hh in range(2):
                            last = t.matmul(ps_q[bq][:, hh * 512:(hh + 1) * 512],
                                            qsl[:, kc * 512 + qc4 * 128:kc * 512 + qc4 * 128 + 128],
                                            xn2[:, kc * TC + hh * 512:kc * TC + (hh + 1) * 512],
                                            start=(kc == 0), stop=(kc == KC - 1))
                    pe.inc(last)
                    qmm[qg] = pe.n
                TEN(_)
                def _(v, hidx=hidx, bq=bq):
                    v.wait_ge(pe.h, pe.n)
                    v.wait_ge(ad.h, ad.n)
                    v.wait_ge(qd.h, qd.n)
                    ve.inc(v.tensor_tensor(tmpa[:], ps_q[bq][0:32, :], cosq[0:32, :], ALU.mult))
                    ve.inc(v.tensor_tensor(tmpb[:], ps_q[bq][32:64, :], sinq[0:32, :], ALU.mult))
                    v.wait_ge(ve.h, ve.n)
                    ve.inc(v.tensor_tensor(qt_tile[0:32, :], tmpa[:], tmpb[:], ALU.subtract))
                    ve.inc(v.tensor_tensor(tmpa[:], ps_q[bq][32:64, :], cosq[32:64, :], ALU.mult))
                    ve.inc(v.tensor_tensor(tmpb[:], ps_q[bq][0:32, :], sinq[32:64, :], ALU.mult))
                    v.wait_ge(ve.h, ve.n)
                    ve.inc(v.tensor_tensor(qt_tile[32:64, :], tmpa[:], tmpb[:], ALU.add))
                    ve.inc(v.tensor_scalar(qt_tile[64:128, :], ps_q[bq][64:128, :], SCALE, None, ALU.mult))
                    qev[hidx] = ve.n
                VEC(_)
                def _(a, hidx=hidx):
                    a.wait_ge(ve.h, qev[hidx])
                    ad.inc(a.dma_start(out=qT_dram.ap()[hidx * 128:(hidx + 1) * 128, :],
                                       in_=qt_tile[:]), 16)
                SCA(_)
        qps.close()

        kps = ExitStack()
        ps_l = [P.ps(f"at_ps_l{i}", [128, TC], F32, kps) for i in range(2)]
        ps_ssq = P.ps("at_ps_ssq", [1, TC], F32, kps)
        ps_rsb = P.ps("at_ps_rsb", [128, TC], F32, kps)
        def _(q):
            q.wait_ge(pe.h, pe.n)
            for kc in range(KC):
                tdma_attnA(q, qsl, kc * 512, D + 0, 512, kc)
        SYNC(_)
        kvafetch = qd.n
        lev = {}
        for lc in range(LC):
            bl = lc % 2
            def _(t, lc=lc, bl=bl):
                t.wait_ge(qd.h, kvafetch)
                if lc >= 2:
                    t.wait_ge(ve.h, lev[lc - 2])
                last = None
                for kc in range(KC):
                    for hh in range(2):
                        last = t.matmul(ps_l[bl][:, hh * 512:(hh + 1) * 512],
                                        qsl[:, kc * 512 + lc * 128:kc * 512 + lc * 128 + 128],
                                        xn2[:, kc * TC + hh * 512:kc * TC + (hh + 1) * 512],
                                        start=(kc == 0), stop=(kc == KC - 1))
                pe.inc(last)
            TEN(_)
            def _(v, lc=lc, bl=bl):
                v.wait_ge(pe.h, pe.n)
                ve.inc(v.tensor_copy(lat_f[:, lc * TC:(lc + 1) * TC], ps_l[bl][:]))
                lev[lc] = ve.n
            VEC(_)
            def _(s, lc=lc):
                s.wait_ge(ve.h, lev[lc])
                se.inc(s.activation(scr[lc % 2][:], lat_f[:, lc * TC:(lc + 1) * TC], AF.Square))
            SCA(_)
            def _(t, lc=lc):
                t.wait_ge(se.h, se.n)
                last = None
                for hh in range(2):
                    last = t.matmul(ps_ssq[0:1, hh * 512:(hh + 1) * 512], ones_col_f[:],
                                    scr[lc % 2][:, hh * 512:(hh + 1) * 512],
                                    start=(lc == 0), stop=(lc == LC - 1))
                pe.inc(last)
            TEN(_)
        def _(v):
            v.wait_ge(pe.h, pe.n)
            if _DEV_DUMP:
                ve.inc(v.tensor_copy(rows[2][:], ps_ssq[0:1, :]))
            ve.inc(v.tensor_scalar(rows[0][:], ps_ssq[0:1, :], 1.0 / LAT, 1e-6, ALU.mult, ALU.add))
        VEC(_)
        if _DEV_DUMP:
            def _(g):
                g.wait_ge(ve.h, ve.n)
                gd.inc(g.dma_start(out=dbg_chain.ap()[0:1, 0:TC], in_=rows[2][:]), 16)
                gd.inc(g.dma_start(out=dbg_chain.ap()[0:1, TC:2 * TC], in_=rows[0][:]), 16)
            GPS(_)
        def _(t):
            t.wait_ge(ve.h, ve.n)
            last = None
            for hh in range(2):
                last = t.matmul(ps_rsb[:, hh * 512:(hh + 1) * 512], ones_row_f[:],
                                rows[0][0:1, hh * 512:(hh + 1) * 512], start=True, stop=True)
            pe.inc(last)
        TEN(_)
        def _(s):
            s.wait_ge(pe.h, pe.n)
            s.wait_ge(se.h, se.n)
            se.inc(s.activation(scr[0][:], ps_rsb[:], AF.Sqrt))
        SCA(_)
        def _(v):
            v.wait_ge(se.h, se.n)
            ve.inc(v.reciprocal(rstd_t[:], scr[0][:]))
        VEC(_)
        for lc in range(LC):
            def _(v, lc=lc):
                v.wait_ge(ve.h, ve.n)
                ve.inc(v.scalar_tensor_tensor(lat_b[:, lc * TC:(lc + 1) * TC],
                                              lat_f[:, lc * TC:(lc + 1) * TC],
                                              kvn_sb[:, lc:lc + 1], rstd_t[:],
                                              ALU.mult, ALU.mult))
            VEC(_)
        if _DEV_DUMP:
            def _(g):
                g.wait_ge(ve.h, ve.n)
                gd.inc(g.dma_start(out=dbg_latf.ap(), in_=lat_f[:]), 16)
                gd.inc(g.dma_start(out=dbg_row.ap(), in_=rows[0][:]), 16)
            GPS(_)
        def _(g):
            g.wait_ge(ve.h, ve.n)
            dst = latn_my.ap().rearrange("(lc p) t -> p lc t", p=128)
            gd.inc(g.dma_start(out=dst, in_=lat_b[:, 0:LC * TC].rearrange("p (lc t) -> p lc t", t=TC)), 16)
            g.wait_ge(gd.h, gd.n)
            cc.inc(g.collective_compute(
                "AllGather", ALU.bypass,
                replica_groups=[[0, 1, 2, 3], [4, 5, 6, 7]],
                ins=[latn_my.ap().opt()], outs=[latn_all.ap().opt()]))
        GPS(_)
        kps.close()
        s1.close()

        s2 = ExitStack()
        latn_sb = P.sb("at_latn", [128, LC * SEQ], BF16, s2)
        kT = P.sb("at_kT", [128, SEQ], BF16, s2)
        vB = P.sb("at_vB", [128, SEQ], BF16, s2)
        qh = [P.sb(f"at_qh{i}", [128, TC], BF16, s2) for i in range(2)]
        pT = [P.sb(f"at_pT{i}", [128, TC], BF16, s2) for i in range(2)]
        kvbsl = [P.sb(f"at_kvb{i}", [128, LC * 256], BF16, s2) for i in range(2)]
        kvnat = [P.sb(f"at_kvn{i}", [128, 1024], BF16, s2) for i in range(2)]
        ksq = P.sb("at_ksq", [128, SEQ], BF16, s2)
        qsq = P.sb("at_qsq", [128, TC], BF16, s2)
        recip_sb = P.sb("at_recip", [128, TC], F32, s2)
        ht = P.sb("at_ht", [128, TC], BF16, s2)
        biasC = P.sb("at_biasC", [128, 1], F32, s2)
        tmpa2 = P.sb("at_tmpa2", [32, TC], F32, s2)
        tmpb2 = P.sb("at_tmpb2", [32, TC], F32, s2)
        srow = P.sb("at_srow", [1, 8], F32, s2)
        nrow = P.sb("at_nrow", [1, TC], F32, s2)

        def _(g):
            g.wait_ge(cc.h, cc.n)
            if _DEV_DUMP:
                gd.inc(g.dma_start(out=dbg_latn.ap(), in_=latn_all.ap()), 16)
                gd.inc(g.dma_start(out=dbg_bncC.ap(), in_=bncC.ap()[C_DOWN_ROWS:C_DOWN_ROWS + 16, :]), 16)
                gd.inc(g.dma_start(out=dbg_gathC.ap()[0:16, :], in_=gathC.ap()[C_DOWN_ROWS:C_DOWN_ROWS + 16, :]), 16)
                gd.inc(g.dma_start(out=dbg_gathC.ap()[16:32, :], in_=gathC.ap()[C_BLK + C_DOWN_ROWS:C_BLK + C_DOWN_ROWS + 16, :]), 16)
            for lc in range(LC):
                src = APr(latn_all, lc * 128 * TC, [[TC, 128], [LAT * TC, KB], [1, TC]])
                dst = latn_sb[:, lc * SEQ:(lc + 1) * SEQ].rearrange("p (r t) -> p r t", t=TC)
                gd.inc(g.dma_start(out=dst, in_=src), 16)
        GPS(_)

        head_end_pe = {}
        head_exp = {}
        qh_cnt = {}
        kvb_evac = {}
        for h in range(NH):
            blk = h // 2
            def _(a, h=h):
                if h >= 2:
                    a.wait_ge(pe.h, head_end_pe[h - 2])
                else:
                    a.wait_ge(ve.h, ve.n)
                    a.wait_ge(gd.h, gd.n)
                ad.inc(a.dma_start(out=qh[h % 2][:],
                                   in_=qT_dram.ap()[h * 128:(h + 1) * 128, :]), 16)
                qh_cnt[h] = ad.n
            SCA(_)
            def _(q, h=h, blk=blk):
                if h >= 2:
                    q.wait_ge(pe.h, head_end_pe[h - 2])
                else:
                    q.wait_ge(ve.h, ve.n)
                    q.wait_ge(gd.h, gd.n)
                for t2 in range(2):
                    src = APr(gathC, (blk * C_BLK + C_DOWN_ROWS) * NMLP
                              + ((h % 2) * 256 + t2 * 128) * LAT,
                              [[LAT, 128], [1, 512]])
                    q.dma_start(out=kvnat[h % 2][:, t2 * 512:(t2 + 1) * 512],
                                in_=src).then_inc(qd.h, 16)
                    qd.n += 16
            SYNC(_)
            kvb_cnt = qd.n

            sc1 = ExitStack()
            ps_kv = [P.ps(f"at_kv{h}_{i}", [128, TC], F32, sc1) for i in range(2)]
            ps_v = P.ps(f"at_v{h}", [128, 512], F32, sc1)
            ps_st = P.ps(f"at_st{h}", [1, 512], F32, sc1)
            ps_b1 = P.ps(f"at_b1{h}", [128, 1], F32, sc1)
            ps_tp = P.ps(f"at_tp{h}", [128, 512], BF16, sc1)

            for t2 in range(2):
                def _(t, h=h, t2=t2):
                    t.wait_ge(qd.h, kvb_cnt)
                    if h == 0 and t2 == 0:
                        t.wait_ge(gd.h, gd.n)
                        t.wait_ge(ad.h, ad.n)
                    if h >= 1 and t2 == 0:
                        t.wait_ge(se.h, se.n)
                        t.wait_ge(ve.h, ve.n)
                    if t2 == 1:
                        t.wait_ge(ve.h, ve.n)
                    last = None
                    for lc in range(LC):
                        last = t.transpose(ps_tp[:, lc * 128:(lc + 1) * 128],
                                           kvnat[h % 2][:, t2 * 512 + lc * 128:t2 * 512 + (lc + 1) * 128],
                                           identb[:])
                    pe.inc(last)
                TEN(_)
                def _(v, h=h, t2=t2):
                    v.wait_ge(pe.h, pe.n)
                    dst = APr(kvbsl[h % 2], t2 * 128, [[1024, 128], [256, 4], [1, 128]])
                    ve.inc(v.tensor_copy(dst, ps_tp[:].rearrange("p (lc n) -> p lc n", n=128)))
                    kvb_evac[h] = ve.n
                VEC(_)

            krope = {}
            for r in range(KB):
                br = r % 2
                def _(t, h=h, r=r, br=br):
                    t.wait_ge(ve.h, kvb_evac[h])
                    if r >= 2:
                        t.wait_ge(ve.h, krope[r - 2])
                    if h >= 1 and r < 2:
                        t.wait_ge(se.h, se.n)
                        t.wait_ge(ve.h, ve.n)
                    last = None
                    for lc in range(LC):
                        for hh in range(2):
                            last = t.matmul(ps_kv[br][:, hh * 512:(hh + 1) * 512],
                                            kvbsl[h % 2][:, lc * 256:lc * 256 + 128],
                                            latn_sb[:, lc * SEQ + r * TC + hh * 512:lc * SEQ + r * TC + (hh + 1) * 512],
                                            start=(lc == 0), stop=(lc == LC - 1))
                    pe.inc(last)
                TEN(_)
                def _(v, r=r, br=br):
                    v.wait_ge(pe.h, pe.n)
                    cols = slice(r * TC, (r + 1) * TC)
                    ve.inc(v.tensor_tensor(tmpa2[:], ps_kv[br][0:32, :], cosk[0:32, cols], ALU.mult))
                    ve.inc(v.tensor_tensor(tmpb2[:], ps_kv[br][32:64, :], sink[0:32, cols], ALU.mult))
                    v.wait_ge(ve.h, ve.n)
                    ve.inc(v.tensor_tensor(kT[0:32, cols], tmpa2[:], tmpb2[:], ALU.subtract))
                    ve.inc(v.tensor_tensor(tmpa2[:], ps_kv[br][32:64, :], cosk[32:64, cols], ALU.mult))
                    ve.inc(v.tensor_tensor(tmpb2[:], ps_kv[br][0:32, :], sink[32:64, cols], ALU.mult))
                    v.wait_ge(ve.h, ve.n)
                    ve.inc(v.tensor_tensor(kT[32:64, cols], tmpa2[:], tmpb2[:], ALU.add))
                    ve.inc(v.tensor_copy(kT[64:128, cols], ps_kv[br][64:128, :]))
                    krope[r] = ve.n
                VEC(_)

            vev = {}
            for kt4 in range(8):
                def _(t, h=h, kt4=kt4):
                    if kt4 >= 1:
                        t.wait_ge(ve.h, vev[kt4 - 1])
                    last = None
                    for j in range(4):
                        kt = kt4 * 4 + j
                        for lc in range(LC):
                            last = t.matmul(ps_v[:, j * 128:(j + 1) * 128],
                                            latn_sb[:, lc * SEQ + kt * 128:lc * SEQ + (kt + 1) * 128],
                                            kvbsl[h % 2][:, lc * 256 + 128:lc * 256 + 256],
                                            start=(lc == 0), stop=(lc == LC - 1))
                    pe.inc(last)
                TEN(_)
                def _(v, kt4=kt4):
                    v.wait_ge(pe.h, pe.n)
                    ve.inc(v.tensor_copy(vB[:, kt4 * 512:(kt4 + 1) * 512], ps_v[:]))
                    vev[kt4] = ve.n
                VEC(_)

            if _DEV_DUMP and h == 3:
                def _(g):
                    g.wait_ge(ve.h, ve.n)
                    gd.inc(g.dma_start(out=dbg_vB2.ap(), in_=vB[:]), 16)
                    gd.inc(g.dma_start(out=dbg_kvb.ap(), in_=kvbsl[1][:]), 16)
                GPS(_)
                def _(s):
                    s.wait_ge(gd.h, gd.n)
                SCA(_)
            def _(s, h=h):
                s.wait_ge(ve.h, ve.n)
                s.wait_ge(ad.h, qh_cnt[h])
                se.inc(s.activation(ksq[:], kT[:], AF.Square))
                se.inc(s.activation(qsq[:], qh[h % 2][:], AF.Square))
            SCA(_)
            def _(v):
                v.wait_ge(ve.h, ve.n)
                ve.inc(v.memset(srow[:], -1e30))
            VEC(_)
            for i in range(10):
                def _(t, i=i):
                    t.wait_ge(se.h, se.n)
                    if i >= 1:
                        t.wait_ge(ve.h, ve.n)
                    src = ksq[:, i * 512:(i + 1) * 512] if i < 8 else qsq[:, (i - 8) * 512:(i - 7) * 512]
                    pe.inc(t.matmul(ps_st[0:1, :], ones_col_b[:], src, start=True, stop=True))
                TEN(_)
                def _(v, i=i):
                    v.wait_ge(pe.h, pe.n)
                    j = 0 if i < 8 else 1
                    ve.inc(v.reduce_max(srow[0:1, 2 + i:3 + i] if False else nrow[0:1, 0:1],
                                        ps_st[0:1, :], axis=AX))
                    v.wait_ge(ve.h, ve.n)
                    ve.inc(v.tensor_tensor(srow[0:1, j:j + 1], srow[0:1, j:j + 1],
                                           nrow[0:1, 0:1], ALU.max))
                VEC(_)
            def _(v):
                v.wait_ge(ve.h, ve.n)
                ve.inc(v.tensor_scalar(srow[0:1, 2:3], srow[0:1, 0:1], -0.5, None, ALU.mult))
                v.wait_ge(ve.h, ve.n)
                ve.inc(v.scalar_tensor_tensor(srow[0:1, 3:4], srow[0:1, 1:2], -0.5,
                                              srow[0:1, 2:3], ALU.mult, ALU.add))
            VEC(_)
            def _(t):
                t.wait_ge(ve.h, ve.n)
                pe.inc(t.matmul(ps_b1[:], ones_row_f[:], srow[0:1, 3:4], start=True, stop=True))
            TEN(_)
            def _(v):
                v.wait_ge(pe.h, pe.n)
                ve.inc(v.tensor_copy(biasC[:], ps_b1[:]))
            VEC(_)
            sc1.close()

            if _DEV_DUMP and h == 3:
                def _(g):
                    g.wait_ge(ve.h, ve.n)
                    gd.inc(g.dma_start(out=dbg_kT.ap(), in_=kT[:]), 16)
                    gd.inc(g.dma_start(out=dbg_vB.ap(), in_=vB[:]), 16)
                GPS(_)
            sc2 = ExitStack()
            ps_sT = [P.ps(f"at_sT{h}_{i}", [128, TC], F32, sc2) for i in range(2)]
            ps_OT = P.ps(f"at_OT{h}", [128, TC], F32, sc2)
            ps_rs = P.ps(f"at_rs{h}", [1, TC], F32, sc2)
            sT_mm = {}
            exp_done = {}
            av_mm = {}
            for kt in range(32):
                bk = kt % 2
                def _(t, h=h, kt=kt, bk=bk):
                    if kt == 0:
                        t.wait_ge(ve.h, ve.n)
                        t.wait_ge(ad.h, qh_cnt[h])
                    if kt >= 2:
                        t.wait_ge(se.h, exp_done[kt - 2])
                    last = None
                    for hh in range(2):
                        last = t.matmul(ps_sT[bk][:, hh * 512:(hh + 1) * 512],
                                        kT[:, kt * 128:(kt + 1) * 128],
                                        qh[h % 2][:, hh * 512:(hh + 1) * 512],
                                        start=True, stop=True)
                    pe.inc(last)
                    sT_mm[kt] = pe.n
                TEN(_)
                def _(s, kt=kt, bk=bk):
                    s.wait_ge(pe.h, sT_mm[kt])
                    if kt >= 2:
                        s.wait_ge(pe.h, av_mm.get(kt - 2, 0))
                    se.inc(s.activation(pT[bk][:], ps_sT[bk][:], AF.Exp, bias=biasC[:, 0:1]))
                    exp_done[kt] = se.n
                SCA(_)
                if _DEV_DUMP and h == 3 and kt == 6:
                    def _(g, kt=kt):
                        g.wait_ge(se.h, se.n)
                        gd.inc(g.dma_start(out=dbg_pT.ap(), in_=pT[0][:]), 16)
                    GPS(_)
                if kt >= 2:
                    def _(t, kt=kt):
                        kk = kt - 2
                        bb2 = kk % 2
                        t.wait_ge(se.h, exp_done[kk])
                        for hh in range(2):
                            t.matmul(ps_OT[:, hh * 512:(hh + 1) * 512],
                                     vB[:, kk * 128:(kk + 1) * 128],
                                     pT[bb2][:, hh * 512:(hh + 1) * 512],
                                     start=(kk == 0), stop=(kk == 31))
                        last = None
                        for hh in range(2):
                            last = t.matmul(ps_rs[0:1, hh * 512:(hh + 1) * 512],
                                            ones_col_b[:],
                                            pT[bb2][:, hh * 512:(hh + 1) * 512],
                                            start=(kk == 0), stop=(kk == 31))
                        pe.inc(last)
                        av_mm[kk] = pe.n
                    TEN(_)
            for kk in (30, 31):
                def _(t, kk=kk):
                    bb2 = kk % 2
                    t.wait_ge(se.h, exp_done[kk])
                    for hh in range(2):
                        t.matmul(ps_OT[:, hh * 512:(hh + 1) * 512],
                                 vB[:, kk * 128:(kk + 1) * 128],
                                 pT[bb2][:, hh * 512:(hh + 1) * 512],
                                 start=(kk == 0), stop=(kk == 31))
                    last = None
                    for hh in range(2):
                        last = t.matmul(ps_rs[0:1, hh * 512:(hh + 1) * 512],
                                        ones_col_b[:],
                                        pT[bb2][:, hh * 512:(hh + 1) * 512],
                                        start=(kk == 0), stop=(kk == 31))
                    pe.inc(last)
                    av_mm[kk] = pe.n
                TEN(_)

            def _(v):
                v.wait_ge(pe.h, pe.n)
                ve.inc(v.reciprocal(nrow[:], ps_rs[0:1, :]))
            VEC(_)
            def _(t):
                t.wait_ge(ve.h, ve.n)
                t.wait_ge(se.h, se.n)
                last = None
                for hh in range(2):
                    last = t.matmul(ps_sT[0][:, hh * 512:(hh + 1) * 512], ones_row_f[:],
                                    nrow[0:1, hh * 512:(hh + 1) * 512], start=True, stop=True)
                pe.inc(last)
            TEN(_)
            def _(v, h=h):
                v.wait_ge(pe.h, pe.n)
                v.wait_ge(gd.h, gd.n)
                if _DEV_DUMP:
                    ve.inc(v.tensor_copy(srow[0:1, 4:5], biasC[0:1, 0:1]))
                    ve.inc(v.tensor_copy(srow[0:1, 5:6], ps_rs[0:1, 0:1]))
                    ve.inc(v.tensor_copy(srow[0:1, 6:7], ps_rs[0:1, 17:18]))
                    ve.inc(v.tensor_copy(srow[0:1, 7:8], ps_OT[0:1, 0:1]))
                ve.inc(v.tensor_copy(recip_sb[:], ps_sT[0][:]))
                v.wait_ge(ve.h, ve.n)
                ve.inc(v.tensor_tensor(ht[:], ps_OT[:], recip_sb[:], ALU.mult))
            VEC(_)
            if _DEV_DUMP:
                def _(g, h=h):
                    g.wait_ge(ve.h, ve.n)
                    gd.inc(g.dma_start(out=dbg_stats.ap()[h:h + 1, :], in_=srow[:]), 16)
                GPS(_)
            def _(g, h=h):
                g.wait_ge(ve.h, ve.n)
                gd.inc(g.dma_start(out=heads_dram.ap()[h * 128:(h + 1) * 128, :], in_=ht[:]), 16)
            GPS(_)
            sc2.close()
            head_end_pe[h] = pe.n
        s2.close()

        s3 = ExitStack()
        heads_sb = P.sb("at_heads", [128, NH * TC], BF16, s3)
        osl = [P.sb(f"at_osl{i}", [128, 16 * 512], BF16, s3, is_transpose_dest=True, align_bytes=4096) for i in range(2)]
        ps_o = [P.ps(f"at_ps_o{i}", [128, TC], F32, s3) for i in range(2)]
        def _(g):
            g.wait_ge(gd.h, gd.n)
            g.wait_ge(pe.h, pe.n)
            g.wait_ge(ve.h, ve.n)
            g.wait_ge(se.h, se.n)
            src = heads_dram.ap().rearrange("(hh p) t -> p hh t", p=128)
            gd.inc(g.dma_start(out=heads_sb[:].rearrange("p (hh t) -> p hh t", t=TC), in_=src), 16)
        GPS(_)
        ofetch = {}
        omm = {}
        oev = {}
        for og in range(4):
            def _(q, og=og):
                if og >= 2:
                    q.wait_ge(pe.h, omm[og - 2])
                else:
                    q.wait_ge(pe.h, pe.n)
                    q.wait_ge(ve.h, ve.n)
                    q.wait_ge(se.h, se.n)
                    q.wait_ge(gd.h, gd.n)
                for odc in range(KC):
                    tdma_attnA(q, osl[og % 2], odc * 512, D + LAT + og * 512, 512, odc)
                ofetch[og] = qd.n
            SYNC(_)
            for oc4 in range(4):
                oc = og * 4 + oc4
                bo = oc % 2
                def _(t, og=og, oc4=oc4, oc=oc, bo=bo):
                    t.wait_ge(qd.h, ofetch[og])
                    if oc == 0:
                        t.wait_ge(gd.h, gd.n)
                    if oc >= 2:
                        t.wait_ge(ve.h, oev[oc - 2])
                    last = None
                    for odc in range(KC):
                        for hh in range(2):
                            last = t.matmul(ps_o[bo][:, hh * 512:(hh + 1) * 512],
                                            osl[og % 2][:, odc * 512 + oc4 * 128:odc * 512 + oc4 * 128 + 128],
                                            heads_sb[:, odc * TC + hh * 512:odc * TC + (hh + 1) * 512],
                                            start=(odc == 0), stop=(odc == KC - 1))
                    pe.inc(last)
                    omm[og] = pe.n
                TEN(_)
                def _(v, oc=oc, bo=bo):
                    v.wait_ge(pe.h, pe.n)
                    ve.inc(v.tensor_tensor(xT[:, oc * TC:(oc + 1) * TC],
                                           xT[:, oc * TC:(oc + 1) * TC],
                                           ps_o[bo][:], ALU.add))
                    oev[oc] = ve.n
                VEC(_)
        s3.close()
        ph.close()

    if stop in ("ffn1", "attn", "full"):
        emit_ffn("f1")
    if stop in ("attn", "full"):
        emit_attn()
    if stop == "full":
        emit_ffn("f2")

    phE = ExitStack()
    xo_sb = P.sb("xo_sb", [128, D], F32, phE)
    ps_e = [P.ps(f"ps_e{i}", [128, 512], F32, phE) for i in range(2)]
    for tt in range(TT):
        for kg in range(4):
            bb = (tt * 4 + kg) % 2
            def _(t, tt=tt, kg=kg, bb=bb):
                if tt == 0 and kg == 0:
                    t.wait_ge(ve.h, ve.n)
                    t.wait_ge(cc.h, cc.n)
                t.wait_ge(ve.h, ve.n)
                last = None
                for j in range(4):
                    kc = kg * 4 + j
                    last = t.transpose(ps_e[bb][:, j * 128:(j + 1) * 128],
                                       xT[:, kc * TC + tt * 128:kc * TC + (tt + 1) * 128],
                                       identf[:])
                pe.inc(last)
            TEN(_)
            def _(v, tt=tt, kg=kg, bb=bb):
                v.wait_ge(pe.h, pe.n)
                if kg == 0:
                    v.wait_ge(ad.h, ad.n)
                    v.wait_ge(gd.h, gd.n)
                ve.inc(v.tensor_copy(xo_sb[:, kg * 512:(kg + 1) * 512], ps_e[bb][:]))
            VEC(_)
        def _(q, tt=tt):
            q.wait_ge(ve.h, ve.n)
            ad.inc(q.dma_start(out=y_out.ap()[tt * 128:(tt + 1) * 128, :], in_=xo_sb[:]), 16)
        SCA(_)
    def _(q):
        q.wait_ge(ad.h, ad.n)
    SCA(_)
    phE.close()

    P.st.close()
    return nc


def kernel(**inputs):
    x = np.asarray(inputs["L0_attn_hidden_states"], np.float32)
    q_w = np.asarray(inputs["L0_attn_q_w"], np.float32)
    kv_a_w = np.asarray(inputs["L0_attn_kv_a_w"], np.float32)
    kv_b_w = np.asarray(inputs["L0_attn_kv_b_w"], np.float32)
    o_w = np.asarray(inputs["L0_attn_o_w"], np.float32)
    kv_norm_w = np.asarray(inputs["L0_attn_kv_norm_w"], np.float32)
    g_pk = np.asarray(inputs["L0_ffn_gate_W_packed"], np.int32)
    g_sc = np.asarray(inputs["L0_ffn_gate_scales"], np.float32)
    u_pk = np.asarray(inputs["L0_ffn_up_W_packed"], np.int32)
    u_sc = np.asarray(inputs["L0_ffn_up_scales"], np.float32)
    d_pk = np.asarray(inputs["L0_ffn_down_W_packed"], np.int32)
    d_sc = np.asarray(inputs["L0_ffn_down_scales"], np.float32)

    Pd = _perm(D)
    Pn = _perm(NMLP)

    xf = np.ascontiguousarray(x.reshape(T, D)[:, Pd])
    q_wp = np.ascontiguousarray(q_w[:, Pd])
    kva_p = np.ascontiguousarray(kv_a_w[ROPE:, Pd])
    o_wp = np.ascontiguousarray(o_w[Pd, :])
    attn_stack = np.concatenate([q_wp, kva_p, o_wp], axis=0)
    g_pkp = np.ascontiguousarray(g_pk[Pn]); g_scp = np.ascontiguousarray(g_sc[Pn])
    u_pkp = np.ascontiguousarray(u_pk[Pn]); u_scp = np.ascontiguousarray(u_sc[Pn])
    d_pkp = np.ascontiguousarray(d_pk[Pd]); d_scp = np.ascontiguousarray(d_sc[Pd])
    kvn_dev = np.ascontiguousarray(kv_norm_w.reshape(LC, 128).T)

    cos, sin = _yarn_cos_sin()
    cosk = np.ascontiguousarray(cos.T)
    sink = np.ascontiguousarray(sin.T)
    cosq_s = (cos.T * np.float32(SCALE)).astype(np.float32)
    sinq_s = (sin.T * np.float32(SCALE)).astype(np.float32)

    import ml_dtypes
    identb = np.eye(128, dtype=ml_dtypes.bfloat16)
    identf = np.eye(128, dtype=np.float32)

    in_maps = []
    for c in range(NCORES):
        sq = (c % GRP) * TC
        in_maps.append({
            "x_in": np.ascontiguousarray(xf[c * TC:(c + 1) * TC]),
            "gpk": np.ascontiguousarray(g_pkp[c * GATE_SH:(c + 1) * GATE_SH]),
            "gsc": np.ascontiguousarray(g_scp[c * GATE_SH:(c + 1) * GATE_SH]),
            "upk": np.ascontiguousarray(u_pkp[c * GATE_SH:(c + 1) * GATE_SH]),
            "usc": np.ascontiguousarray(u_scp[c * GATE_SH:(c + 1) * GATE_SH]),
            "dpk": np.ascontiguousarray(d_pkp[c * DOWN_SH:(c + 1) * DOWN_SH]),
            "dsc": np.ascontiguousarray(d_scp[c * DOWN_SH:(c + 1) * DOWN_SH]),
            "attnA": np.ascontiguousarray(attn_stack[c * ATTN_SH:(c + 1) * ATTN_SH]),
            "kvbW": np.ascontiguousarray(kv_b_w[c * KVB_SH:(c + 1) * KVB_SH]),
            "kvnorm": kvn_dev,
            "cosq": np.ascontiguousarray(cosq_s[:, sq:sq + TC]),
            "sinq": np.ascontiguousarray(sinq_s[:, sq:sq + TC]),
            "cosk": cosk,
            "sink": sink,
            "identb": identb,
            "identf": identf,
        })

    nc = _build()
    res = run_bass_kernel_spmd(nc, in_maps, core_ids=list(range(NCORES)))

    out = np.empty((T, D), np.float32)
    for c in range(NCORES):
        out[c * TC:(c + 1) * TC] = res.results[c]["y"]
    inv = np.empty((T, D), np.float32)
    inv[:, Pd] = out
    return inv.reshape(B, S, D)


# version 03
# speedup vs baseline: 1.0024x; 1.0024x over previous
import math
import numpy as np
from contextlib import ExitStack

import concourse.bass as bass
import concourse.mybir as mybir
from concourse.bass_utils import run_bass_kernel_spmd

F32 = mybir.dt.float32
BF16 = mybir.dt.bfloat16
I32 = mybir.dt.int32
AF = mybir.ActivationFunctionType
ALU = mybir.AluOpType
AX = mybir.AxisListType.X

D = 2048
KC = D // 128
NMLP = 16384
NH = 16
HD = 128
ROPE = 64
LAT = 512
LC = LAT // 128
B, S = 2, 4096
T = B * S
NCORES = 8
TC = T // NCORES
TT = TC // 128
SCALE = float(HD) ** -0.5
GRP = NCORES // B
SEQ = S
KB = SEQ // TC

GATE_SH = NMLP // NCORES
DOWN_SH = D // NCORES
ATTN_ROWS = D + LAT + D
ATTN_SH = ATTN_ROWS // NCORES
KVB_SH = (2 * D) // NCORES
A_BLK = 2 * GATE_SH + ATTN_SH
A_ATTN_OFF = 2 * GATE_SH
C_DOWN_ROWS = DOWN_SH
C_KVB_ROWS = (KVB_SH * LAT) // NMLP
C_BLK = C_DOWN_ROWS + C_KVB_ROWS

_DEV_STOP = "full"
_DEV_DUMP = False


def _yarn_cos_sin():
    ROPE_DIM = 64
    MAX_POS = 131072
    ORIG_POS = 8192
    BASE = 500000.0
    BETA_FAST, BETA_SLOW = (32.0, 1.0)
    scaling = MAX_POS / ORIG_POS
    pos_freqs = BASE ** (np.arange(0, ROPE_DIM, 2, dtype=np.float64) / ROPE_DIM)
    extrap = 1.0 / pos_freqs
    interp = 1.0 / (scaling * pos_freqs)
    low = max(math.floor(ROPE_DIM * math.log(ORIG_POS / (BETA_FAST * 2 * math.pi)) / (2 * math.log(BASE))), 0)
    high = min(math.ceil(ROPE_DIM * math.log(ORIG_POS / (BETA_SLOW * 2 * math.pi)) / (2 * math.log(BASE))), ROPE_DIM - 1)
    i = np.arange(ROPE_DIM // 2, dtype=np.float64)
    smooth = np.clip((i - low) / max(high - low, 1), 0.0, 1.0)
    inv_freq = np.asarray((1.0 - smooth) * interp + smooth * extrap, dtype=np.float32)
    pos = np.arange(S, dtype=np.float32)
    freqs = pos[:, None] * inv_freq[None, :]
    emb = np.concatenate([freqs, freqs], axis=-1)
    mscale = np.float32(0.1 * math.log(scaling) + 1.0)
    cos = (np.cos(emb) * mscale).astype(np.float32)
    sin = (np.sin(emb) * mscale).astype(np.float32)
    return cos, sin


def _perm(n):
    return np.concatenate([np.arange(0, n, 2), np.arange(1, n, 2)])


class Sem:
    def __init__(self, st, nc, name):
        self.h = st.enter_context(nc.semaphore(name))
        self.n = 0

    def inc(self, instr, k=1):
        instr.then_inc(self.h, k)
        self.n += k
        return self.n


def APr(handle, offset, dims):
    return bass.AP(handle, int(offset), [[int(s), int(n)] for s, n in dims])


class Prog:

    def __init__(self):
        self.nc = bass.Bass("TRN2", target_bir_lowering=False, debug=False)
        self.st = ExitStack()

    def sem(self, name):
        return Sem(self.st, self.nc, name)

    def sb(self, name, shape, dtype, stack=None, **kw):
        return (stack or self.st).enter_context(self.nc.sbuf_tensor(name, shape, dtype, **kw))

    def ps(self, name, shape, dtype=F32, stack=None):
        return (stack or self.st).enter_context(self.nc.psum_tensor(name, shape, dtype))


def _build(stop=None):
    stop = stop or _DEV_STOP
    P = Prog()
    nc = P.nc

    x_in = nc.dram_tensor("x_in", [TC, D], F32, kind="ExternalInput")
    gpk = nc.dram_tensor("gpk", [GATE_SH, D // 2], I32, kind="ExternalInput")
    gsc = nc.dram_tensor("gsc", [GATE_SH, D // 32], F32, kind="ExternalInput")
    upk = nc.dram_tensor("upk", [GATE_SH, D // 2], I32, kind="ExternalInput")
    usc = nc.dram_tensor("usc", [GATE_SH, D // 32], F32, kind="ExternalInput")
    dpk = nc.dram_tensor("dpk", [DOWN_SH, NMLP // 2], I32, kind="ExternalInput")
    dsc = nc.dram_tensor("dsc", [DOWN_SH, NMLP // 32], F32, kind="ExternalInput")
    attnA_in = nc.dram_tensor("attnA", [ATTN_SH, D], F32, kind="ExternalInput")
    kvbW_in = nc.dram_tensor("kvbW", [KVB_SH, LAT], F32, kind="ExternalInput")
    kvnorm_in = nc.dram_tensor("kvnorm", [128, LC], F32, kind="ExternalInput")
    cosq_in = nc.dram_tensor("cosq", [ROPE, TC], F32, kind="ExternalInput")
    sinq_in = nc.dram_tensor("sinq", [ROPE, TC], F32, kind="ExternalInput")
    cosk_in = nc.dram_tensor("cosk", [ROPE, SEQ], F32, kind="ExternalInput")
    sink_in = nc.dram_tensor("sink", [ROPE, SEQ], F32, kind="ExternalInput")
    identb_in = nc.dram_tensor("identb", [128, 128], BF16, kind="ExternalInput")
    identf_in = nc.dram_tensor("identf", [128, 128], F32, kind="ExternalInput")
    y_out = nc.dram_tensor("y", [TC, D], F32, kind="ExternalOutput")

    bncA = nc.dram_tensor("bncA", [A_BLK, D], BF16)
    gathA = nc.dram_tensor("gathA", [NCORES * A_BLK, D], BF16, addr_space="Shared")
    bncC = nc.dram_tensor("bncC", [C_BLK, NMLP], BF16)
    gathC = nc.dram_tensor("gathC", [NCORES * C_BLK, NMLP], BF16, addr_space="Shared")
    dbg = "ExternalOutput" if _DEV_DUMP else "Internal"
    latn_my = nc.dram_tensor("latn_my", [LAT, TC], BF16)
    latn_all = nc.dram_tensor("latn_all", [GRP * LAT, TC], BF16)
    dbg_latn = nc.dram_tensor("dbg_latn", [GRP * LAT, TC], BF16, kind="ExternalOutput") if _DEV_DUMP else None
    qT_dram = nc.dram_tensor("qT_dram", [NH * 128, TC], BF16, kind=dbg)
    heads_dram = nc.dram_tensor("heads_dram", [NH * 128, TC], BF16, kind=dbg)
    dbg_stats = nc.dram_tensor("dbg_stats", [NH, 8], F32, kind=dbg) if _DEV_DUMP else None
    dbg_latf = nc.dram_tensor("dbg_latf", [128, LC * TC], F32, kind=dbg) if _DEV_DUMP else None
    dbg_row = nc.dram_tensor("dbg_row", [1, TC], F32, kind=dbg) if _DEV_DUMP else None
    dbg_chain = nc.dram_tensor("dbg_chain", [1, 4 * TC], F32, kind=dbg) if _DEV_DUMP else None
    dbg_kT = nc.dram_tensor("dbg_kT", [128, SEQ], BF16, kind=dbg) if _DEV_DUMP else None
    dbg_vB = nc.dram_tensor("dbg_vB", [128, SEQ], BF16, kind=dbg) if _DEV_DUMP else None
    dbg_vB2 = nc.dram_tensor("dbg_vB2", [128, SEQ], BF16, kind=dbg) if _DEV_DUMP else None
    dbg_kvb = nc.dram_tensor("dbg_kvb", [128, LC * 256], BF16, kind=dbg) if _DEV_DUMP else None
    dbg_bncC = nc.dram_tensor("dbg_bncC", [16, NMLP], BF16, kind=dbg) if _DEV_DUMP else None
    dbg_gathC = nc.dram_tensor("dbg_gathC", [32, NMLP], BF16, kind=dbg) if _DEV_DUMP else None
    dbg_xtok = nc.dram_tensor("dbg_xtok", [128, 512], F32, kind=dbg) if _DEV_DUMP else None
    dbg_deq = nc.dram_tensor("dbg_deq", [128, 512], BF16, kind=dbg) if _DEV_DUMP else None
    dbg_bnc2 = nc.dram_tensor("dbg_bnc2", [4, NMLP], BF16, kind=dbg) if _DEV_DUMP else None
    dbg_pT = nc.dram_tensor("dbg_pT", [128, TC], BF16, kind=dbg) if _DEV_DUMP else None

    qd = P.sem("qdma")
    ad = P.sem("adma")
    gd = P.sem("gdma")
    pe = P.sem("pe")
    ve = P.sem("ve")
    se = P.sem("se")
    po = P.sem("po")
    cc = P.sem("cc")

    xT = P.sb("xT", [128, KC * TC], F32)
    identb = P.sb("identb_sb", [128, 128], BF16)
    identf = P.sb("identf_sb", [128, 128], F32)
    ones_col_f = P.sb("ones_col_f", [128, 1], F32)
    ones_row_f = P.sb("ones_row_f", [1, 128], F32)
    ones_col_b = P.sb("ones_col_b", [128, 1], BF16)
    kvn_sb = P.sb("kvn_sb", [128, LC], F32)

    block = P.st.enter_context(nc.Block())

    def SYNC(f):
        block.sync(f)

    def TEN(f):
        block.tensor(f)

    def VEC(f):
        block.vector(f)

    def SCA(f):
        block.scalar(f)

    def GPS(f):
        block.gpsimd(f)

    phA = ExitStack()
    x_tok = [P.sb(f"x_tok{i}", [128, D], F32, phA) for i in range(2)]
    ps_tr = [P.ps(f"ps_tr{i}", [128, 512], F32, phA) for i in range(2)]

    def _(q):
        q.dma_start(out=identb[:], in_=identb_in.ap()).then_inc(qd.h, 16)
        q.dma_start(out=identf[:], in_=identf_in.ap()).then_inc(qd.h, 16)
        q.dma_start(out=kvn_sb[:], in_=kvnorm_in.ap()).then_inc(qd.h, 16)
        qd.n += 48
    SYNC(_)

    def _(g):
        g.memset(ones_col_f[:], 1.0)
        g.memset(ones_row_f[:], 1.0)
        po.inc(g.memset(ones_col_b[:], 1.0), 3)
    GPS(_)

    xtok_ready = [None] * TT
    xtok_done = [None] * TT
    evac_cnt = {}

    def _emit_xdma(i):
        def _(q, i=i):
            if i >= 2:
                q.wait_ge(pe.h, xtok_done[i - 2])
            qd.inc(q.dma_start(out=x_tok[i % 2][:], in_=x_in.ap()[i * 128:(i + 1) * 128, :]), 16)
            xtok_ready[i] = qd.n
        SYNC(_)

    _emit_xdma(0)
    _emit_xdma(1)
    for i in range(TT):
        for kg in range(4):
            bb = (i * 4 + kg) % 2
            def _(t, i=i, kg=kg, bb=bb):
                if kg == 0:
                    t.wait_ge(qd.h, xtok_ready[i])
                    t.wait_ge(po.h, po.n)
                key = i * 4 + kg
                if key >= 2:
                    t.wait_ge(ve.h, evac_cnt[key - 2])
                last = None
                for j in range(4):
                    kc = kg * 4 + j
                    last = t.transpose(ps_tr[bb][:, j * 128:(j + 1) * 128],
                                       x_tok[i % 2][:, kc * 128:(kc + 1) * 128], identf[:])
                pe.inc(last)
            TEN(_)
            def _(v, i=i, kg=kg, bb=bb):
                v.wait_ge(pe.h, pe.n)
                src = ps_tr[bb][:].rearrange("p (j c) -> p j c", c=128)
                dst = APr(xT, (kg * 4) * TC + i * 128, [[KC * TC, 128], [TC, 4], [1, 128]])
                ve.inc(v.tensor_copy(dst, src))
                evac_cnt[i * 4 + kg] = ve.n
            VEC(_)
        xtok_done[i] = pe.n
        if i + 2 < TT:
            _emit_xdma(i + 2)

    dq = ExitStack()
    w_sb = P.sb("w_sb", [128, 2048], I32, dq)
    sc_sb = P.sb("sc_sb", [128, 128], F32, dq)
    bc_ = P.sb("dq_c", [128, 2048], I32, dq)
    bc8 = P.sb("dq_c8", [128, 2048], I32, dq)
    bs31 = P.sb("dq_s31", [128, 2048], I32, dq)
    bt1 = P.sb("dq_t1", [128, 2048], I32, dq)
    bu = P.sb("dq_u", [128, 2048], I32, dq)
    deq_sb = P.sb("deq_sb", [128, 4096], BF16, dq)
    wT_sb = P.sb("wT_sb", [128, 4096], BF16, dq)
    ps_wq = [P.ps(f"ps_wq{i}", [128, 512], BF16, dq) for i in range(2)]

    def emit_dq_unit(wait_dma_cnt, Fk, sc_cols, out_planes, fold05):
        NB = Fk // 16
        for plane in range(2):
            def _(v, plane=plane):
                c = bc_[:, 0:Fk]
                c8 = bc8[:, 0:Fk]
                s31 = bs31[:, 0:Fk]
                t1 = bt1[:, 0:Fk]
                u = bu[:, 0:Fk]
                v.wait_ge(qd.h, wait_dma_cnt)
                v.wait_ge(ve.h, ve.n)
                if plane == 0:
                    ve.inc(v.tensor_scalar(c, w_sb[:, 0:Fk], 15, None, ALU.bitwise_and))
                else:
                    ve.inc(v.tensor_scalar(c, w_sb[:, 0:Fk], 4, None, ALU.logical_shift_right))
                v.wait_ge(ve.h, ve.n)
                ve.inc(v.tensor_scalar(c8, c, 7, None, ALU.bitwise_and))
                v.wait_ge(ve.h, ve.n)
                ve.inc(v.tensor_scalar(s31, c, 3, 31, ALU.logical_shift_right, ALU.logical_shift_left))
                v.wait_ge(ve.h, ve.n)
                ve.inc(v.tensor_scalar(t1, c8, -251, 252, ALU.mult, ALU.add))
                v.wait_ge(ve.h, ve.n)
                ve.inc(v.tensor_scalar(t1, t1, 0, None, ALU.max))
                v.wait_ge(ve.h, ve.n)
                ve.inc(v.tensor_scalar(u, c8, 1 << 22, 0x3F000000, ALU.mult, ALU.add))
                v.wait_ge(ve.h, ve.n)
                ve.inc(v.scalar_tensor_tensor(c8, t1, -(1 << 22), u, ALU.mult, ALU.add))
                v.wait_ge(ve.h, ve.n)
                ve.inc(v.tensor_tensor(c8, c8, s31, ALU.bitwise_or))
                v.wait_ge(ve.h, ve.n)
                bits_f = c8.bitcast(F32).rearrange("p (b j) -> p b j", j=16)
                sc_view = sc_sb[:, 0:NB, None].broadcast_to([128, NB, 16])
                out = out_planes[plane]
                if fold05:
                    ve.inc(v.scalar_tensor_tensor(out, bits_f, 0.5, sc_view, ALU.mult, ALU.mult))
                else:
                    ve.inc(v.tensor_tensor(out, bits_f, sc_view, ALU.mult))
            VEC(_)

    def emit_shard_transpose_and_store(n_pieces, dst_fn):
        for grp4 in range((n_pieces + 3) // 4):
            bb = grp4 % 2
            def _(t, grp4=grp4, bb=bb):
                t.wait_ge(ve.h, ve.n)
                last = None
                for j in range(4):
                    pc = grp4 * 4 + j
                    last = t.transpose(ps_wq[bb][:, j * 128:(j + 1) * 128],
                                       deq_sb[:, pc * 128:(pc + 1) * 128], identb[:])
                pe.inc(last)
            TEN(_)
            def _(v, grp4=grp4, bb=bb):
                v.wait_ge(pe.h, pe.n)
                v.wait_ge(gd.h, gd.n)
                ve.inc(v.tensor_copy(wT_sb[:, grp4 * 512:(grp4 + 1) * 512], ps_wq[bb][:]))
            VEC(_)
        for gi in range(n_pieces // 16):
            def _(g, gi=gi):
                g.wait_ge(ve.h, ve.n)
                dst, src = dst_fn(gi)
                gd.inc(g.dma_start(out=dst, in_=src), 16)
            GPS(_)

    att_tiles = [(0, 128), (128, 128), (256, 128), (384, 128), (512, 64)]
    for (r0, nr) in att_tiles:
        def _(q, r0=r0, nr=nr):
            q.wait_ge(ve.h, ve.n)
            qd.inc(q.dma_start(out=x_tok[0][0:nr, :], in_=attnA_in.ap()[r0:r0 + nr, :]), 16)
        SYNC(_)
        def _(v, r0=r0, nr=nr):
            v.wait_ge(qd.h, qd.n)
            v.wait_ge(gd.h, gd.n)
            ve.inc(v.tensor_copy(deq_sb[0:nr, 0:2048], x_tok[0][0:nr, :]))
        VEC(_)
        def _(g, r0=r0, nr=nr):
            g.wait_ge(ve.h, ve.n)
            gd.inc(g.dma_start(out=bncA.ap()[A_ATTN_OFF + r0:A_ATTN_OFF + r0 + nr, :],
                               in_=deq_sb[0:nr, 0:2048]), 16)
        GPS(_)
    for r in range(KVB_SH // 128):
        def _(q, r=r):
            q.wait_ge(ve.h, ve.n)
            qd.inc(q.dma_start(out=x_tok[0][:, 0:512], in_=kvbW_in.ap()[r * 128:(r + 1) * 128, :]), 16)
        SYNC(_)
        def _(v, r=r):
            v.wait_ge(qd.h, qd.n)
            v.wait_ge(gd.h, gd.n)
            ve.inc(v.tensor_copy(deq_sb[:, 0:512], x_tok[0][:, 0:512]))
        VEC(_)
        def _(g, r=r):
            g.wait_ge(ve.h, ve.n)
            gd.inc(g.dma_start(out=APr(bncC, C_DOWN_ROWS * NMLP + r * 128 * 512,
                                       [[512, 128], [1, 512]]),
                               in_=deq_sb[:, 0:512]), 16)
            if _DEV_DUMP and r == 3:
                gd.inc(g.dma_start(out=dbg_xtok.ap(), in_=x_tok[0][:, 0:512]), 16)
                gd.inc(g.dma_start(out=dbg_deq.ap(), in_=deq_sb[:, 0:512]), 16)
                g.wait_ge(gd.h, gd.n)
                gd.inc(g.dma_start(out=dbg_bnc2.ap(),
                                   in_=bncC.ap()[C_DOWN_ROWS + 12:C_DOWN_ROWS + 16, :]), 16)
        GPS(_)

    for hb in range(DOWN_SH // 128):
        for kq in range(4):
            def _(q, hb=hb, kq=kq):
                q.wait_ge(ve.h, ve.n)
                q.dma_start(out=w_sb[:, 0:2048],
                            in_=dpk.ap()[hb * 128:(hb + 1) * 128, kq * 2048:(kq + 1) * 2048]).then_inc(qd.h, 16)
                q.dma_start(out=sc_sb[:, 0:128],
                            in_=dsc.ap()[hb * 128:(hb + 1) * 128, kq * 128:(kq + 1) * 128]).then_inc(qd.h, 16)
                qd.n += 32
            SYNC(_)
            lo = deq_sb[:, 0:2048].rearrange("p (b j) -> p b j", j=16)
            hi = deq_sb[:, 2048:4096].rearrange("p (b j) -> p b j", j=16)
            def _(v):
                v.wait_ge(pe.h, pe.n)
            VEC(_)
            emit_dq_unit(qd.n, 2048, 128, (lo, hi), True)

            def dst_fn(gi, hb=hb, kq=kq):
                n0 = kq * 2048 + (0 if gi == 0 else NMLP // 2)
                dst = APr(bncC, n0 * DOWN_SH + hb * 128,
                          [[DOWN_SH, 128], [128 * DOWN_SH, 16], [1, 128]])
                src = wT_sb[:, gi * 2048:(gi + 1) * 2048].rearrange("p (kc n) -> p kc n", n=128)
                return dst, src
            emit_shard_transpose_and_store(32, dst_fn)

    def _(g):
        g.wait_ge(gd.h, gd.n)
        g.wait_ge(ve.h, ve.n)
        cc.inc(g.collective_compute(
            "AllGather", ALU.bypass,
            replica_groups=[list(range(NCORES))],
            ins=[bncC.ap().opt()], outs=[gathC.ap().opt()]))
    GPS(_)

    for seg, (pk_t, sc_t, blob_row0) in enumerate([(gpk, gsc, 0), (upk, usc, GATE_SH)]):
        for r in range(GATE_SH // 128):
            def _(q, pk_t=pk_t, sc_t=sc_t, r=r):
                q.wait_ge(ve.h, ve.n)
                q.dma_start(out=w_sb[:, 0:1024], in_=pk_t.ap()[r * 128:(r + 1) * 128, :]).then_inc(qd.h, 16)
                q.dma_start(out=sc_sb[:, 0:64], in_=sc_t.ap()[r * 128:(r + 1) * 128, :]).then_inc(qd.h, 16)
                qd.n += 32
            SYNC(_)
            lo = deq_sb[:, 0:1024].rearrange("p (b j) -> p b j", j=16)
            hi = deq_sb[:, 1024:2048].rearrange("p (b j) -> p b j", j=16)
            def _(v):
                v.wait_ge(pe.h, pe.n)
            VEC(_)
            emit_dq_unit(qd.n, 1024, 64, (lo, hi), False)

            def dst_fn(gi, r=r, blob_row0=blob_row0):
                dst = APr(bncA, blob_row0 * D + r * 128,
                          [[D, 128], [128 * D, 16], [1, 128]])
                src = wT_sb[:, 0:2048].rearrange("p (kc n) -> p kc n", n=128)
                return dst, src
            emit_shard_transpose_and_store(16, dst_fn)

    def _(g):
        g.wait_ge(gd.h, gd.n)
        g.wait_ge(ve.h, ve.n)
        cc.inc(g.collective_compute(
            "AllGather", ALU.bypass,
            replica_groups=[list(range(NCORES))],
            ins=[bncA.ap().opt()], outs=[gathA.ap().opt()]))
    GPS(_)

    dq.close()
    phA.close()

    def emit_ln(dst, scr, rstd_t, ps_st):
        ps_sum, ps_sq, ps_mb, ps_rb, rows = ps_st
        for kc in range(KC):
            def _(v, kc=kc):
                if kc == 0:
                    v.wait_ge(ve.h, ve.n)
                    v.wait_ge(pe.h, pe.n)
                    v.wait_ge(gd.h, gd.n)
                    v.wait_ge(ad.h, ad.n)
                else:
                    v.wait_ge(pe.h, pe.n)
                ve.inc(v.tensor_tensor(scr[kc % 2][:], xT[:, kc * TC:kc * TC + TC],
                                       xT[:, kc * TC:kc * TC + TC], ALU.mult))
            VEC(_)
            def _(t, kc=kc):
                t.wait_ge(ve.h, ve.n)
                for h in range(2):
                    t.matmul(ps_sum[0:1, h * 512:(h + 1) * 512], ones_col_f[:],
                             xT[:, kc * TC + h * 512:kc * TC + (h + 1) * 512],
                             start=(kc == 0), stop=(kc == KC - 1))
                last = None
                for h in range(2):
                    last = t.matmul(ps_sq[0:1, h * 512:(h + 1) * 512], ones_col_f[:],
                                    scr[kc % 2][:, h * 512:(h + 1) * 512],
                                    start=(kc == 0), stop=(kc == KC - 1))
                pe.inc(last)
            TEN(_)
        def _(v):
            v.wait_ge(pe.h, pe.n)
            ve.inc(v.tensor_scalar(rows[0][:], ps_sum[0:1, :], 1.0 / D, None, ALU.mult))
            v.wait_ge(ve.h, ve.n)
            ve.inc(v.tensor_scalar(rows[1][:], ps_sq[0:1, :], 1.0 / D, None, ALU.mult))
            v.wait_ge(ve.h, ve.n)
            ve.inc(v.tensor_tensor(rows[2][:], rows[0][:], rows[0][:], ALU.mult))
            v.wait_ge(ve.h, ve.n)
            ve.inc(v.scalar_tensor_tensor(rows[1][:], rows[1][:], 1e-5, rows[2][:], ALU.add, ALU.subtract))
        VEC(_)
        def _(t):
            t.wait_ge(ve.h, ve.n)
            for h in range(2):
                t.matmul(ps_mb[:, h * 512:(h + 1) * 512], ones_row_f[:],
                         rows[0][0:1, h * 512:(h + 1) * 512], start=True, stop=True)
            last = None
            for h in range(2):
                last = t.matmul(ps_rb[:, h * 512:(h + 1) * 512], ones_row_f[:],
                                rows[1][0:1, h * 512:(h + 1) * 512], start=True, stop=True)
            pe.inc(last)
        TEN(_)
        def _(s):
            s.wait_ge(pe.h, pe.n)
            s.wait_ge(ve.h, ve.n)
            se.inc(s.activation(scr[0][:], ps_rb[:], AF.Sqrt))
        SCA(_)
        def _(v):
            v.wait_ge(se.h, se.n)
            ve.inc(v.reciprocal(rstd_t[:], scr[0][:]))
        VEC(_)
        for kc in range(KC):
            def _(v, kc=kc):
                v.wait_ge(pe.h, pe.n)
                v.wait_ge(ve.h, ve.n)
                ve.inc(v.tensor_tensor(scr[kc % 2][:], xT[:, kc * TC:kc * TC + TC], ps_mb[:], ALU.subtract))
                v.wait_ge(ve.h, ve.n)
                ve.inc(v.tensor_tensor(dst[:, kc * TC:kc * TC + TC], scr[kc % 2][:], rstd_t[:], ALU.mult))
            VEC(_)

    def emit_ffn(tag):
        ph = ExitStack()
        xnT = P.sb(f"{tag}_xnT", [128, KC * TC], BF16, ph)
        interT = P.sb(f"{tag}_inter", [128, 16 * TC], BF16, ph)
        scr = [P.sb(f"{tag}_scr{i}", [128, TC], F32, ph) for i in range(2)]
        rows = [P.sb(f"{tag}_row{i}", [1, TC], F32, ph) for i in range(3)]
        wg_sl = [P.sb(f"{tag}_wg{i}", [128, 2048], BF16, ph) for i in range(2)]
        wu_sl = [P.sb(f"{tag}_wu{i}", [128, 2048], BF16, ph) for i in range(2)]
        wd_sl = [P.sb(f"{tag}_wd{i}", [128, 8192], BF16, ph) for i in range(2)]
        rstd_t = P.sb(f"{tag}_rstd", [128, TC], F32, ph)

        lnps = ExitStack()
        ps_st = (P.ps(f"{tag}_ln_sum", [1, TC], F32, lnps),
                 P.ps(f"{tag}_ln_sq", [1, TC], F32, lnps),
                 P.ps(f"{tag}_ln_mb", [128, TC], F32, lnps),
                 P.ps(f"{tag}_ln_rb", [128, TC], F32, lnps),
                 rows)
        emit_ln(xnT, scr, rstd_t, ps_st)
        lnps.close()

        ps_g = P.ps(f"{tag}_ps_g", [128, TC], F32, ph)
        ps_u = [P.ps(f"{tag}_ps_u{i}", [128, TC], F32, ph) for i in range(2)]
        ps_d = [P.ps(f"{tag}_ps_d{i}", [128, 512], F32, ph) for i in range(2)]

        ln_done = ve.n
        g_mm = {}
        gu_mm = {}
        gelu_done = {}
        mult_done = {}
        fetch_cnt = {}
        fetchd_cnt = {}
        d_read = {}
        slabd_mm = {}

        for g in range(8):
            for nci in range(16):
                nabs = g * 16 + nci
                bw = nabs % 2

                def _(q, g=g, nci=nci, nabs=nabs, bw=bw):
                    if nabs >= 2:
                        q.wait_ge(pe.h, gu_mm[nabs - 2])
                    if nabs == 0:
                        q.wait_ge(cc.h, cc.n)
                        q.wait_ge(gd.h, gd.n)
                        q.wait_ge(ad.h, ad.n)
                    if nabs < 2:
                        q.wait_ge(ve.h, ln_done)
                    src_g = APr(gathA, (g * A_BLK) * D + nci * 128,
                                [[D, 128], [128 * D, 16], [1, 128]])
                    src_u = APr(gathA, (g * A_BLK + GATE_SH) * D + nci * 128,
                                [[D, 128], [128 * D, 16], [1, 128]])
                    q.dma_start(out=wg_sl[bw][:].rearrange("p (kc n) -> p kc n", n=128),
                                in_=src_g).then_inc(qd.h, 16)
                    q.dma_start(out=wu_sl[bw][:].rearrange("p (kc n) -> p kc n", n=128),
                                in_=src_u).then_inc(qd.h, 16)
                    qd.n += 32
                    fetch_cnt[nabs] = qd.n
                SYNC(_)

                def _(t, nabs=nabs, bw=bw):
                    t.wait_ge(qd.h, fetch_cnt[nabs])
                    if nabs == 0:
                        t.wait_ge(ve.h, ln_done)
                    if nabs >= 1:
                        t.wait_ge(se.h, gelu_done[nabs - 1])
                    if nabs >= 2:
                        t.wait_ge(ve.h, mult_done[nabs - 2])
                    last = None
                    for kc in range(KC):
                        for h in range(2):
                            last = t.matmul(ps_g[:, h * 512:(h + 1) * 512],
                                            wg_sl[bw][:, kc * 128:(kc + 1) * 128],
                                            xnT[:, kc * TC + h * 512:kc * TC + (h + 1) * 512],
                                            start=(kc == 0), stop=(kc == KC - 1))
                    pe.inc(last)
                    g_mm[nabs] = pe.n
                    for kc in range(KC):
                        for h in range(2):
                            last = t.matmul(ps_u[bw][:, h * 512:(h + 1) * 512],
                                            wu_sl[bw][:, kc * 128:(kc + 1) * 128],
                                            xnT[:, kc * TC + h * 512:kc * TC + (h + 1) * 512],
                                            start=(kc == 0), stop=(kc == KC - 1))
                    pe.inc(last)
                    gu_mm[nabs] = pe.n
                TEN(_)

                def _(s, nabs=nabs, bw=bw):
                    s.wait_ge(pe.h, g_mm[nabs])
                    if nabs >= 2:
                        s.wait_ge(ve.h, mult_done[nabs - 2])
                    else:
                        s.wait_ge(ve.h, ln_done)
                    se.inc(s.activation(scr[bw][:], ps_g[:], AF.Gelu_apprx_tanh))
                    gelu_done[nabs] = se.n
                SCA(_)

                def _(v, nabs=nabs, bw=bw, nci=nci):
                    v.wait_ge(se.h, gelu_done[nabs])
                    v.wait_ge(pe.h, gu_mm[nabs])
                    ve.inc(v.tensor_tensor(interT[:, nci * TC:(nci + 1) * TC],
                                           scr[bw][:], ps_u[bw][:], ALU.mult))
                    mult_done[nabs] = ve.n
                VEC(_)

            for hq in range(4):
                dseq = g * 4 + hq
                bws = dseq % 2

                def _(q, g=g, hq=hq, dseq=dseq, bws=bws):
                    if dseq >= 2:
                        q.wait_ge(pe.h, slabd_mm[dseq - 2])
                    if dseq < 2:
                        q.wait_ge(ve.h, ln_done)
                    for r in range(2):
                        src = APr(gathC,
                                  (hq * 2 + r) * C_BLK * NMLP + (g * 2048) * DOWN_SH,
                                  [[DOWN_SH, 128], [128 * DOWN_SH, 16], [1, 256]])
                        dst = APr(wd_sl[bws], r * 256, [[8192, 128], [512, 16], [1, 256]])
                        q.dma_start(out=dst, in_=src).then_inc(qd.h, 16)
                        qd.n += 16
                    fetchd_cnt[dseq] = qd.n
                SYNC(_)

                for hc in range(4):
                    habs = hq * 4 + hc
                    for th in range(2):
                        sseq = (dseq * 8) + hc * 2 + th
                        bd = sseq % 2

                        def _(t, g=g, hq=hq, hc=hc, th=th, habs=habs, dseq=dseq, bws=bws, bd=bd, sseq=sseq):
                            t.wait_ge(qd.h, fetchd_cnt[dseq])
                            if hc == 0 and th == 0:
                                t.wait_ge(ve.h, mult_done[g * 16 + 15])
                            if sseq >= 2 and (sseq - 2) in d_read:
                                t.wait_ge(ve.h, d_read[sseq - 2])
                            last = None
                            for nci in range(16):
                                last = t.matmul(ps_d[bd][:],
                                                wd_sl[bws][:, nci * 512 + hc * 128:nci * 512 + (hc + 1) * 128],
                                                interT[:, nci * TC + th * 512:nci * TC + (th + 1) * 512],
                                                start=(nci == 0), stop=(nci == 15))
                            pe.inc(last)
                            if hc == 3 and th == 1:
                                slabd_mm[dseq] = pe.n
                        TEN(_)

                        def _(v, habs=habs, th=th, bd=bd, sseq=sseq):
                            v.wait_ge(pe.h, pe.n)
                            ve.inc(v.tensor_tensor(
                                xT[:, habs * TC + th * 512:habs * TC + (th + 1) * 512],
                                xT[:, habs * TC + th * 512:habs * TC + (th + 1) * 512],
                                ps_d[bd][:], ALU.add))
                            d_read[sseq] = ve.n
                        VEC(_)

        ph.close()

    def tdma_attnA(q, dst_tile, dst_col0, row0, nrows, kc):
        done = 0
        n = 0
        while done < nrows:
            r = row0 + done
            blk = r // ATTN_SH
            within = r % ATTN_SH
            glen = min(ATTN_SH - within, nrows - done)
            src = APr(gathA, (blk * A_BLK + A_ATTN_OFF + within) * D + kc * 128,
                      [[D, glen], [1, 128]])
            q.dma_start(out=dst_tile[:, dst_col0 + done:dst_col0 + done + glen],
                        in_=src, transpose=True).then_inc(qd.h, 16)
            qd.n += 16
            done += glen
            n += 1
        return n

    def emit_attn():
        ph = ExitStack()
        cosk = P.sb("at_cosk", [ROPE, SEQ], F32, ph)
        sink = P.sb("at_sink", [ROPE, SEQ], F32, ph)

        s1 = ExitStack()
        cosq = P.sb("at_cosq", [ROPE, TC], F32, s1)
        sinq = P.sb("at_sinq", [ROPE, TC], F32, s1)
        xn2 = P.sb("at_xn", [128, KC * TC], BF16, s1)
        scr = [P.sb(f"at_scr{i}", [128, TC], F32, s1) for i in range(2)]
        rows = [P.sb(f"at_row{i}", [1, TC], F32, s1) for i in range(3)]
        qsl = P.sb("at_qsl", [128, 16 * 512], BF16, s1, is_transpose_dest=True, align_bytes=4096)
        rstd_t = P.sb("at_rstd", [128, TC], F32, s1)
        qt_tile = P.sb("at_qt", [128, TC], BF16, s1)
        tmpa = P.sb("at_tmpa", [32, TC], F32, s1)
        tmpb = P.sb("at_tmpb", [32, TC], F32, s1)
        lat_f = P.sb("at_latf", [128, LC * TC], F32, s1)
        lat_b = qsl

        def _(q):
            q.wait_ge(pe.h, pe.n)
            q.wait_ge(ve.h, ve.n)
            q.wait_ge(se.h, se.n)
            q.wait_ge(gd.h, gd.n)
            q.wait_ge(ad.h, ad.n)
            for dst, srct in ((cosk, cosk_in), (sink, sink_in), (cosq, cosq_in), (sinq, sinq_in)):
                q.dma_start(out=dst[:], in_=srct.ap()).then_inc(qd.h, 16)
                qd.n += 16
        SYNC(_)
        def _(a):
            a.wait_ge(pe.h, pe.n)
            a.wait_ge(ve.h, ve.n)
            a.wait_ge(gd.h, gd.n)
        SCA(_)

        lnps = ExitStack()
        ps_st_ln = (P.ps("at_ln_sum", [1, TC], F32, lnps),
                    P.ps("at_ln_sq", [1, TC], F32, lnps),
                    P.ps("at_ln_mb", [128, TC], F32, lnps),
                    P.ps("at_ln_rb", [128, TC], F32, lnps),
                    rows)
        emit_ln(xn2, scr, rstd_t, ps_st_ln)
        lnps.close()

        qps = ExitStack()
        ps_q = [P.ps(f"at_ps_q{i}", [128, TC], F32, qps) for i in range(2)]
        qfetch = {}
        qmm = {}
        qev = {}
        for qg in range(4):
            def _(q, qg=qg):
                if qg >= 1:
                    q.wait_ge(pe.h, qmm[qg - 1])
                for kc in range(KC):
                    tdma_attnA(q, qsl, kc * 512, qg * 512, 512, kc)
                qfetch[qg] = qd.n
            SYNC(_)
            for qc4 in range(4):
                hidx = qg * 4 + qc4
                bq = hidx % 2
                def _(t, qg=qg, qc4=qc4, hidx=hidx, bq=bq):
                    t.wait_ge(qd.h, qfetch[qg])
                    if qc4 == 0 and qg == 0:
                        t.wait_ge(ve.h, ve.n)
                    if hidx >= 2:
                        t.wait_ge(ve.h, qev[hidx - 2])
                    last = None
                    for kc in range(KC):
                        for hh in range(2):
                            last = t.matmul(ps_q[bq][:, hh * 512:(hh + 1) * 512],
                                            qsl[:, kc * 512 + qc4 * 128:kc * 512 + qc4 * 128 + 128],
                                            xn2[:, kc * TC + hh * 512:kc * TC + (hh + 1) * 512],
                                            start=(kc == 0), stop=(kc == KC - 1))
                    pe.inc(last)
                    qmm[qg] = pe.n
                TEN(_)
                def _(v, hidx=hidx, bq=bq):
                    v.wait_ge(pe.h, pe.n)
                    v.wait_ge(ad.h, ad.n)
                    v.wait_ge(qd.h, qd.n)
                    ve.inc(v.tensor_tensor(tmpa[:], ps_q[bq][0:32, :], cosq[0:32, :], ALU.mult))
                    ve.inc(v.tensor_tensor(tmpb[:], ps_q[bq][32:64, :], sinq[0:32, :], ALU.mult))
                    v.wait_ge(ve.h, ve.n)
                    ve.inc(v.tensor_tensor(qt_tile[0:32, :], tmpa[:], tmpb[:], ALU.subtract))
                    ve.inc(v.tensor_tensor(tmpa[:], ps_q[bq][32:64, :], cosq[32:64, :], ALU.mult))
                    ve.inc(v.tensor_tensor(tmpb[:], ps_q[bq][0:32, :], sinq[32:64, :], ALU.mult))
                    v.wait_ge(ve.h, ve.n)
                    ve.inc(v.tensor_tensor(qt_tile[32:64, :], tmpa[:], tmpb[:], ALU.add))
                    ve.inc(v.tensor_scalar(qt_tile[64:128, :], ps_q[bq][64:128, :], SCALE, None, ALU.mult))
                    qev[hidx] = ve.n
                VEC(_)
                def _(a, hidx=hidx):
                    a.wait_ge(ve.h, qev[hidx])
                    ad.inc(a.dma_start(out=qT_dram.ap()[hidx * 128:(hidx + 1) * 128, :],
                                       in_=qt_tile[:]), 16)
                SCA(_)
        qps.close()

        kps = ExitStack()
        ps_l = [P.ps(f"at_ps_l{i}", [128, TC], F32, kps) for i in range(2)]
        ps_ssq = P.ps("at_ps_ssq", [1, TC], F32, kps)
        ps_rsb = P.ps("at_ps_rsb", [128, TC], F32, kps)
        def _(q):
            q.wait_ge(pe.h, pe.n)
            for kc in range(KC):
                tdma_attnA(q, qsl, kc * 512, D + 0, 512, kc)
        SYNC(_)
        kvafetch = qd.n
        lev = {}
        for lc in range(LC):
            bl = lc % 2
            def _(t, lc=lc, bl=bl):
                t.wait_ge(qd.h, kvafetch)
                if lc >= 2:
                    t.wait_ge(ve.h, lev[lc - 2])
                last = None
                for kc in range(KC):
                    for hh in range(2):
                        last = t.matmul(ps_l[bl][:, hh * 512:(hh + 1) * 512],
                                        qsl[:, kc * 512 + lc * 128:kc * 512 + lc * 128 + 128],
                                        xn2[:, kc * TC + hh * 512:kc * TC + (hh + 1) * 512],
                                        start=(kc == 0), stop=(kc == KC - 1))
                pe.inc(last)
            TEN(_)
            def _(v, lc=lc, bl=bl):
                v.wait_ge(pe.h, pe.n)
                ve.inc(v.tensor_copy(lat_f[:, lc * TC:(lc + 1) * TC], ps_l[bl][:]))
                lev[lc] = ve.n
            VEC(_)
            def _(v, lc=lc):
                v.wait_ge(ve.h, lev[lc])
                ve.inc(v.tensor_tensor(scr[lc % 2][:], lat_f[:, lc * TC:(lc + 1) * TC],
                                       lat_f[:, lc * TC:(lc + 1) * TC], ALU.mult))
            VEC(_)
            def _(t, lc=lc):
                t.wait_ge(ve.h, ve.n)
                last = None
                for hh in range(2):
                    last = t.matmul(ps_ssq[0:1, hh * 512:(hh + 1) * 512], ones_col_f[:],
                                    scr[lc % 2][:, hh * 512:(hh + 1) * 512],
                                    start=(lc == 0), stop=(lc == LC - 1))
                pe.inc(last)
            TEN(_)
        def _(v):
            v.wait_ge(pe.h, pe.n)
            if _DEV_DUMP:
                ve.inc(v.tensor_copy(rows[2][:], ps_ssq[0:1, :]))
            ve.inc(v.tensor_scalar(rows[0][:], ps_ssq[0:1, :], 1.0 / LAT, 1e-6, ALU.mult, ALU.add))
        VEC(_)
        if _DEV_DUMP:
            def _(g):
                g.wait_ge(ve.h, ve.n)
                gd.inc(g.dma_start(out=dbg_chain.ap()[0:1, 0:TC], in_=rows[2][:]), 16)
                gd.inc(g.dma_start(out=dbg_chain.ap()[0:1, TC:2 * TC], in_=rows[0][:]), 16)
            GPS(_)
        def _(t):
            t.wait_ge(ve.h, ve.n)
            last = None
            for hh in range(2):
                last = t.matmul(ps_rsb[:, hh * 512:(hh + 1) * 512], ones_row_f[:],
                                rows[0][0:1, hh * 512:(hh + 1) * 512], start=True, stop=True)
            pe.inc(last)
        TEN(_)
        def _(s):
            s.wait_ge(pe.h, pe.n)
            s.wait_ge(se.h, se.n)
            se.inc(s.activation(scr[0][:], ps_rsb[:], AF.Sqrt))
        SCA(_)
        def _(v):
            v.wait_ge(se.h, se.n)
            ve.inc(v.reciprocal(rstd_t[:], scr[0][:]))
        VEC(_)
        for lc in range(LC):
            def _(v, lc=lc):
                v.wait_ge(ve.h, ve.n)
                ve.inc(v.scalar_tensor_tensor(lat_b[:, lc * TC:(lc + 1) * TC],
                                              lat_f[:, lc * TC:(lc + 1) * TC],
                                              kvn_sb[:, lc:lc + 1], rstd_t[:],
                                              ALU.mult, ALU.mult))
            VEC(_)
        if _DEV_DUMP:
            def _(g):
                g.wait_ge(ve.h, ve.n)
                gd.inc(g.dma_start(out=dbg_latf.ap(), in_=lat_f[:]), 16)
                gd.inc(g.dma_start(out=dbg_row.ap(), in_=rows[0][:]), 16)
            GPS(_)
        def _(g):
            g.wait_ge(ve.h, ve.n)
            dst = latn_my.ap().rearrange("(lc p) t -> p lc t", p=128)
            gd.inc(g.dma_start(out=dst, in_=lat_b[:, 0:LC * TC].rearrange("p (lc t) -> p lc t", t=TC)), 16)
            g.wait_ge(gd.h, gd.n)
            cc.inc(g.collective_compute(
                "AllGather", ALU.bypass,
                replica_groups=[[0, 1, 2, 3], [4, 5, 6, 7]],
                ins=[latn_my.ap().opt()], outs=[latn_all.ap().opt()]))
        GPS(_)
        kps.close()
        s1.close()

        s2 = ExitStack()
        latn_sb = P.sb("at_latn", [128, LC * SEQ], BF16, s2)
        kT = P.sb("at_kT", [128, SEQ], BF16, s2)
        vB = P.sb("at_vB", [128, SEQ], BF16, s2)
        qh = [P.sb(f"at_qh{i}", [128, TC], BF16, s2) for i in range(2)]
        pT = [P.sb(f"at_pT{i}", [128, TC], BF16, s2) for i in range(2)]
        kvbsl = [P.sb(f"at_kvb{i}", [128, LC * 256], BF16, s2) for i in range(2)]
        kvnat = [P.sb(f"at_kvn{i}", [128, 1024], BF16, s2) for i in range(2)]
        ksq = P.sb("at_ksq", [128, SEQ], BF16, s2)
        qsq = P.sb("at_qsq", [128, TC], BF16, s2)
        recip_sb = P.sb("at_recip", [128, TC], F32, s2)
        ht = P.sb("at_ht", [128, TC], BF16, s2)
        biasC = P.sb("at_biasC", [128, 1], F32, s2)
        tmpa2 = P.sb("at_tmpa2", [32, TC], F32, s2)
        tmpb2 = P.sb("at_tmpb2", [32, TC], F32, s2)
        srow = P.sb("at_srow", [1, 8], F32, s2)
        nrow = P.sb("at_nrow", [1, TC], F32, s2)

        def _(g):
            g.wait_ge(cc.h, cc.n)
            if _DEV_DUMP:
                gd.inc(g.dma_start(out=dbg_latn.ap(), in_=latn_all.ap()), 16)
                gd.inc(g.dma_start(out=dbg_bncC.ap(), in_=bncC.ap()[C_DOWN_ROWS:C_DOWN_ROWS + 16, :]), 16)
                gd.inc(g.dma_start(out=dbg_gathC.ap()[0:16, :], in_=gathC.ap()[C_DOWN_ROWS:C_DOWN_ROWS + 16, :]), 16)
                gd.inc(g.dma_start(out=dbg_gathC.ap()[16:32, :], in_=gathC.ap()[C_BLK + C_DOWN_ROWS:C_BLK + C_DOWN_ROWS + 16, :]), 16)
            for lc in range(LC):
                src = APr(latn_all, lc * 128 * TC, [[TC, 128], [LAT * TC, KB], [1, TC]])
                dst = latn_sb[:, lc * SEQ:(lc + 1) * SEQ].rearrange("p (r t) -> p r t", t=TC)
                gd.inc(g.dma_start(out=dst, in_=src), 16)
        GPS(_)

        head_end_pe = {}
        head_exp = {}
        qh_cnt = {}
        kvb_evac = {}
        for h in range(NH):
            blk = h // 2
            def _(a, h=h):
                if h >= 2:
                    a.wait_ge(pe.h, head_end_pe[h - 2])
                else:
                    a.wait_ge(ve.h, ve.n)
                    a.wait_ge(gd.h, gd.n)
                ad.inc(a.dma_start(out=qh[h % 2][:],
                                   in_=qT_dram.ap()[h * 128:(h + 1) * 128, :]), 16)
                qh_cnt[h] = ad.n
            SCA(_)
            def _(q, h=h, blk=blk):
                if h >= 2:
                    q.wait_ge(pe.h, head_end_pe[h - 2])
                else:
                    q.wait_ge(ve.h, ve.n)
                    q.wait_ge(gd.h, gd.n)
                for t2 in range(2):
                    src = APr(gathC, (blk * C_BLK + C_DOWN_ROWS) * NMLP
                              + ((h % 2) * 256 + t2 * 128) * LAT,
                              [[LAT, 128], [1, 512]])
                    q.dma_start(out=kvnat[h % 2][:, t2 * 512:(t2 + 1) * 512],
                                in_=src).then_inc(qd.h, 16)
                    qd.n += 16
            SYNC(_)
            kvb_cnt = qd.n

            sc1 = ExitStack()
            ps_kv = [P.ps(f"at_kv{h}_{i}", [128, TC], F32, sc1) for i in range(2)]
            ps_v = P.ps(f"at_v{h}", [128, 512], F32, sc1)
            ps_st = P.ps(f"at_st{h}", [1, 512], F32, sc1)
            ps_b1 = P.ps(f"at_b1{h}", [128, 1], F32, sc1)
            ps_tp = P.ps(f"at_tp{h}", [128, 512], BF16, sc1)

            for t2 in range(2):
                def _(t, h=h, t2=t2):
                    t.wait_ge(qd.h, kvb_cnt)
                    if h == 0 and t2 == 0:
                        t.wait_ge(gd.h, gd.n)
                        t.wait_ge(ad.h, ad.n)
                    if h >= 1 and t2 == 0:
                        t.wait_ge(se.h, se.n)
                        t.wait_ge(ve.h, ve.n)
                    if t2 == 1:
                        t.wait_ge(ve.h, ve.n)
                    last = None
                    for lc in range(LC):
                        last = t.transpose(ps_tp[:, lc * 128:(lc + 1) * 128],
                                           kvnat[h % 2][:, t2 * 512 + lc * 128:t2 * 512 + (lc + 1) * 128],
                                           identb[:])
                    pe.inc(last)
                TEN(_)
                def _(v, h=h, t2=t2):
                    v.wait_ge(pe.h, pe.n)
                    dst = APr(kvbsl[h % 2], t2 * 128, [[1024, 128], [256, 4], [1, 128]])
                    ve.inc(v.tensor_copy(dst, ps_tp[:].rearrange("p (lc n) -> p lc n", n=128)))
                    kvb_evac[h] = ve.n
                VEC(_)

            krope = {}
            for r in range(KB):
                br = r % 2
                def _(t, h=h, r=r, br=br):
                    t.wait_ge(ve.h, kvb_evac[h])
                    if r >= 2:
                        t.wait_ge(ve.h, krope[r - 2])
                    if h >= 1 and r < 2:
                        t.wait_ge(se.h, se.n)
                        t.wait_ge(ve.h, ve.n)
                    last = None
                    for lc in range(LC):
                        for hh in range(2):
                            last = t.matmul(ps_kv[br][:, hh * 512:(hh + 1) * 512],
                                            kvbsl[h % 2][:, lc * 256:lc * 256 + 128],
                                            latn_sb[:, lc * SEQ + r * TC + hh * 512:lc * SEQ + r * TC + (hh + 1) * 512],
                                            start=(lc == 0), stop=(lc == LC - 1))
                    pe.inc(last)
                TEN(_)
                def _(v, r=r, br=br):
                    v.wait_ge(pe.h, pe.n)
                    cols = slice(r * TC, (r + 1) * TC)
                    ve.inc(v.tensor_tensor(tmpa2[:], ps_kv[br][0:32, :], cosk[0:32, cols], ALU.mult))
                    ve.inc(v.tensor_tensor(tmpb2[:], ps_kv[br][32:64, :], sink[0:32, cols], ALU.mult))
                    v.wait_ge(ve.h, ve.n)
                    ve.inc(v.tensor_tensor(kT[0:32, cols], tmpa2[:], tmpb2[:], ALU.subtract))
                    ve.inc(v.tensor_tensor(tmpa2[:], ps_kv[br][32:64, :], cosk[32:64, cols], ALU.mult))
                    ve.inc(v.tensor_tensor(tmpb2[:], ps_kv[br][0:32, :], sink[32:64, cols], ALU.mult))
                    v.wait_ge(ve.h, ve.n)
                    ve.inc(v.tensor_tensor(kT[32:64, cols], tmpa2[:], tmpb2[:], ALU.add))
                    ve.inc(v.tensor_copy(kT[64:128, cols], ps_kv[br][64:128, :]))
                    krope[r] = ve.n
                VEC(_)

            vev = {}
            for kt4 in range(8):
                def _(t, h=h, kt4=kt4):
                    if kt4 >= 1:
                        t.wait_ge(ve.h, vev[kt4 - 1])
                    last = None
                    for j in range(4):
                        kt = kt4 * 4 + j
                        for lc in range(LC):
                            last = t.matmul(ps_v[:, j * 128:(j + 1) * 128],
                                            latn_sb[:, lc * SEQ + kt * 128:lc * SEQ + (kt + 1) * 128],
                                            kvbsl[h % 2][:, lc * 256 + 128:lc * 256 + 256],
                                            start=(lc == 0), stop=(lc == LC - 1))
                    pe.inc(last)
                TEN(_)
                def _(v, kt4=kt4):
                    v.wait_ge(pe.h, pe.n)
                    ve.inc(v.tensor_copy(vB[:, kt4 * 512:(kt4 + 1) * 512], ps_v[:]))
                    vev[kt4] = ve.n
                VEC(_)

            if _DEV_DUMP and h == 3:
                def _(g):
                    g.wait_ge(ve.h, ve.n)
                    gd.inc(g.dma_start(out=dbg_vB2.ap(), in_=vB[:]), 16)
                    gd.inc(g.dma_start(out=dbg_kvb.ap(), in_=kvbsl[1][:]), 16)
                GPS(_)
                def _(s):
                    s.wait_ge(gd.h, gd.n)
                SCA(_)
            def _(v, h=h):
                v.wait_ge(ve.h, ve.n)
                v.wait_ge(ad.h, qh_cnt[h])
                ve.inc(v.tensor_tensor(ksq[:], kT[:], kT[:], ALU.mult))
                ve.inc(v.tensor_tensor(qsq[:], qh[h % 2][:], qh[h % 2][:], ALU.mult))
            VEC(_)
            def _(v):
                v.wait_ge(ve.h, ve.n)
                ve.inc(v.memset(srow[:], -1e30))
            VEC(_)
            for i in range(10):
                def _(t, i=i):
                    t.wait_ge(ve.h, ve.n)
                    src = ksq[:, i * 512:(i + 1) * 512] if i < 8 else qsq[:, (i - 8) * 512:(i - 7) * 512]
                    pe.inc(t.matmul(ps_st[0:1, :], ones_col_b[:], src, start=True, stop=True))
                TEN(_)
                def _(v, i=i):
                    v.wait_ge(pe.h, pe.n)
                    j = 0 if i < 8 else 1
                    ve.inc(v.reduce_max(srow[0:1, 2 + i:3 + i] if False else nrow[0:1, 0:1],
                                        ps_st[0:1, :], axis=AX))
                    v.wait_ge(ve.h, ve.n)
                    ve.inc(v.tensor_tensor(srow[0:1, j:j + 1], srow[0:1, j:j + 1],
                                           nrow[0:1, 0:1], ALU.max))
                VEC(_)
            def _(v):
                v.wait_ge(ve.h, ve.n)
                ve.inc(v.tensor_scalar(srow[0:1, 2:3], srow[0:1, 0:1], -0.5, None, ALU.mult))
                v.wait_ge(ve.h, ve.n)
                ve.inc(v.scalar_tensor_tensor(srow[0:1, 3:4], srow[0:1, 1:2], -0.5,
                                              srow[0:1, 2:3], ALU.mult, ALU.add))
            VEC(_)
            def _(t):
                t.wait_ge(ve.h, ve.n)
                pe.inc(t.matmul(ps_b1[:], ones_row_f[:], srow[0:1, 3:4], start=True, stop=True))
            TEN(_)
            def _(v):
                v.wait_ge(pe.h, pe.n)
                ve.inc(v.tensor_copy(biasC[:], ps_b1[:]))
            VEC(_)
            sc1.close()

            if _DEV_DUMP and h == 3:
                def _(g):
                    g.wait_ge(ve.h, ve.n)
                    gd.inc(g.dma_start(out=dbg_kT.ap(), in_=kT[:]), 16)
                    gd.inc(g.dma_start(out=dbg_vB.ap(), in_=vB[:]), 16)
                GPS(_)
            sc2 = ExitStack()
            ps_sT = [P.ps(f"at_sT{h}_{i}", [128, TC], F32, sc2) for i in range(2)]
            ps_OT = P.ps(f"at_OT{h}", [128, TC], F32, sc2)
            ps_rs = P.ps(f"at_rs{h}", [1, TC], F32, sc2)
            sT_mm = {}
            exp_done = {}
            av_mm = {}
            for kt in range(32):
                bk = kt % 2
                def _(t, h=h, kt=kt, bk=bk):
                    if kt == 0:
                        t.wait_ge(ve.h, ve.n)
                        t.wait_ge(ad.h, qh_cnt[h])
                    if kt >= 2:
                        t.wait_ge(se.h, exp_done[kt - 2])
                    last = None
                    for hh in range(2):
                        last = t.matmul(ps_sT[bk][:, hh * 512:(hh + 1) * 512],
                                        kT[:, kt * 128:(kt + 1) * 128],
                                        qh[h % 2][:, hh * 512:(hh + 1) * 512],
                                        start=True, stop=True)
                    pe.inc(last)
                    sT_mm[kt] = pe.n
                TEN(_)
                def _(s, kt=kt, bk=bk):
                    s.wait_ge(pe.h, sT_mm[kt])
                    if kt >= 2:
                        s.wait_ge(pe.h, av_mm.get(kt - 2, 0))
                    se.inc(s.activation(pT[bk][:], ps_sT[bk][:], AF.Exp, bias=biasC[:, 0:1]))
                    exp_done[kt] = se.n
                SCA(_)
                if _DEV_DUMP and h == 3 and kt == 6:
                    def _(g, kt=kt):
                        g.wait_ge(se.h, se.n)
                        gd.inc(g.dma_start(out=dbg_pT.ap(), in_=pT[0][:]), 16)
                    GPS(_)
                if kt >= 2:
                    def _(t, kt=kt):
                        kk = kt - 2
                        bb2 = kk % 2
                        t.wait_ge(se.h, exp_done[kk])
                        for hh in range(2):
                            t.matmul(ps_OT[:, hh * 512:(hh + 1) * 512],
                                     vB[:, kk * 128:(kk + 1) * 128],
                                     pT[bb2][:, hh * 512:(hh + 1) * 512],
                                     start=(kk == 0), stop=(kk == 31))
                        last = None
                        for hh in range(2):
                            last = t.matmul(ps_rs[0:1, hh * 512:(hh + 1) * 512],
                                            ones_col_b[:],
                                            pT[bb2][:, hh * 512:(hh + 1) * 512],
                                            start=(kk == 0), stop=(kk == 31))
                        pe.inc(last)
                        av_mm[kk] = pe.n
                    TEN(_)
            for kk in (30, 31):
                def _(t, kk=kk):
                    bb2 = kk % 2
                    t.wait_ge(se.h, exp_done[kk])
                    for hh in range(2):
                        t.matmul(ps_OT[:, hh * 512:(hh + 1) * 512],
                                 vB[:, kk * 128:(kk + 1) * 128],
                                 pT[bb2][:, hh * 512:(hh + 1) * 512],
                                 start=(kk == 0), stop=(kk == 31))
                    last = None
                    for hh in range(2):
                        last = t.matmul(ps_rs[0:1, hh * 512:(hh + 1) * 512],
                                        ones_col_b[:],
                                        pT[bb2][:, hh * 512:(hh + 1) * 512],
                                        start=(kk == 0), stop=(kk == 31))
                    pe.inc(last)
                    av_mm[kk] = pe.n
                TEN(_)

            def _(v):
                v.wait_ge(pe.h, pe.n)
                ve.inc(v.reciprocal(nrow[:], ps_rs[0:1, :]))
            VEC(_)
            def _(t):
                t.wait_ge(ve.h, ve.n)
                t.wait_ge(se.h, se.n)
                last = None
                for hh in range(2):
                    last = t.matmul(ps_sT[0][:, hh * 512:(hh + 1) * 512], ones_row_f[:],
                                    nrow[0:1, hh * 512:(hh + 1) * 512], start=True, stop=True)
                pe.inc(last)
            TEN(_)
            def _(v, h=h):
                v.wait_ge(pe.h, pe.n)
                v.wait_ge(gd.h, gd.n)
                if _DEV_DUMP:
                    ve.inc(v.tensor_copy(srow[0:1, 4:5], biasC[0:1, 0:1]))
                    ve.inc(v.tensor_copy(srow[0:1, 5:6], ps_rs[0:1, 0:1]))
                    ve.inc(v.tensor_copy(srow[0:1, 6:7], ps_rs[0:1, 17:18]))
                    ve.inc(v.tensor_copy(srow[0:1, 7:8], ps_OT[0:1, 0:1]))
                ve.inc(v.tensor_copy(recip_sb[:], ps_sT[0][:]))
                v.wait_ge(ve.h, ve.n)
                ve.inc(v.tensor_tensor(ht[:], ps_OT[:], recip_sb[:], ALU.mult))
            VEC(_)
            if _DEV_DUMP:
                def _(g, h=h):
                    g.wait_ge(ve.h, ve.n)
                    gd.inc(g.dma_start(out=dbg_stats.ap()[h:h + 1, :], in_=srow[:]), 16)
                GPS(_)
            def _(g, h=h):
                g.wait_ge(ve.h, ve.n)
                gd.inc(g.dma_start(out=heads_dram.ap()[h * 128:(h + 1) * 128, :], in_=ht[:]), 16)
            GPS(_)
            sc2.close()
            head_end_pe[h] = pe.n
        s2.close()

        s3 = ExitStack()
        heads_sb = P.sb("at_heads", [128, NH * TC], BF16, s3)
        osl = [P.sb(f"at_osl{i}", [128, 16 * 512], BF16, s3, is_transpose_dest=True, align_bytes=4096) for i in range(2)]
        ps_o = [P.ps(f"at_ps_o{i}", [128, TC], F32, s3) for i in range(2)]
        def _(g):
            g.wait_ge(gd.h, gd.n)
            g.wait_ge(pe.h, pe.n)
            g.wait_ge(ve.h, ve.n)
            g.wait_ge(se.h, se.n)
            src = heads_dram.ap().rearrange("(hh p) t -> p hh t", p=128)
            gd.inc(g.dma_start(out=heads_sb[:].rearrange("p (hh t) -> p hh t", t=TC), in_=src), 16)
        GPS(_)
        ofetch = {}
        omm = {}
        oev = {}
        for og in range(4):
            def _(q, og=og):
                if og >= 2:
                    q.wait_ge(pe.h, omm[og - 2])
                else:
                    q.wait_ge(pe.h, pe.n)
                    q.wait_ge(ve.h, ve.n)
                    q.wait_ge(se.h, se.n)
                    q.wait_ge(gd.h, gd.n)
                for odc in range(KC):
                    tdma_attnA(q, osl[og % 2], odc * 512, D + LAT + og * 512, 512, odc)
                ofetch[og] = qd.n
            SYNC(_)
            for oc4 in range(4):
                oc = og * 4 + oc4
                bo = oc % 2
                def _(t, og=og, oc4=oc4, oc=oc, bo=bo):
                    t.wait_ge(qd.h, ofetch[og])
                    if oc == 0:
                        t.wait_ge(gd.h, gd.n)
                    if oc >= 2:
                        t.wait_ge(ve.h, oev[oc - 2])
                    last = None
                    for odc in range(KC):
                        for hh in range(2):
                            last = t.matmul(ps_o[bo][:, hh * 512:(hh + 1) * 512],
                                            osl[og % 2][:, odc * 512 + oc4 * 128:odc * 512 + oc4 * 128 + 128],
                                            heads_sb[:, odc * TC + hh * 512:odc * TC + (hh + 1) * 512],
                                            start=(odc == 0), stop=(odc == KC - 1))
                    pe.inc(last)
                    omm[og] = pe.n
                TEN(_)
                def _(v, oc=oc, bo=bo):
                    v.wait_ge(pe.h, pe.n)
                    ve.inc(v.tensor_tensor(xT[:, oc * TC:(oc + 1) * TC],
                                           xT[:, oc * TC:(oc + 1) * TC],
                                           ps_o[bo][:], ALU.add))
                    oev[oc] = ve.n
                VEC(_)
        s3.close()
        ph.close()

    if stop in ("ffn1", "attn", "full"):
        emit_ffn("f1")
    if stop in ("attn", "full"):
        emit_attn()
    if stop == "full":
        emit_ffn("f2")

    phE = ExitStack()
    xo_sb = P.sb("xo_sb", [128, D], F32, phE)
    ps_e = [P.ps(f"ps_e{i}", [128, 512], F32, phE) for i in range(2)]
    for tt in range(TT):
        for kg in range(4):
            bb = (tt * 4 + kg) % 2
            def _(t, tt=tt, kg=kg, bb=bb):
                if tt == 0 and kg == 0:
                    t.wait_ge(ve.h, ve.n)
                    t.wait_ge(cc.h, cc.n)
                t.wait_ge(ve.h, ve.n)
                last = None
                for j in range(4):
                    kc = kg * 4 + j
                    last = t.transpose(ps_e[bb][:, j * 128:(j + 1) * 128],
                                       xT[:, kc * TC + tt * 128:kc * TC + (tt + 1) * 128],
                                       identf[:])
                pe.inc(last)
            TEN(_)
            def _(v, tt=tt, kg=kg, bb=bb):
                v.wait_ge(pe.h, pe.n)
                if kg == 0:
                    v.wait_ge(ad.h, ad.n)
                    v.wait_ge(gd.h, gd.n)
                ve.inc(v.tensor_copy(xo_sb[:, kg * 512:(kg + 1) * 512], ps_e[bb][:]))
            VEC(_)
        def _(q, tt=tt):
            q.wait_ge(ve.h, ve.n)
            ad.inc(q.dma_start(out=y_out.ap()[tt * 128:(tt + 1) * 128, :], in_=xo_sb[:]), 16)
        SCA(_)
    def _(q):
        q.wait_ge(ad.h, ad.n)
    SCA(_)
    phE.close()

    P.st.close()
    return nc


def kernel(**inputs):
    x = np.asarray(inputs["L0_attn_hidden_states"], np.float32)
    q_w = np.asarray(inputs["L0_attn_q_w"], np.float32)
    kv_a_w = np.asarray(inputs["L0_attn_kv_a_w"], np.float32)
    kv_b_w = np.asarray(inputs["L0_attn_kv_b_w"], np.float32)
    o_w = np.asarray(inputs["L0_attn_o_w"], np.float32)
    kv_norm_w = np.asarray(inputs["L0_attn_kv_norm_w"], np.float32)
    g_pk = np.asarray(inputs["L0_ffn_gate_W_packed"], np.int32)
    g_sc = np.asarray(inputs["L0_ffn_gate_scales"], np.float32)
    u_pk = np.asarray(inputs["L0_ffn_up_W_packed"], np.int32)
    u_sc = np.asarray(inputs["L0_ffn_up_scales"], np.float32)
    d_pk = np.asarray(inputs["L0_ffn_down_W_packed"], np.int32)
    d_sc = np.asarray(inputs["L0_ffn_down_scales"], np.float32)

    Pd = _perm(D)
    Pn = _perm(NMLP)

    xf = np.ascontiguousarray(x.reshape(T, D)[:, Pd])
    q_wp = np.ascontiguousarray(q_w[:, Pd])
    kva_p = np.ascontiguousarray(kv_a_w[ROPE:, Pd])
    o_wp = np.ascontiguousarray(o_w[Pd, :])
    attn_stack = np.concatenate([q_wp, kva_p, o_wp], axis=0)
    g_pkp = np.ascontiguousarray(g_pk[Pn]); g_scp = np.ascontiguousarray(g_sc[Pn])
    u_pkp = np.ascontiguousarray(u_pk[Pn]); u_scp = np.ascontiguousarray(u_sc[Pn])
    d_pkp = np.ascontiguousarray(d_pk[Pd]); d_scp = np.ascontiguousarray(d_sc[Pd])
    kvn_dev = np.ascontiguousarray(kv_norm_w.reshape(LC, 128).T)

    cos, sin = _yarn_cos_sin()
    cosk = np.ascontiguousarray(cos.T)
    sink = np.ascontiguousarray(sin.T)
    cosq_s = (cos.T * np.float32(SCALE)).astype(np.float32)
    sinq_s = (sin.T * np.float32(SCALE)).astype(np.float32)

    import ml_dtypes
    identb = np.eye(128, dtype=ml_dtypes.bfloat16)
    identf = np.eye(128, dtype=np.float32)

    in_maps = []
    for c in range(NCORES):
        sq = (c % GRP) * TC
        in_maps.append({
            "x_in": np.ascontiguousarray(xf[c * TC:(c + 1) * TC]),
            "gpk": np.ascontiguousarray(g_pkp[c * GATE_SH:(c + 1) * GATE_SH]),
            "gsc": np.ascontiguousarray(g_scp[c * GATE_SH:(c + 1) * GATE_SH]),
            "upk": np.ascontiguousarray(u_pkp[c * GATE_SH:(c + 1) * GATE_SH]),
            "usc": np.ascontiguousarray(u_scp[c * GATE_SH:(c + 1) * GATE_SH]),
            "dpk": np.ascontiguousarray(d_pkp[c * DOWN_SH:(c + 1) * DOWN_SH]),
            "dsc": np.ascontiguousarray(d_scp[c * DOWN_SH:(c + 1) * DOWN_SH]),
            "attnA": np.ascontiguousarray(attn_stack[c * ATTN_SH:(c + 1) * ATTN_SH]),
            "kvbW": np.ascontiguousarray(kv_b_w[c * KVB_SH:(c + 1) * KVB_SH]),
            "kvnorm": kvn_dev,
            "cosq": np.ascontiguousarray(cosq_s[:, sq:sq + TC]),
            "sinq": np.ascontiguousarray(sinq_s[:, sq:sq + TC]),
            "cosk": cosk,
            "sink": sink,
            "identb": identb,
            "identf": identf,
        })

    nc = _build()
    res = run_bass_kernel_spmd(nc, in_maps, core_ids=list(range(NCORES)))

    out = np.empty((T, D), np.float32)
    for c in range(NCORES):
        out[c * TC:(c + 1) * TC] = res.results[c]["y"]
    inv = np.empty((T, D), np.float32)
    inv[:, Pd] = out
    return inv.reshape(B, S, D)


# version 04
# speedup vs baseline: 1.1051x; 1.1025x over previous
import math
import numpy as np
from contextlib import ExitStack

import concourse.bass as bass
import concourse.mybir as mybir
from concourse.bass_utils import run_bass_kernel_spmd

F32 = mybir.dt.float32
BF16 = mybir.dt.bfloat16
I32 = mybir.dt.int32
AF = mybir.ActivationFunctionType
ALU = mybir.AluOpType
AX = mybir.AxisListType.X

D = 2048
KC = D // 128
NMLP = 16384
NH = 16
HD = 128
ROPE = 64
LAT = 512
LC = LAT // 128
B, S = 2, 4096
T = B * S
NCORES = 8
TC = T // NCORES
TT = TC // 128
SCALE = float(HD) ** -0.5
GRP = NCORES // B
SEQ = S
KB = SEQ // TC

GATE_SH = NMLP // NCORES
DOWN_SH = D // NCORES
ATTN_ROWS = D + LAT + D
ATTN_SH = ATTN_ROWS // NCORES
KVB_SH = (2 * D) // NCORES
A_BLK = 2 * GATE_SH + ATTN_SH
A_ATTN_OFF = 2 * GATE_SH
C_DOWN_ROWS = DOWN_SH
C_KVB_ROWS = (KVB_SH * LAT) // NMLP
C_BLK = C_DOWN_ROWS + C_KVB_ROWS

_DEV_STOP = "full"
_DEV_DUMP = False


def _yarn_cos_sin():
    ROPE_DIM = 64
    MAX_POS = 131072
    ORIG_POS = 8192
    BASE = 500000.0
    BETA_FAST, BETA_SLOW = (32.0, 1.0)
    scaling = MAX_POS / ORIG_POS
    pos_freqs = BASE ** (np.arange(0, ROPE_DIM, 2, dtype=np.float64) / ROPE_DIM)
    extrap = 1.0 / pos_freqs
    interp = 1.0 / (scaling * pos_freqs)
    low = max(math.floor(ROPE_DIM * math.log(ORIG_POS / (BETA_FAST * 2 * math.pi)) / (2 * math.log(BASE))), 0)
    high = min(math.ceil(ROPE_DIM * math.log(ORIG_POS / (BETA_SLOW * 2 * math.pi)) / (2 * math.log(BASE))), ROPE_DIM - 1)
    i = np.arange(ROPE_DIM // 2, dtype=np.float64)
    smooth = np.clip((i - low) / max(high - low, 1), 0.0, 1.0)
    inv_freq = np.asarray((1.0 - smooth) * interp + smooth * extrap, dtype=np.float32)
    pos = np.arange(S, dtype=np.float32)
    freqs = pos[:, None] * inv_freq[None, :]
    emb = np.concatenate([freqs, freqs], axis=-1)
    mscale = np.float32(0.1 * math.log(scaling) + 1.0)
    cos = (np.cos(emb) * mscale).astype(np.float32)
    sin = (np.sin(emb) * mscale).astype(np.float32)
    return cos, sin


def _perm(n):
    return np.concatenate([np.arange(0, n, 2), np.arange(1, n, 2)])


class Sem:
    def __init__(self, st, nc, name):
        self.h = st.enter_context(nc.semaphore(name))
        self.n = 0

    def inc(self, instr, k=1):
        instr.then_inc(self.h, k)
        self.n += k
        return self.n


def APr(handle, offset, dims):
    return bass.AP(handle, int(offset), [[int(s), int(n)] for s, n in dims])


class Prog:

    def __init__(self):
        self.nc = bass.Bass("TRN2", target_bir_lowering=False, debug=False)
        self.st = ExitStack()

    def sem(self, name):
        return Sem(self.st, self.nc, name)

    def sb(self, name, shape, dtype, stack=None, **kw):
        return (stack or self.st).enter_context(self.nc.sbuf_tensor(name, shape, dtype, **kw))

    def ps(self, name, shape, dtype=F32, stack=None):
        return (stack or self.st).enter_context(self.nc.psum_tensor(name, shape, dtype))


def _build(stop=None):
    stop = stop or _DEV_STOP
    P = Prog()
    nc = P.nc

    x_in = nc.dram_tensor("x_in", [TC, D], F32, kind="ExternalInput")
    gpk = nc.dram_tensor("gpk", [GATE_SH, D // 2], I32, kind="ExternalInput")
    gsc = nc.dram_tensor("gsc", [GATE_SH, D // 32], F32, kind="ExternalInput")
    upk = nc.dram_tensor("upk", [GATE_SH, D // 2], I32, kind="ExternalInput")
    usc = nc.dram_tensor("usc", [GATE_SH, D // 32], F32, kind="ExternalInput")
    dpk = nc.dram_tensor("dpk", [DOWN_SH, NMLP // 2], I32, kind="ExternalInput")
    dsc = nc.dram_tensor("dsc", [DOWN_SH, NMLP // 32], F32, kind="ExternalInput")
    attnA_in = nc.dram_tensor("attnA", [ATTN_SH, D], F32, kind="ExternalInput")
    kvbW_in = nc.dram_tensor("kvbW", [KVB_SH, LAT], F32, kind="ExternalInput")
    kvnorm_in = nc.dram_tensor("kvnorm", [128, LC], F32, kind="ExternalInput")
    cosq_in = nc.dram_tensor("cosq", [ROPE, TC], F32, kind="ExternalInput")
    sinq_in = nc.dram_tensor("sinq", [ROPE, TC], F32, kind="ExternalInput")
    cosk_in = nc.dram_tensor("cosk", [ROPE, SEQ], F32, kind="ExternalInput")
    sink_in = nc.dram_tensor("sink", [ROPE, SEQ], F32, kind="ExternalInput")
    identb_in = nc.dram_tensor("identb", [128, 128], BF16, kind="ExternalInput")
    identf_in = nc.dram_tensor("identf", [128, 128], F32, kind="ExternalInput")
    y_out = nc.dram_tensor("y", [TC, D], F32, kind="ExternalOutput")

    bncA = nc.dram_tensor("bncA", [A_BLK, D], BF16)
    gathA = nc.dram_tensor("gathA", [NCORES * A_BLK, D], BF16, addr_space="Shared")
    bncC = nc.dram_tensor("bncC", [C_BLK, NMLP], BF16)
    gathC = nc.dram_tensor("gathC", [NCORES * C_BLK, NMLP], BF16, addr_space="Shared")
    dbg = "ExternalOutput" if _DEV_DUMP else "Internal"
    latn_my = nc.dram_tensor("latn_my", [LAT, TC], BF16)
    latn_all = nc.dram_tensor("latn_all", [GRP * LAT, TC], BF16)
    dbg_latn = nc.dram_tensor("dbg_latn", [GRP * LAT, TC], BF16, kind="ExternalOutput") if _DEV_DUMP else None
    qT_dram = nc.dram_tensor("qT_dram", [NH * 128, TC], BF16, kind=dbg)
    heads_dram = nc.dram_tensor("heads_dram", [NH * 128, TC], BF16, kind=dbg)
    dbg_stats = nc.dram_tensor("dbg_stats", [NH, 8], F32, kind=dbg) if _DEV_DUMP else None
    dbg_latf = nc.dram_tensor("dbg_latf", [128, LC * TC], F32, kind=dbg) if _DEV_DUMP else None
    dbg_row = nc.dram_tensor("dbg_row", [1, TC], F32, kind=dbg) if _DEV_DUMP else None
    dbg_chain = nc.dram_tensor("dbg_chain", [1, 4 * TC], F32, kind=dbg) if _DEV_DUMP else None
    dbg_kT = nc.dram_tensor("dbg_kT", [128, SEQ], BF16, kind=dbg) if _DEV_DUMP else None
    dbg_vB = nc.dram_tensor("dbg_vB", [128, SEQ], BF16, kind=dbg) if _DEV_DUMP else None
    dbg_vB2 = nc.dram_tensor("dbg_vB2", [128, SEQ], BF16, kind=dbg) if _DEV_DUMP else None
    dbg_kvb = nc.dram_tensor("dbg_kvb", [128, LC * 256], BF16, kind=dbg) if _DEV_DUMP else None
    dbg_bncC = nc.dram_tensor("dbg_bncC", [16, NMLP], BF16, kind=dbg) if _DEV_DUMP else None
    dbg_gathC = nc.dram_tensor("dbg_gathC", [32, NMLP], BF16, kind=dbg) if _DEV_DUMP else None
    dbg_xtok = nc.dram_tensor("dbg_xtok", [128, 512], F32, kind=dbg) if _DEV_DUMP else None
    dbg_deq = nc.dram_tensor("dbg_deq", [128, 512], BF16, kind=dbg) if _DEV_DUMP else None
    dbg_bnc2 = nc.dram_tensor("dbg_bnc2", [4, NMLP], BF16, kind=dbg) if _DEV_DUMP else None
    dbg_pT = nc.dram_tensor("dbg_pT", [128, TC], BF16, kind=dbg) if _DEV_DUMP else None

    qd = P.sem("qdma")
    ad = P.sem("adma")
    gd = P.sem("gdma")
    pe = P.sem("pe")
    ve = P.sem("ve")
    se = P.sem("se")
    po = P.sem("po")
    cc = P.sem("cc")

    xT = P.sb("xT", [128, KC * TC], F32)
    identb = P.sb("identb_sb", [128, 128], BF16)
    identf = P.sb("identf_sb", [128, 128], F32)
    ones_col_f = P.sb("ones_col_f", [128, 1], F32)
    ones_row_f = P.sb("ones_row_f", [1, 128], F32)
    ones_col_b = P.sb("ones_col_b", [128, 1], BF16)
    kvn_sb = P.sb("kvn_sb", [128, LC], F32)

    block = P.st.enter_context(nc.Block())

    def SYNC(f):
        block.sync(f)

    def TEN(f):
        block.tensor(f)

    def VEC(f):
        block.vector(f)

    def SCA(f):
        block.scalar(f)

    def GPS(f):
        block.gpsimd(f)

    phA = ExitStack()
    x_tok = [P.sb(f"x_tok{i}", [128, D], F32, phA) for i in range(2)]
    ps_tr = [P.ps(f"ps_tr{i}", [128, 512], F32, phA) for i in range(2)]

    def _(q):
        q.dma_start(out=identb[:], in_=identb_in.ap()).then_inc(qd.h, 16)
        q.dma_start(out=identf[:], in_=identf_in.ap()).then_inc(qd.h, 16)
        q.dma_start(out=kvn_sb[:], in_=kvnorm_in.ap()).then_inc(qd.h, 16)
        qd.n += 48
    SYNC(_)

    def _(g):
        g.memset(ones_col_f[:], 1.0)
        g.memset(ones_row_f[:], 1.0)
        po.inc(g.memset(ones_col_b[:], 1.0), 3)
    GPS(_)

    xtok_ready = [None] * TT
    xtok_done = [None] * TT
    evac_cnt = {}

    def _emit_xdma(i):
        def _(q, i=i):
            if i >= 2:
                q.wait_ge(pe.h, xtok_done[i - 2])
            qd.inc(q.dma_start(out=x_tok[i % 2][:], in_=x_in.ap()[i * 128:(i + 1) * 128, :]), 16)
            xtok_ready[i] = qd.n
        SYNC(_)

    _emit_xdma(0)
    _emit_xdma(1)
    for i in range(TT):
        for kg in range(4):
            bb = (i * 4 + kg) % 2
            def _(t, i=i, kg=kg, bb=bb):
                if kg == 0:
                    t.wait_ge(qd.h, xtok_ready[i])
                    t.wait_ge(po.h, po.n)
                key = i * 4 + kg
                if key >= 2:
                    t.wait_ge(ve.h, evac_cnt[key - 2])
                last = None
                for j in range(4):
                    kc = kg * 4 + j
                    last = t.transpose(ps_tr[bb][:, j * 128:(j + 1) * 128],
                                       x_tok[i % 2][:, kc * 128:(kc + 1) * 128], identf[:])
                pe.inc(last)
            TEN(_)
            def _(v, i=i, kg=kg, bb=bb):
                v.wait_ge(pe.h, pe.n)
                src = ps_tr[bb][:].rearrange("p (j c) -> p j c", c=128)
                dst = APr(xT, (kg * 4) * TC + i * 128, [[KC * TC, 128], [TC, 4], [1, 128]])
                ve.inc(v.tensor_copy(dst, src))
                evac_cnt[i * 4 + kg] = ve.n
            VEC(_)
        xtok_done[i] = pe.n
        if i + 2 < TT:
            _emit_xdma(i + 2)

    dq = ExitStack()
    w_sb = P.sb("w_sb", [128, 2048], I32, dq)
    sc_sb = P.sb("sc_sb", [128, 128], F32, dq)
    bc_ = P.sb("dq_c", [128, 2048], I32, dq)
    bc8 = P.sb("dq_c8", [128, 2048], I32, dq)
    bs31 = P.sb("dq_s31", [128, 2048], I32, dq)
    bt1 = P.sb("dq_t1", [128, 2048], I32, dq)
    bu = P.sb("dq_u", [128, 2048], I32, dq)
    deq_sb = P.sb("deq_sb", [128, 4096], BF16, dq)
    wT_sb = P.sb("wT_sb", [128, 4096], BF16, dq)
    ps_wq = [P.ps(f"ps_wq{i}", [128, 512], BF16, dq) for i in range(2)]

    def emit_dq_unit(wait_dma_cnt, Fk, sc_cols, out_planes, fold05):
        NB = Fk // 16
        for plane in range(2):
            def _(v, plane=plane):
                c = bc_[:, 0:Fk]
                c8 = bc8[:, 0:Fk]
                s31 = bs31[:, 0:Fk]
                t1 = bt1[:, 0:Fk]
                u = bu[:, 0:Fk]
                v.wait_ge(qd.h, wait_dma_cnt)
                v.wait_ge(ve.h, ve.n)
                if plane == 0:
                    ve.inc(v.tensor_scalar(c, w_sb[:, 0:Fk], 15, None, ALU.bitwise_and))
                else:
                    ve.inc(v.tensor_scalar(c, w_sb[:, 0:Fk], 4, None, ALU.logical_shift_right))
                v.wait_ge(ve.h, ve.n)
                ve.inc(v.tensor_scalar(c8, c, 7, None, ALU.bitwise_and))
                v.wait_ge(ve.h, ve.n)
                ve.inc(v.tensor_scalar(s31, c, 3, 31, ALU.logical_shift_right, ALU.logical_shift_left))
                v.wait_ge(ve.h, ve.n)
                ve.inc(v.tensor_scalar(t1, c8, -251, 252, ALU.mult, ALU.add))
                v.wait_ge(ve.h, ve.n)
                ve.inc(v.tensor_scalar(t1, t1, 0, None, ALU.max))
                v.wait_ge(ve.h, ve.n)
                ve.inc(v.tensor_scalar(u, c8, 1 << 22, 0x3F000000, ALU.mult, ALU.add))
                v.wait_ge(ve.h, ve.n)
                ve.inc(v.scalar_tensor_tensor(c8, t1, -(1 << 22), u, ALU.mult, ALU.add))
                v.wait_ge(ve.h, ve.n)
                ve.inc(v.tensor_tensor(c8, c8, s31, ALU.bitwise_or))
                v.wait_ge(ve.h, ve.n)
                bits_f = c8.bitcast(F32).rearrange("p (b j) -> p b j", j=16)
                sc_view = sc_sb[:, 0:NB, None].broadcast_to([128, NB, 16])
                out = out_planes[plane]
                if fold05:
                    ve.inc(v.scalar_tensor_tensor(out, bits_f, 0.5, sc_view, ALU.mult, ALU.mult))
                else:
                    ve.inc(v.tensor_tensor(out, bits_f, sc_view, ALU.mult))
            VEC(_)

    def emit_shard_transpose_and_store(n_pieces, dst_fn):
        for grp4 in range((n_pieces + 3) // 4):
            bb = grp4 % 2
            def _(t, grp4=grp4, bb=bb):
                t.wait_ge(ve.h, ve.n)
                last = None
                for j in range(4):
                    pc = grp4 * 4 + j
                    last = t.transpose(ps_wq[bb][:, j * 128:(j + 1) * 128],
                                       deq_sb[:, pc * 128:(pc + 1) * 128], identb[:])
                pe.inc(last)
            TEN(_)
            def _(v, grp4=grp4, bb=bb):
                v.wait_ge(pe.h, pe.n)
                v.wait_ge(gd.h, gd.n)
                ve.inc(v.tensor_copy(wT_sb[:, grp4 * 512:(grp4 + 1) * 512], ps_wq[bb][:]))
            VEC(_)
        for gi in range(n_pieces // 16):
            def _(g, gi=gi):
                g.wait_ge(ve.h, ve.n)
                dst, src = dst_fn(gi)
                gd.inc(g.dma_start(out=dst, in_=src), 16)
            GPS(_)

    att_tiles = [(0, 128), (128, 128), (256, 128), (384, 128), (512, 64)]
    for (r0, nr) in att_tiles:
        def _(q, r0=r0, nr=nr):
            q.wait_ge(ve.h, ve.n)
            qd.inc(q.dma_start(out=x_tok[0][0:nr, :], in_=attnA_in.ap()[r0:r0 + nr, :]), 16)
        SYNC(_)
        def _(v, r0=r0, nr=nr):
            v.wait_ge(qd.h, qd.n)
            v.wait_ge(gd.h, gd.n)
            ve.inc(v.tensor_copy(deq_sb[0:nr, 0:2048], x_tok[0][0:nr, :]))
        VEC(_)
        def _(g, r0=r0, nr=nr):
            g.wait_ge(ve.h, ve.n)
            gd.inc(g.dma_start(out=bncA.ap()[A_ATTN_OFF + r0:A_ATTN_OFF + r0 + nr, :],
                               in_=deq_sb[0:nr, 0:2048]), 16)
        GPS(_)
    for r in range(KVB_SH // 128):
        def _(q, r=r):
            q.wait_ge(ve.h, ve.n)
            qd.inc(q.dma_start(out=x_tok[0][:, 0:512], in_=kvbW_in.ap()[r * 128:(r + 1) * 128, :]), 16)
        SYNC(_)
        def _(v, r=r):
            v.wait_ge(qd.h, qd.n)
            v.wait_ge(gd.h, gd.n)
            ve.inc(v.tensor_copy(deq_sb[:, 0:512], x_tok[0][:, 0:512]))
        VEC(_)
        def _(g, r=r):
            g.wait_ge(ve.h, ve.n)
            gd.inc(g.dma_start(out=APr(bncC, C_DOWN_ROWS * NMLP + r * 128 * 512,
                                       [[512, 128], [1, 512]]),
                               in_=deq_sb[:, 0:512]), 16)
            if _DEV_DUMP and r == 3:
                gd.inc(g.dma_start(out=dbg_xtok.ap(), in_=x_tok[0][:, 0:512]), 16)
                gd.inc(g.dma_start(out=dbg_deq.ap(), in_=deq_sb[:, 0:512]), 16)
                g.wait_ge(gd.h, gd.n)
                gd.inc(g.dma_start(out=dbg_bnc2.ap(),
                                   in_=bncC.ap()[C_DOWN_ROWS + 12:C_DOWN_ROWS + 16, :]), 16)
        GPS(_)

    for hb in range(DOWN_SH // 128):
        for kq in range(4):
            def _(q, hb=hb, kq=kq):
                q.wait_ge(ve.h, ve.n)
                q.dma_start(out=w_sb[:, 0:2048],
                            in_=dpk.ap()[hb * 128:(hb + 1) * 128, kq * 2048:(kq + 1) * 2048]).then_inc(qd.h, 16)
                q.dma_start(out=sc_sb[:, 0:128],
                            in_=dsc.ap()[hb * 128:(hb + 1) * 128, kq * 128:(kq + 1) * 128]).then_inc(qd.h, 16)
                qd.n += 32
            SYNC(_)
            lo = deq_sb[:, 0:2048].rearrange("p (b j) -> p b j", j=16)
            hi = deq_sb[:, 2048:4096].rearrange("p (b j) -> p b j", j=16)
            def _(v):
                v.wait_ge(pe.h, pe.n)
            VEC(_)
            emit_dq_unit(qd.n, 2048, 128, (lo, hi), True)

            def dst_fn(gi, hb=hb, kq=kq):
                n0 = kq * 2048 + (0 if gi == 0 else NMLP // 2)
                dst = APr(bncC, n0 * DOWN_SH + hb * 128,
                          [[DOWN_SH, 128], [128 * DOWN_SH, 16], [1, 128]])
                src = wT_sb[:, gi * 2048:(gi + 1) * 2048].rearrange("p (kc n) -> p kc n", n=128)
                return dst, src
            emit_shard_transpose_and_store(32, dst_fn)

    def _(g):
        g.wait_ge(gd.h, gd.n)
        g.wait_ge(ve.h, ve.n)
        cc.inc(g.collective_compute(
            "AllGather", ALU.bypass,
            replica_groups=[list(range(NCORES))],
            ins=[bncC.ap().opt()], outs=[gathC.ap().opt()]))
    GPS(_)

    for seg, (pk_t, sc_t, blob_row0) in enumerate([(gpk, gsc, 0), (upk, usc, GATE_SH)]):
        for r in range(GATE_SH // 128):
            def _(q, pk_t=pk_t, sc_t=sc_t, r=r):
                q.wait_ge(ve.h, ve.n)
                q.dma_start(out=w_sb[:, 0:1024], in_=pk_t.ap()[r * 128:(r + 1) * 128, :]).then_inc(qd.h, 16)
                q.dma_start(out=sc_sb[:, 0:64], in_=sc_t.ap()[r * 128:(r + 1) * 128, :]).then_inc(qd.h, 16)
                qd.n += 32
            SYNC(_)
            lo = deq_sb[:, 0:1024].rearrange("p (b j) -> p b j", j=16)
            hi = deq_sb[:, 1024:2048].rearrange("p (b j) -> p b j", j=16)
            def _(v):
                v.wait_ge(pe.h, pe.n)
            VEC(_)
            emit_dq_unit(qd.n, 1024, 64, (lo, hi), False)

            def dst_fn(gi, r=r, blob_row0=blob_row0):
                dst = APr(bncA, blob_row0 * D + r * 128,
                          [[D, 128], [128 * D, 16], [1, 128]])
                src = wT_sb[:, 0:2048].rearrange("p (kc n) -> p kc n", n=128)
                return dst, src
            emit_shard_transpose_and_store(16, dst_fn)

    def _(g):
        g.wait_ge(gd.h, gd.n)
        g.wait_ge(ve.h, ve.n)
        cc.inc(g.collective_compute(
            "AllGather", ALU.bypass,
            replica_groups=[list(range(NCORES))],
            ins=[bncA.ap().opt()], outs=[gathA.ap().opt()]))
    GPS(_)

    dq.close()
    phA.close()

    def emit_ln(dst, scr, rstd_t, ps_st):
        ps_sum, ps_sq, ps_mb, ps_rb, rows = ps_st
        for kc in range(KC):
            def _(v, kc=kc):
                if kc == 0:
                    v.wait_ge(ve.h, ve.n)
                    v.wait_ge(pe.h, pe.n)
                    v.wait_ge(gd.h, gd.n)
                    v.wait_ge(ad.h, ad.n)
                else:
                    v.wait_ge(pe.h, pe.n)
                ve.inc(v.tensor_tensor(scr[kc % 2][:], xT[:, kc * TC:kc * TC + TC],
                                       xT[:, kc * TC:kc * TC + TC], ALU.mult))
            VEC(_)
            def _(t, kc=kc):
                t.wait_ge(ve.h, ve.n)
                for h in range(2):
                    t.matmul(ps_sum[0:1, h * 512:(h + 1) * 512], ones_col_f[:],
                             xT[:, kc * TC + h * 512:kc * TC + (h + 1) * 512],
                             start=(kc == 0), stop=(kc == KC - 1))
                last = None
                for h in range(2):
                    last = t.matmul(ps_sq[0:1, h * 512:(h + 1) * 512], ones_col_f[:],
                                    scr[kc % 2][:, h * 512:(h + 1) * 512],
                                    start=(kc == 0), stop=(kc == KC - 1))
                pe.inc(last)
            TEN(_)
        def _(v):
            v.wait_ge(pe.h, pe.n)
            ve.inc(v.tensor_scalar(rows[0][:], ps_sum[0:1, :], 1.0 / D, None, ALU.mult))
            v.wait_ge(ve.h, ve.n)
            ve.inc(v.tensor_scalar(rows[1][:], ps_sq[0:1, :], 1.0 / D, None, ALU.mult))
            v.wait_ge(ve.h, ve.n)
            ve.inc(v.tensor_tensor(rows[2][:], rows[0][:], rows[0][:], ALU.mult))
            v.wait_ge(ve.h, ve.n)
            ve.inc(v.scalar_tensor_tensor(rows[1][:], rows[1][:], 1e-5, rows[2][:], ALU.add, ALU.subtract))
        VEC(_)
        def _(t):
            t.wait_ge(ve.h, ve.n)
            for h in range(2):
                t.matmul(ps_mb[:, h * 512:(h + 1) * 512], ones_row_f[:],
                         rows[0][0:1, h * 512:(h + 1) * 512], start=True, stop=True)
            last = None
            for h in range(2):
                last = t.matmul(ps_rb[:, h * 512:(h + 1) * 512], ones_row_f[:],
                                rows[1][0:1, h * 512:(h + 1) * 512], start=True, stop=True)
            pe.inc(last)
        TEN(_)
        def _(s):
            s.wait_ge(pe.h, pe.n)
            s.wait_ge(ve.h, ve.n)
            se.inc(s.activation(scr[0][:], ps_rb[:], AF.Sqrt))
        SCA(_)
        def _(v):
            v.wait_ge(se.h, se.n)
            ve.inc(v.reciprocal(rstd_t[:], scr[0][:]))
        VEC(_)
        for kc in range(KC):
            def _(v, kc=kc):
                v.wait_ge(pe.h, pe.n)
                v.wait_ge(ve.h, ve.n)
                ve.inc(v.tensor_tensor(scr[kc % 2][:], xT[:, kc * TC:kc * TC + TC], ps_mb[:], ALU.subtract))
                v.wait_ge(ve.h, ve.n)
                ve.inc(v.tensor_tensor(dst[:, kc * TC:kc * TC + TC], scr[kc % 2][:], rstd_t[:], ALU.mult))
            VEC(_)

    def emit_ffn(tag):
        ph = ExitStack()
        xnT = P.sb(f"{tag}_xnT", [128, KC * TC], BF16, ph)
        interT = P.sb(f"{tag}_inter", [128, 16 * TC], BF16, ph)
        scr = [P.sb(f"{tag}_scr{i}", [128, TC], F32, ph) for i in range(2)]
        rows = [P.sb(f"{tag}_row{i}", [1, TC], F32, ph) for i in range(3)]
        wg_sl = [P.sb(f"{tag}_wg{i}", [128, 2048], BF16, ph) for i in range(2)]
        wu_sl = [P.sb(f"{tag}_wu{i}", [128, 2048], BF16, ph) for i in range(2)]
        wd_sl = [P.sb(f"{tag}_wd{i}", [128, 8192], BF16, ph) for i in range(2)]
        rstd_t = P.sb(f"{tag}_rstd", [128, TC], F32, ph)

        lnps = ExitStack()
        ps_st = (P.ps(f"{tag}_ln_sum", [1, TC], F32, lnps),
                 P.ps(f"{tag}_ln_sq", [1, TC], F32, lnps),
                 P.ps(f"{tag}_ln_mb", [128, TC], F32, lnps),
                 P.ps(f"{tag}_ln_rb", [128, TC], F32, lnps),
                 rows)
        emit_ln(xnT, scr, rstd_t, ps_st)
        lnps.close()

        ps_g = P.ps(f"{tag}_ps_g", [128, TC], F32, ph)
        ps_u = [P.ps(f"{tag}_ps_u{i}", [128, TC], F32, ph) for i in range(2)]
        ps_d = [P.ps(f"{tag}_ps_d{i}", [128, 512], F32, ph) for i in range(2)]

        ln_done = ve.n
        g_mm = {}
        gu_mm = {}
        gelu_done = {}
        mult_done = {}
        fetch_cnt = {}
        fetchd_cnt = {}
        d_read = {}
        slabd_mm = {}

        for g in range(8):
            for nci in range(16):
                nabs = g * 16 + nci
                bw = nabs % 2

                def _(q, g=g, nci=nci, nabs=nabs, bw=bw):
                    if nabs >= 2:
                        q.wait_ge(pe.h, gu_mm[nabs - 2])
                    if nabs == 0:
                        q.wait_ge(cc.h, cc.n)
                        q.wait_ge(gd.h, gd.n)
                        q.wait_ge(ad.h, ad.n)
                    if nabs < 2:
                        q.wait_ge(ve.h, ln_done)
                    src_g = APr(gathA, (g * A_BLK) * D + nci * 128,
                                [[D, 128], [128 * D, 16], [1, 128]])
                    src_u = APr(gathA, (g * A_BLK + GATE_SH) * D + nci * 128,
                                [[D, 128], [128 * D, 16], [1, 128]])
                    q.dma_start(out=wg_sl[bw][:].rearrange("p (kc n) -> p kc n", n=128),
                                in_=src_g).then_inc(qd.h, 16)
                    q.dma_start(out=wu_sl[bw][:].rearrange("p (kc n) -> p kc n", n=128),
                                in_=src_u).then_inc(qd.h, 16)
                    qd.n += 32
                    fetch_cnt[nabs] = qd.n
                SYNC(_)

                def _(t, nabs=nabs, bw=bw):
                    t.wait_ge(qd.h, fetch_cnt[nabs])
                    if nabs == 0:
                        t.wait_ge(ve.h, ln_done)
                    if nabs >= 1:
                        t.wait_ge(se.h, gelu_done[nabs - 1])
                    if nabs >= 2:
                        t.wait_ge(ve.h, mult_done[nabs - 2])
                    last = None
                    for kc in range(KC):
                        for h in range(2):
                            last = t.matmul(ps_g[:, h * 512:(h + 1) * 512],
                                            wg_sl[bw][:, kc * 128:(kc + 1) * 128],
                                            xnT[:, kc * TC + h * 512:kc * TC + (h + 1) * 512],
                                            start=(kc == 0), stop=(kc == KC - 1))
                    pe.inc(last)
                    g_mm[nabs] = pe.n
                    for kc in range(KC):
                        for h in range(2):
                            last = t.matmul(ps_u[bw][:, h * 512:(h + 1) * 512],
                                            wu_sl[bw][:, kc * 128:(kc + 1) * 128],
                                            xnT[:, kc * TC + h * 512:kc * TC + (h + 1) * 512],
                                            start=(kc == 0), stop=(kc == KC - 1))
                    pe.inc(last)
                    gu_mm[nabs] = pe.n
                TEN(_)

                def _(s, nabs=nabs, bw=bw):
                    s.wait_ge(pe.h, g_mm[nabs])
                    if nabs >= 2:
                        s.wait_ge(ve.h, mult_done[nabs - 2])
                    else:
                        s.wait_ge(ve.h, ln_done)
                    se.inc(s.activation(scr[bw][:], ps_g[:], AF.Gelu_apprx_tanh))
                    gelu_done[nabs] = se.n
                SCA(_)

                def _(v, nabs=nabs, bw=bw, nci=nci):
                    v.wait_ge(se.h, gelu_done[nabs])
                    v.wait_ge(pe.h, gu_mm[nabs])
                    ve.inc(v.tensor_tensor(interT[:, nci * TC:(nci + 1) * TC],
                                           scr[bw][:], ps_u[bw][:], ALU.mult))
                    mult_done[nabs] = ve.n
                VEC(_)

            for hq in range(4):
                dseq = g * 4 + hq
                bws = dseq % 2

                def _(q, g=g, hq=hq, dseq=dseq, bws=bws):
                    if dseq >= 2:
                        q.wait_ge(pe.h, slabd_mm[dseq - 2])
                    if dseq < 2:
                        q.wait_ge(ve.h, ln_done)
                    for r in range(2):
                        src = APr(gathC,
                                  (hq * 2 + r) * C_BLK * NMLP + (g * 2048) * DOWN_SH,
                                  [[DOWN_SH, 128], [128 * DOWN_SH, 16], [1, 256]])
                        dst = APr(wd_sl[bws], r * 256, [[8192, 128], [512, 16], [1, 256]])
                        q.dma_start(out=dst, in_=src).then_inc(qd.h, 16)
                        qd.n += 16
                    fetchd_cnt[dseq] = qd.n
                SYNC(_)

                for hc in range(4):
                    habs = hq * 4 + hc
                    for th in range(2):
                        sseq = (dseq * 8) + hc * 2 + th
                        bd = sseq % 2

                        def _(t, g=g, hq=hq, hc=hc, th=th, habs=habs, dseq=dseq, bws=bws, bd=bd, sseq=sseq):
                            t.wait_ge(qd.h, fetchd_cnt[dseq])
                            if hc == 0 and th == 0:
                                t.wait_ge(ve.h, mult_done[g * 16 + 15])
                            if sseq >= 2 and (sseq - 2) in d_read:
                                t.wait_ge(ve.h, d_read[sseq - 2])
                            last = None
                            for nci in range(16):
                                last = t.matmul(ps_d[bd][:],
                                                wd_sl[bws][:, nci * 512 + hc * 128:nci * 512 + (hc + 1) * 128],
                                                interT[:, nci * TC + th * 512:nci * TC + (th + 1) * 512],
                                                start=(nci == 0), stop=(nci == 15))
                            pe.inc(last)
                            if hc == 3 and th == 1:
                                slabd_mm[dseq] = pe.n
                        TEN(_)

                        def _(v, habs=habs, th=th, bd=bd, sseq=sseq):
                            v.wait_ge(pe.h, pe.n)
                            ve.inc(v.tensor_tensor(
                                xT[:, habs * TC + th * 512:habs * TC + (th + 1) * 512],
                                xT[:, habs * TC + th * 512:habs * TC + (th + 1) * 512],
                                ps_d[bd][:], ALU.add))
                            d_read[sseq] = ve.n
                        VEC(_)

        ph.close()

    def tdma_attnA(q, dst_tile, dst_col0, row0, nrows, kc):
        done = 0
        n = 0
        while done < nrows:
            r = row0 + done
            blk = r // ATTN_SH
            within = r % ATTN_SH
            glen = min(ATTN_SH - within, nrows - done)
            src = APr(gathA, (blk * A_BLK + A_ATTN_OFF + within) * D + kc * 128,
                      [[D, glen], [1, 128]])
            q.dma_start(out=dst_tile[:, dst_col0 + done:dst_col0 + done + glen],
                        in_=src, transpose=True).then_inc(qd.h, 16)
            qd.n += 16
            done += glen
            n += 1
        return n

    def emit_attn():
        ph = ExitStack()
        cosk = P.sb("at_cosk", [ROPE, SEQ], F32, ph)
        sink = P.sb("at_sink", [ROPE, SEQ], F32, ph)

        s1 = ExitStack()
        cosq = P.sb("at_cosq", [ROPE, TC], F32, s1)
        sinq = P.sb("at_sinq", [ROPE, TC], F32, s1)
        xn2 = P.sb("at_xn", [128, KC * TC], BF16, s1)
        scr = [P.sb(f"at_scr{i}", [128, TC], F32, s1) for i in range(2)]
        rows = [P.sb(f"at_row{i}", [1, TC], F32, s1) for i in range(3)]
        qsl = P.sb("at_qsl", [128, 16 * 512], BF16, s1, is_transpose_dest=True, align_bytes=4096)
        rstd_t = P.sb("at_rstd", [128, TC], F32, s1)
        qt_tile = P.sb("at_qt", [128, TC], BF16, s1)
        tmpa = P.sb("at_tmpa", [32, TC], F32, s1)
        tmpb = P.sb("at_tmpb", [32, TC], F32, s1)
        lat_f = P.sb("at_latf", [128, LC * TC], F32, s1)
        lat_b = qsl

        def _(q):
            q.wait_ge(pe.h, pe.n)
            q.wait_ge(ve.h, ve.n)
            q.wait_ge(se.h, se.n)
            q.wait_ge(gd.h, gd.n)
            q.wait_ge(ad.h, ad.n)
            for dst, srct in ((cosk, cosk_in), (sink, sink_in), (cosq, cosq_in), (sinq, sinq_in)):
                q.dma_start(out=dst[:], in_=srct.ap()).then_inc(qd.h, 16)
                qd.n += 16
        SYNC(_)
        def _(a):
            a.wait_ge(pe.h, pe.n)
            a.wait_ge(ve.h, ve.n)
            a.wait_ge(gd.h, gd.n)
        SCA(_)

        lnps = ExitStack()
        ps_st_ln = (P.ps("at_ln_sum", [1, TC], F32, lnps),
                    P.ps("at_ln_sq", [1, TC], F32, lnps),
                    P.ps("at_ln_mb", [128, TC], F32, lnps),
                    P.ps("at_ln_rb", [128, TC], F32, lnps),
                    rows)
        emit_ln(xn2, scr, rstd_t, ps_st_ln)
        lnps.close()

        qps = ExitStack()
        ps_q = [P.ps(f"at_ps_q{i}", [128, TC], F32, qps) for i in range(2)]
        qfetch = {}
        qmm = {}
        qev = {}
        for qg in range(4):
            def _(q, qg=qg):
                if qg >= 1:
                    q.wait_ge(pe.h, qmm[qg - 1])
                for kc in range(KC):
                    tdma_attnA(q, qsl, kc * 512, qg * 512, 512, kc)
                qfetch[qg] = qd.n
            SYNC(_)
            for qc4 in range(4):
                hidx = qg * 4 + qc4
                bq = hidx % 2
                def _(t, qg=qg, qc4=qc4, hidx=hidx, bq=bq):
                    t.wait_ge(qd.h, qfetch[qg])
                    if qc4 == 0 and qg == 0:
                        t.wait_ge(ve.h, ve.n)
                    if hidx >= 2:
                        t.wait_ge(ve.h, qev[hidx - 2])
                    last = None
                    for kc in range(KC):
                        for hh in range(2):
                            last = t.matmul(ps_q[bq][:, hh * 512:(hh + 1) * 512],
                                            qsl[:, kc * 512 + qc4 * 128:kc * 512 + qc4 * 128 + 128],
                                            xn2[:, kc * TC + hh * 512:kc * TC + (hh + 1) * 512],
                                            start=(kc == 0), stop=(kc == KC - 1))
                    pe.inc(last)
                    qmm[qg] = pe.n
                TEN(_)
                def _(v, hidx=hidx, bq=bq):
                    v.wait_ge(pe.h, pe.n)
                    v.wait_ge(ad.h, ad.n)
                    v.wait_ge(qd.h, qd.n)
                    ve.inc(v.tensor_tensor(tmpa[:], ps_q[bq][0:32, :], cosq[0:32, :], ALU.mult))
                    ve.inc(v.tensor_tensor(tmpb[:], ps_q[bq][32:64, :], sinq[0:32, :], ALU.mult))
                    v.wait_ge(ve.h, ve.n)
                    ve.inc(v.tensor_tensor(qt_tile[0:32, :], tmpa[:], tmpb[:], ALU.subtract))
                    ve.inc(v.tensor_tensor(tmpa[:], ps_q[bq][32:64, :], cosq[32:64, :], ALU.mult))
                    ve.inc(v.tensor_tensor(tmpb[:], ps_q[bq][0:32, :], sinq[32:64, :], ALU.mult))
                    v.wait_ge(ve.h, ve.n)
                    ve.inc(v.tensor_tensor(qt_tile[32:64, :], tmpa[:], tmpb[:], ALU.add))
                    ve.inc(v.tensor_scalar(qt_tile[64:128, :], ps_q[bq][64:128, :], SCALE, None, ALU.mult))
                    qev[hidx] = ve.n
                VEC(_)
                def _(a, hidx=hidx):
                    a.wait_ge(ve.h, qev[hidx])
                    ad.inc(a.dma_start(out=qT_dram.ap()[hidx * 128:(hidx + 1) * 128, :],
                                       in_=qt_tile[:]), 16)
                SCA(_)
        qps.close()

        kps = ExitStack()
        ps_l = [P.ps(f"at_ps_l{i}", [128, TC], F32, kps) for i in range(2)]
        ps_ssq = P.ps("at_ps_ssq", [1, TC], F32, kps)
        ps_rsb = P.ps("at_ps_rsb", [128, TC], F32, kps)
        def _(q):
            q.wait_ge(pe.h, pe.n)
            for kc in range(KC):
                tdma_attnA(q, qsl, kc * 512, D + 0, 512, kc)
        SYNC(_)
        kvafetch = qd.n
        lev = {}
        for lc in range(LC):
            bl = lc % 2
            def _(t, lc=lc, bl=bl):
                t.wait_ge(qd.h, kvafetch)
                if lc >= 2:
                    t.wait_ge(ve.h, lev[lc - 2])
                last = None
                for kc in range(KC):
                    for hh in range(2):
                        last = t.matmul(ps_l[bl][:, hh * 512:(hh + 1) * 512],
                                        qsl[:, kc * 512 + lc * 128:kc * 512 + lc * 128 + 128],
                                        xn2[:, kc * TC + hh * 512:kc * TC + (hh + 1) * 512],
                                        start=(kc == 0), stop=(kc == KC - 1))
                pe.inc(last)
            TEN(_)
            def _(v, lc=lc, bl=bl):
                v.wait_ge(pe.h, pe.n)
                ve.inc(v.tensor_copy(lat_f[:, lc * TC:(lc + 1) * TC], ps_l[bl][:]))
                lev[lc] = ve.n
            VEC(_)
            def _(v, lc=lc):
                v.wait_ge(ve.h, lev[lc])
                ve.inc(v.tensor_tensor(scr[lc % 2][:], lat_f[:, lc * TC:(lc + 1) * TC],
                                       lat_f[:, lc * TC:(lc + 1) * TC], ALU.mult))
            VEC(_)
            def _(t, lc=lc):
                t.wait_ge(ve.h, ve.n)
                last = None
                for hh in range(2):
                    last = t.matmul(ps_ssq[0:1, hh * 512:(hh + 1) * 512], ones_col_f[:],
                                    scr[lc % 2][:, hh * 512:(hh + 1) * 512],
                                    start=(lc == 0), stop=(lc == LC - 1))
                pe.inc(last)
            TEN(_)
        def _(v):
            v.wait_ge(pe.h, pe.n)
            if _DEV_DUMP:
                ve.inc(v.tensor_copy(rows[2][:], ps_ssq[0:1, :]))
            ve.inc(v.tensor_scalar(rows[0][:], ps_ssq[0:1, :], 1.0 / LAT, 1e-6, ALU.mult, ALU.add))
        VEC(_)
        if _DEV_DUMP:
            def _(g):
                g.wait_ge(ve.h, ve.n)
                gd.inc(g.dma_start(out=dbg_chain.ap()[0:1, 0:TC], in_=rows[2][:]), 16)
                gd.inc(g.dma_start(out=dbg_chain.ap()[0:1, TC:2 * TC], in_=rows[0][:]), 16)
            GPS(_)
        def _(t):
            t.wait_ge(ve.h, ve.n)
            last = None
            for hh in range(2):
                last = t.matmul(ps_rsb[:, hh * 512:(hh + 1) * 512], ones_row_f[:],
                                rows[0][0:1, hh * 512:(hh + 1) * 512], start=True, stop=True)
            pe.inc(last)
        TEN(_)
        def _(s):
            s.wait_ge(pe.h, pe.n)
            s.wait_ge(se.h, se.n)
            se.inc(s.activation(scr[0][:], ps_rsb[:], AF.Sqrt))
        SCA(_)
        def _(v):
            v.wait_ge(se.h, se.n)
            ve.inc(v.reciprocal(rstd_t[:], scr[0][:]))
        VEC(_)
        for lc in range(LC):
            def _(v, lc=lc):
                v.wait_ge(ve.h, ve.n)
                ve.inc(v.scalar_tensor_tensor(lat_b[:, lc * TC:(lc + 1) * TC],
                                              lat_f[:, lc * TC:(lc + 1) * TC],
                                              kvn_sb[:, lc:lc + 1], rstd_t[:],
                                              ALU.mult, ALU.mult))
            VEC(_)
        if _DEV_DUMP:
            def _(g):
                g.wait_ge(ve.h, ve.n)
                gd.inc(g.dma_start(out=dbg_latf.ap(), in_=lat_f[:]), 16)
                gd.inc(g.dma_start(out=dbg_row.ap(), in_=rows[0][:]), 16)
            GPS(_)
        def _(g):
            g.wait_ge(ve.h, ve.n)
            dst = latn_my.ap().rearrange("(lc p) t -> p lc t", p=128)
            gd.inc(g.dma_start(out=dst, in_=lat_b[:, 0:LC * TC].rearrange("p (lc t) -> p lc t", t=TC)), 16)
            g.wait_ge(gd.h, gd.n)
            cc.inc(g.collective_compute(
                "AllGather", ALU.bypass,
                replica_groups=[[0, 1, 2, 3], [4, 5, 6, 7]],
                ins=[latn_my.ap().opt()], outs=[latn_all.ap().opt()]))
        GPS(_)
        kps.close()
        s1.close()

        s2 = ExitStack()
        latn_sb = P.sb("at_latn", [128, LC * SEQ], BF16, s2)
        kT = P.sb("at_kT", [128, SEQ], BF16, s2)
        vB = P.sb("at_vB", [128, SEQ], BF16, s2)
        qh = [P.sb(f"at_qh{i}", [128, TC], BF16, s2) for i in range(2)]
        pT = [P.sb(f"at_pT{i}", [128, TC], BF16, s2) for i in range(2)]
        kvbsl = [P.sb(f"at_kvb{i}", [128, LC * 256], BF16, s2) for i in range(2)]
        kvnat = [P.sb(f"at_kvn{i}", [128, 1024], BF16, s2) for i in range(2)]
        ksq = P.sb("at_ksq", [128, SEQ], BF16, s2)
        qsq = P.sb("at_qsq", [128, TC], BF16, s2)
        recip_sb = P.sb("at_recip", [128, TC], F32, s2)
        ht = P.sb("at_ht", [128, TC], BF16, s2)
        biasC = P.sb("at_biasC", [128, 1], F32, s2)
        tmpa2 = P.sb("at_tmpa2", [32, TC], F32, s2)
        tmpb2 = P.sb("at_tmpb2", [32, TC], F32, s2)
        srow = P.sb("at_srow", [1, 8], F32, s2)
        nrow = P.sb("at_nrow", [1, TC], F32, s2)

        def _(g):
            g.wait_ge(cc.h, cc.n)
            if _DEV_DUMP:
                gd.inc(g.dma_start(out=dbg_latn.ap(), in_=latn_all.ap()), 16)
                gd.inc(g.dma_start(out=dbg_bncC.ap(), in_=bncC.ap()[C_DOWN_ROWS:C_DOWN_ROWS + 16, :]), 16)
                gd.inc(g.dma_start(out=dbg_gathC.ap()[0:16, :], in_=gathC.ap()[C_DOWN_ROWS:C_DOWN_ROWS + 16, :]), 16)
                gd.inc(g.dma_start(out=dbg_gathC.ap()[16:32, :], in_=gathC.ap()[C_BLK + C_DOWN_ROWS:C_BLK + C_DOWN_ROWS + 16, :]), 16)
            for lc in range(LC):
                src = APr(latn_all, lc * 128 * TC, [[TC, 128], [LAT * TC, KB], [1, TC]])
                dst = latn_sb[:, lc * SEQ:(lc + 1) * SEQ].rearrange("p (r t) -> p r t", t=TC)
                gd.inc(g.dma_start(out=dst, in_=src), 16)
        GPS(_)

        head_end_pe = {}
        head_exp = {}
        qh_cnt = {}
        kvb_evac = {}
        for h in range(NH):
            blk = h // 2
            def _(a, h=h):
                if h >= 2:
                    a.wait_ge(pe.h, head_end_pe[h - 2])
                else:
                    a.wait_ge(ve.h, ve.n)
                    a.wait_ge(gd.h, gd.n)
                ad.inc(a.dma_start(out=qh[h % 2][:],
                                   in_=qT_dram.ap()[h * 128:(h + 1) * 128, :]), 16)
                qh_cnt[h] = ad.n
            SCA(_)
            def _(q, h=h, blk=blk):
                if h >= 2:
                    q.wait_ge(pe.h, head_end_pe[h - 2])
                else:
                    q.wait_ge(ve.h, ve.n)
                    q.wait_ge(gd.h, gd.n)
                for t2 in range(2):
                    src = APr(gathC, (blk * C_BLK + C_DOWN_ROWS) * NMLP
                              + ((h % 2) * 256 + t2 * 128) * LAT,
                              [[LAT, 128], [1, 512]])
                    q.dma_start(out=kvnat[h % 2][:, t2 * 512:(t2 + 1) * 512],
                                in_=src).then_inc(qd.h, 16)
                    qd.n += 16
            SYNC(_)
            kvb_cnt = qd.n

            sc1 = ExitStack()
            ps_kv = [P.ps(f"at_kv{h}_{i}", [128, TC], F32, sc1) for i in range(2)]
            ps_v = P.ps(f"at_v{h}", [128, 512], F32, sc1)
            ps_st = P.ps(f"at_st{h}", [1, 512], F32, sc1)
            ps_b1 = P.ps(f"at_b1{h}", [128, 1], F32, sc1)
            ps_tp = P.ps(f"at_tp{h}", [128, 512], BF16, sc1)

            for t2 in range(2):
                def _(t, h=h, t2=t2):
                    t.wait_ge(qd.h, kvb_cnt)
                    if h == 0 and t2 == 0:
                        t.wait_ge(gd.h, gd.n)
                        t.wait_ge(ad.h, ad.n)
                    if h >= 1 and t2 == 0:
                        t.wait_ge(se.h, se.n)
                        t.wait_ge(ve.h, ve.n)
                    if t2 == 1:
                        t.wait_ge(ve.h, ve.n)
                    last = None
                    for lc in range(LC):
                        last = t.transpose(ps_tp[:, lc * 128:(lc + 1) * 128],
                                           kvnat[h % 2][:, t2 * 512 + lc * 128:t2 * 512 + (lc + 1) * 128],
                                           identb[:])
                    pe.inc(last)
                TEN(_)
                def _(v, h=h, t2=t2):
                    v.wait_ge(pe.h, pe.n)
                    dst = APr(kvbsl[h % 2], t2 * 128, [[1024, 128], [256, 4], [1, 128]])
                    ve.inc(v.tensor_copy(dst, ps_tp[:].rearrange("p (lc n) -> p lc n", n=128)))
                    kvb_evac[h] = ve.n
                VEC(_)

            krope = {}
            for r in range(KB):
                br = r % 2
                def _(t, h=h, r=r, br=br):
                    t.wait_ge(ve.h, kvb_evac[h])
                    if r >= 2:
                        t.wait_ge(ve.h, krope[r - 2])
                    if h >= 1 and r < 2:
                        t.wait_ge(se.h, se.n)
                        t.wait_ge(ve.h, ve.n)
                    last = None
                    for lc in range(LC):
                        for hh in range(2):
                            last = t.matmul(ps_kv[br][:, hh * 512:(hh + 1) * 512],
                                            kvbsl[h % 2][:, lc * 256:lc * 256 + 128],
                                            latn_sb[:, lc * SEQ + r * TC + hh * 512:lc * SEQ + r * TC + (hh + 1) * 512],
                                            start=(lc == 0), stop=(lc == LC - 1))
                    pe.inc(last)
                TEN(_)
                def _(v, r=r, br=br):
                    v.wait_ge(pe.h, pe.n)
                    cols = slice(r * TC, (r + 1) * TC)
                    ve.inc(v.tensor_tensor(tmpa2[:], ps_kv[br][0:32, :], cosk[0:32, cols], ALU.mult))
                    ve.inc(v.tensor_tensor(tmpb2[:], ps_kv[br][32:64, :], sink[0:32, cols], ALU.mult))
                    v.wait_ge(ve.h, ve.n)
                    ve.inc(v.tensor_tensor(kT[0:32, cols], tmpa2[:], tmpb2[:], ALU.subtract))
                    ve.inc(v.tensor_tensor(tmpa2[:], ps_kv[br][32:64, :], cosk[32:64, cols], ALU.mult))
                    ve.inc(v.tensor_tensor(tmpb2[:], ps_kv[br][0:32, :], sink[32:64, cols], ALU.mult))
                    v.wait_ge(ve.h, ve.n)
                    ve.inc(v.tensor_tensor(kT[32:64, cols], tmpa2[:], tmpb2[:], ALU.add))
                    ve.inc(v.tensor_copy(kT[64:128, cols], ps_kv[br][64:128, :]))
                    krope[r] = ve.n
                VEC(_)

            vev = {}
            for kt4 in range(8):
                def _(t, h=h, kt4=kt4):
                    if kt4 >= 1:
                        t.wait_ge(ve.h, vev[kt4 - 1])
                    last = None
                    for j in range(4):
                        kt = kt4 * 4 + j
                        for lc in range(LC):
                            last = t.matmul(ps_v[:, j * 128:(j + 1) * 128],
                                            latn_sb[:, lc * SEQ + kt * 128:lc * SEQ + (kt + 1) * 128],
                                            kvbsl[h % 2][:, lc * 256 + 128:lc * 256 + 256],
                                            start=(lc == 0), stop=(lc == LC - 1))
                    pe.inc(last)
                TEN(_)
                def _(v, kt4=kt4):
                    v.wait_ge(pe.h, pe.n)
                    ve.inc(v.tensor_copy(vB[:, kt4 * 512:(kt4 + 1) * 512], ps_v[:]))
                    vev[kt4] = ve.n
                VEC(_)

            if _DEV_DUMP and h == 3:
                def _(g):
                    g.wait_ge(ve.h, ve.n)
                    gd.inc(g.dma_start(out=dbg_vB2.ap(), in_=vB[:]), 16)
                    gd.inc(g.dma_start(out=dbg_kvb.ap(), in_=kvbsl[1][:]), 16)
                GPS(_)
                def _(s):
                    s.wait_ge(gd.h, gd.n)
                SCA(_)
            def _(v, h=h):
                v.wait_ge(ve.h, ve.n)
                v.wait_ge(ad.h, qh_cnt[h])
                ve.inc(v.tensor_tensor(ksq[:], kT[:], kT[:], ALU.mult))
                ve.inc(v.tensor_tensor(qsq[:], qh[h % 2][:], qh[h % 2][:], ALU.mult))
            VEC(_)
            def _(v):
                v.wait_ge(ve.h, ve.n)
                ve.inc(v.memset(srow[:], -1e30))
            VEC(_)
            for i in range(10):
                def _(t, i=i):
                    t.wait_ge(ve.h, ve.n)
                    src = ksq[:, i * 512:(i + 1) * 512] if i < 8 else qsq[:, (i - 8) * 512:(i - 7) * 512]
                    pe.inc(t.matmul(ps_st[0:1, :], ones_col_b[:], src, start=True, stop=True))
                TEN(_)
                def _(v, i=i):
                    v.wait_ge(pe.h, pe.n)
                    j = 0 if i < 8 else 1
                    ve.inc(v.reduce_max(srow[0:1, 2 + i:3 + i] if False else nrow[0:1, 0:1],
                                        ps_st[0:1, :], axis=AX))
                    v.wait_ge(ve.h, ve.n)
                    ve.inc(v.tensor_tensor(srow[0:1, j:j + 1], srow[0:1, j:j + 1],
                                           nrow[0:1, 0:1], ALU.max))
                VEC(_)
            def _(v):
                v.wait_ge(ve.h, ve.n)
                ve.inc(v.tensor_scalar(srow[0:1, 2:3], srow[0:1, 0:1], -0.5, None, ALU.mult))
                v.wait_ge(ve.h, ve.n)
                ve.inc(v.scalar_tensor_tensor(srow[0:1, 3:4], srow[0:1, 1:2], -0.5,
                                              srow[0:1, 2:3], ALU.mult, ALU.add))
            VEC(_)
            def _(t):
                t.wait_ge(ve.h, ve.n)
                pe.inc(t.matmul(ps_b1[:], ones_row_f[:], srow[0:1, 3:4], start=True, stop=True))
            TEN(_)
            def _(v):
                v.wait_ge(pe.h, pe.n)
                ve.inc(v.tensor_copy(biasC[:], ps_b1[:]))
            VEC(_)
            sc1.close()

            if _DEV_DUMP and h == 3:
                def _(g):
                    g.wait_ge(ve.h, ve.n)
                    gd.inc(g.dma_start(out=dbg_kT.ap(), in_=kT[:]), 16)
                    gd.inc(g.dma_start(out=dbg_vB.ap(), in_=vB[:]), 16)
                GPS(_)
            sc2 = ExitStack()
            ps_sT = [P.ps(f"at_sT{h}_{i}", [128, TC], F32, sc2) for i in range(2)]
            ps_OT = P.ps(f"at_OT{h}", [128, TC], F32, sc2)
            ps_rs = P.ps(f"at_rs{h}", [1, TC], F32, sc2)
            sT_mm = {}
            exp_done = {}
            av_mm = {}
            _peb = pe.n
            _c = _peb
            for kt in range(32):
                _c += 1
                sT_mm[kt] = _c
                if kt >= 2:
                    _c += 1
                    av_mm[kt - 2] = _c
            av_mm[30] = _c + 1
            av_mm[31] = _c + 2
            _seb = se.n
            for kt in range(32):
                exp_done[kt] = _seb + kt + 1

            def _(t, h=h):
                for kt in range(32):
                    bk = kt % 2
                    if kt == 0:
                        t.wait_ge(ve.h, ve.n)
                        t.wait_ge(ad.h, qh_cnt[h])
                    if kt >= 2:
                        t.wait_ge(se.h, exp_done[kt - 2])
                    last = None
                    for hh in range(2):
                        last = t.matmul(ps_sT[bk][:, hh * 512:(hh + 1) * 512],
                                        kT[:, kt * 128:(kt + 1) * 128],
                                        qh[h % 2][:, hh * 512:(hh + 1) * 512],
                                        start=True, stop=True)
                    pe.inc(last)
                    assert pe.n == sT_mm[kt]
                    if kt >= 2:
                        kk = kt - 2
                        bb2 = kk % 2
                        t.wait_ge(se.h, exp_done[kk])
                        for hh in range(2):
                            t.matmul(ps_OT[:, hh * 512:(hh + 1) * 512],
                                     vB[:, kk * 128:(kk + 1) * 128],
                                     pT[bb2][:, hh * 512:(hh + 1) * 512],
                                     start=(kk == 0), stop=(kk == 31))
                        last = None
                        for hh in range(2):
                            last = t.matmul(ps_rs[0:1, hh * 512:(hh + 1) * 512],
                                            ones_col_b[:],
                                            pT[bb2][:, hh * 512:(hh + 1) * 512],
                                            start=(kk == 0), stop=(kk == 31))
                        pe.inc(last)
                        assert pe.n == av_mm[kk]
                for kk in (30, 31):
                    bb2 = kk % 2
                    t.wait_ge(se.h, exp_done[kk])
                    for hh in range(2):
                        t.matmul(ps_OT[:, hh * 512:(hh + 1) * 512],
                                 vB[:, kk * 128:(kk + 1) * 128],
                                 pT[bb2][:, hh * 512:(hh + 1) * 512],
                                 start=(kk == 0), stop=(kk == 31))
                    last = None
                    for hh in range(2):
                        last = t.matmul(ps_rs[0:1, hh * 512:(hh + 1) * 512],
                                        ones_col_b[:],
                                        pT[bb2][:, hh * 512:(hh + 1) * 512],
                                        start=(kk == 0), stop=(kk == 31))
                    pe.inc(last)
                    assert pe.n == av_mm[kk]
            TEN(_)

            def _(s, h=h):
                for kt in range(32):
                    bk = kt % 2
                    s.wait_ge(pe.h, sT_mm[kt])
                    if kt >= 2:
                        s.wait_ge(pe.h, av_mm.get(kt - 2, 0))
                    se.inc(s.activation(pT[bk][:], ps_sT[bk][:], AF.Exp, bias=biasC[:, 0:1]))
                    assert se.n == exp_done[kt]
            SCA(_)

            def _(v):
                v.wait_ge(pe.h, pe.n)
                ve.inc(v.reciprocal(nrow[:], ps_rs[0:1, :]))
            VEC(_)
            def _(t):
                t.wait_ge(ve.h, ve.n)
                t.wait_ge(se.h, se.n)
                last = None
                for hh in range(2):
                    last = t.matmul(ps_sT[0][:, hh * 512:(hh + 1) * 512], ones_row_f[:],
                                    nrow[0:1, hh * 512:(hh + 1) * 512], start=True, stop=True)
                pe.inc(last)
            TEN(_)
            def _(v, h=h):
                v.wait_ge(pe.h, pe.n)
                v.wait_ge(gd.h, gd.n)
                if _DEV_DUMP:
                    ve.inc(v.tensor_copy(srow[0:1, 4:5], biasC[0:1, 0:1]))
                    ve.inc(v.tensor_copy(srow[0:1, 5:6], ps_rs[0:1, 0:1]))
                    ve.inc(v.tensor_copy(srow[0:1, 6:7], ps_rs[0:1, 17:18]))
                    ve.inc(v.tensor_copy(srow[0:1, 7:8], ps_OT[0:1, 0:1]))
                ve.inc(v.tensor_copy(recip_sb[:], ps_sT[0][:]))
                v.wait_ge(ve.h, ve.n)
                ve.inc(v.tensor_tensor(ht[:], ps_OT[:], recip_sb[:], ALU.mult))
            VEC(_)
            if _DEV_DUMP:
                def _(g, h=h):
                    g.wait_ge(ve.h, ve.n)
                    gd.inc(g.dma_start(out=dbg_stats.ap()[h:h + 1, :], in_=srow[:]), 16)
                GPS(_)
            def _(g, h=h):
                g.wait_ge(ve.h, ve.n)
                gd.inc(g.dma_start(out=heads_dram.ap()[h * 128:(h + 1) * 128, :], in_=ht[:]), 16)
            GPS(_)
            sc2.close()
            head_end_pe[h] = pe.n
        s2.close()

        s3 = ExitStack()
        heads_sb = P.sb("at_heads", [128, NH * TC], BF16, s3)
        osl = [P.sb(f"at_osl{i}", [128, 16 * 512], BF16, s3, is_transpose_dest=True, align_bytes=4096) for i in range(2)]
        ps_o = [P.ps(f"at_ps_o{i}", [128, TC], F32, s3) for i in range(2)]
        def _(g):
            g.wait_ge(gd.h, gd.n)
            g.wait_ge(pe.h, pe.n)
            g.wait_ge(ve.h, ve.n)
            g.wait_ge(se.h, se.n)
            src = heads_dram.ap().rearrange("(hh p) t -> p hh t", p=128)
            gd.inc(g.dma_start(out=heads_sb[:].rearrange("p (hh t) -> p hh t", t=TC), in_=src), 16)
        GPS(_)
        ofetch = {}
        omm = {}
        oev = {}
        for og in range(4):
            def _(q, og=og):
                if og >= 2:
                    q.wait_ge(pe.h, omm[og - 2])
                else:
                    q.wait_ge(pe.h, pe.n)
                    q.wait_ge(ve.h, ve.n)
                    q.wait_ge(se.h, se.n)
                    q.wait_ge(gd.h, gd.n)
                for odc in range(KC):
                    tdma_attnA(q, osl[og % 2], odc * 512, D + LAT + og * 512, 512, odc)
                ofetch[og] = qd.n
            SYNC(_)
            for oc4 in range(4):
                oc = og * 4 + oc4
                bo = oc % 2
                def _(t, og=og, oc4=oc4, oc=oc, bo=bo):
                    t.wait_ge(qd.h, ofetch[og])
                    if oc == 0:
                        t.wait_ge(gd.h, gd.n)
                    if oc >= 2:
                        t.wait_ge(ve.h, oev[oc - 2])
                    last = None
                    for odc in range(KC):
                        for hh in range(2):
                            last = t.matmul(ps_o[bo][:, hh * 512:(hh + 1) * 512],
                                            osl[og % 2][:, odc * 512 + oc4 * 128:odc * 512 + oc4 * 128 + 128],
                                            heads_sb[:, odc * TC + hh * 512:odc * TC + (hh + 1) * 512],
                                            start=(odc == 0), stop=(odc == KC - 1))
                    pe.inc(last)
                    omm[og] = pe.n
                TEN(_)
                def _(v, oc=oc, bo=bo):
                    v.wait_ge(pe.h, pe.n)
                    ve.inc(v.tensor_tensor(xT[:, oc * TC:(oc + 1) * TC],
                                           xT[:, oc * TC:(oc + 1) * TC],
                                           ps_o[bo][:], ALU.add))
                    oev[oc] = ve.n
                VEC(_)
        s3.close()
        ph.close()

    if stop in ("ffn1", "attn", "full"):
        emit_ffn("f1")
    if stop in ("attn", "full"):
        emit_attn()
    if stop == "full":
        emit_ffn("f2")

    phE = ExitStack()
    xo_sb = P.sb("xo_sb", [128, D], F32, phE)
    ps_e = [P.ps(f"ps_e{i}", [128, 512], F32, phE) for i in range(2)]
    for tt in range(TT):
        for kg in range(4):
            bb = (tt * 4 + kg) % 2
            def _(t, tt=tt, kg=kg, bb=bb):
                if tt == 0 and kg == 0:
                    t.wait_ge(ve.h, ve.n)
                    t.wait_ge(cc.h, cc.n)
                t.wait_ge(ve.h, ve.n)
                last = None
                for j in range(4):
                    kc = kg * 4 + j
                    last = t.transpose(ps_e[bb][:, j * 128:(j + 1) * 128],
                                       xT[:, kc * TC + tt * 128:kc * TC + (tt + 1) * 128],
                                       identf[:])
                pe.inc(last)
            TEN(_)
            def _(v, tt=tt, kg=kg, bb=bb):
                v.wait_ge(pe.h, pe.n)
                if kg == 0:
                    v.wait_ge(ad.h, ad.n)
                    v.wait_ge(gd.h, gd.n)
                ve.inc(v.tensor_copy(xo_sb[:, kg * 512:(kg + 1) * 512], ps_e[bb][:]))
            VEC(_)
        def _(q, tt=tt):
            q.wait_ge(ve.h, ve.n)
            ad.inc(q.dma_start(out=y_out.ap()[tt * 128:(tt + 1) * 128, :], in_=xo_sb[:]), 16)
        SCA(_)
    def _(q):
        q.wait_ge(ad.h, ad.n)
    SCA(_)
    phE.close()

    P.st.close()
    return nc


def kernel(**inputs):
    x = np.asarray(inputs["L0_attn_hidden_states"], np.float32)
    q_w = np.asarray(inputs["L0_attn_q_w"], np.float32)
    kv_a_w = np.asarray(inputs["L0_attn_kv_a_w"], np.float32)
    kv_b_w = np.asarray(inputs["L0_attn_kv_b_w"], np.float32)
    o_w = np.asarray(inputs["L0_attn_o_w"], np.float32)
    kv_norm_w = np.asarray(inputs["L0_attn_kv_norm_w"], np.float32)
    g_pk = np.asarray(inputs["L0_ffn_gate_W_packed"], np.int32)
    g_sc = np.asarray(inputs["L0_ffn_gate_scales"], np.float32)
    u_pk = np.asarray(inputs["L0_ffn_up_W_packed"], np.int32)
    u_sc = np.asarray(inputs["L0_ffn_up_scales"], np.float32)
    d_pk = np.asarray(inputs["L0_ffn_down_W_packed"], np.int32)
    d_sc = np.asarray(inputs["L0_ffn_down_scales"], np.float32)

    Pd = _perm(D)
    Pn = _perm(NMLP)

    xf = np.ascontiguousarray(x.reshape(T, D)[:, Pd])
    q_wp = np.ascontiguousarray(q_w[:, Pd])
    kva_p = np.ascontiguousarray(kv_a_w[ROPE:, Pd])
    o_wp = np.ascontiguousarray(o_w[Pd, :])
    attn_stack = np.concatenate([q_wp, kva_p, o_wp], axis=0)
    g_pkp = np.ascontiguousarray(g_pk[Pn]); g_scp = np.ascontiguousarray(g_sc[Pn])
    u_pkp = np.ascontiguousarray(u_pk[Pn]); u_scp = np.ascontiguousarray(u_sc[Pn])
    d_pkp = np.ascontiguousarray(d_pk[Pd]); d_scp = np.ascontiguousarray(d_sc[Pd])
    kvn_dev = np.ascontiguousarray(kv_norm_w.reshape(LC, 128).T)

    cos, sin = _yarn_cos_sin()
    cosk = np.ascontiguousarray(cos.T)
    sink = np.ascontiguousarray(sin.T)
    cosq_s = (cos.T * np.float32(SCALE)).astype(np.float32)
    sinq_s = (sin.T * np.float32(SCALE)).astype(np.float32)

    import ml_dtypes
    identb = np.eye(128, dtype=ml_dtypes.bfloat16)
    identf = np.eye(128, dtype=np.float32)

    in_maps = []
    for c in range(NCORES):
        sq = (c % GRP) * TC
        in_maps.append({
            "x_in": np.ascontiguousarray(xf[c * TC:(c + 1) * TC]),
            "gpk": np.ascontiguousarray(g_pkp[c * GATE_SH:(c + 1) * GATE_SH]),
            "gsc": np.ascontiguousarray(g_scp[c * GATE_SH:(c + 1) * GATE_SH]),
            "upk": np.ascontiguousarray(u_pkp[c * GATE_SH:(c + 1) * GATE_SH]),
            "usc": np.ascontiguousarray(u_scp[c * GATE_SH:(c + 1) * GATE_SH]),
            "dpk": np.ascontiguousarray(d_pkp[c * DOWN_SH:(c + 1) * DOWN_SH]),
            "dsc": np.ascontiguousarray(d_scp[c * DOWN_SH:(c + 1) * DOWN_SH]),
            "attnA": np.ascontiguousarray(attn_stack[c * ATTN_SH:(c + 1) * ATTN_SH]),
            "kvbW": np.ascontiguousarray(kv_b_w[c * KVB_SH:(c + 1) * KVB_SH]),
            "kvnorm": kvn_dev,
            "cosq": np.ascontiguousarray(cosq_s[:, sq:sq + TC]),
            "sinq": np.ascontiguousarray(sinq_s[:, sq:sq + TC]),
            "cosk": cosk,
            "sink": sink,
            "identb": identb,
            "identf": identf,
        })

    nc = _build()
    res = run_bass_kernel_spmd(nc, in_maps, core_ids=list(range(NCORES)))

    out = np.empty((T, D), np.float32)
    for c in range(NCORES):
        out[c * TC:(c + 1) * TC] = res.results[c]["y"]
    inv = np.empty((T, D), np.float32)
    inv[:, Pd] = out
    return inv.reshape(B, S, D)


# version 05 (final)
# speedup vs baseline: 1.1070x; 1.0017x over previous
import math
import numpy as np
from contextlib import ExitStack

import concourse.bass as bass
import concourse.mybir as mybir
from concourse.bass_utils import run_bass_kernel_spmd

F32 = mybir.dt.float32
BF16 = mybir.dt.bfloat16
I32 = mybir.dt.int32
AF = mybir.ActivationFunctionType
ALU = mybir.AluOpType
AX = mybir.AxisListType.X

D = 2048
KC = D // 128
NMLP = 16384
NH = 16
HD = 128
ROPE = 64
LAT = 512
LC = LAT // 128
B, S = 2, 4096
T = B * S
NCORES = 8
TC = T // NCORES
TT = TC // 128
SCALE = float(HD) ** -0.5
GRP = NCORES // B
SEQ = S
KB = SEQ // TC

GATE_SH = NMLP // NCORES
DOWN_SH = D // NCORES
ATTN_ROWS = D + LAT + D
ATTN_SH = ATTN_ROWS // NCORES
KVB_SH = (2 * D) // NCORES
A_BLK = 2 * GATE_SH + ATTN_SH
A_ATTN_OFF = 2 * GATE_SH
C_DOWN_ROWS = DOWN_SH
C_KVB_ROWS = (KVB_SH * LAT) // NMLP
C_BLK = C_DOWN_ROWS + C_KVB_ROWS

_DEV_STOP = "full"
_DEV_DUMP = False


def _yarn_cos_sin():
    ROPE_DIM = 64
    MAX_POS = 131072
    ORIG_POS = 8192
    BASE = 500000.0
    BETA_FAST, BETA_SLOW = (32.0, 1.0)
    scaling = MAX_POS / ORIG_POS
    pos_freqs = BASE ** (np.arange(0, ROPE_DIM, 2, dtype=np.float64) / ROPE_DIM)
    extrap = 1.0 / pos_freqs
    interp = 1.0 / (scaling * pos_freqs)
    low = max(math.floor(ROPE_DIM * math.log(ORIG_POS / (BETA_FAST * 2 * math.pi)) / (2 * math.log(BASE))), 0)
    high = min(math.ceil(ROPE_DIM * math.log(ORIG_POS / (BETA_SLOW * 2 * math.pi)) / (2 * math.log(BASE))), ROPE_DIM - 1)
    i = np.arange(ROPE_DIM // 2, dtype=np.float64)
    smooth = np.clip((i - low) / max(high - low, 1), 0.0, 1.0)
    inv_freq = np.asarray((1.0 - smooth) * interp + smooth * extrap, dtype=np.float32)
    pos = np.arange(S, dtype=np.float32)
    freqs = pos[:, None] * inv_freq[None, :]
    emb = np.concatenate([freqs, freqs], axis=-1)
    mscale = np.float32(0.1 * math.log(scaling) + 1.0)
    cos = (np.cos(emb) * mscale).astype(np.float32)
    sin = (np.sin(emb) * mscale).astype(np.float32)
    return cos, sin


def _perm(n):
    return np.concatenate([np.arange(0, n, 2), np.arange(1, n, 2)])


class Sem:
    def __init__(self, st, nc, name):
        self.h = st.enter_context(nc.semaphore(name))
        self.n = 0

    def inc(self, instr, k=1):
        instr.then_inc(self.h, k)
        self.n += k
        return self.n


def APr(handle, offset, dims):
    return bass.AP(handle, int(offset), [[int(s), int(n)] for s, n in dims])


class Prog:

    def __init__(self):
        self.nc = bass.Bass("TRN2", target_bir_lowering=False, debug=False)
        self.st = ExitStack()

    def sem(self, name):
        return Sem(self.st, self.nc, name)

    def sb(self, name, shape, dtype, stack=None, **kw):
        return (stack or self.st).enter_context(self.nc.sbuf_tensor(name, shape, dtype, **kw))

    def ps(self, name, shape, dtype=F32, stack=None):
        return (stack or self.st).enter_context(self.nc.psum_tensor(name, shape, dtype))


def _build(stop=None):
    stop = stop or _DEV_STOP
    P = Prog()
    nc = P.nc

    x_in = nc.dram_tensor("x_in", [TC, D], F32, kind="ExternalInput")
    gpk = nc.dram_tensor("gpk", [GATE_SH, D // 2], I32, kind="ExternalInput")
    gsc = nc.dram_tensor("gsc", [GATE_SH, D // 32], F32, kind="ExternalInput")
    upk = nc.dram_tensor("upk", [GATE_SH, D // 2], I32, kind="ExternalInput")
    usc = nc.dram_tensor("usc", [GATE_SH, D // 32], F32, kind="ExternalInput")
    dpk = nc.dram_tensor("dpk", [DOWN_SH, NMLP // 2], I32, kind="ExternalInput")
    dsc = nc.dram_tensor("dsc", [DOWN_SH, NMLP // 32], F32, kind="ExternalInput")
    attnA_in = nc.dram_tensor("attnA", [ATTN_SH, D], F32, kind="ExternalInput")
    kvbW_in = nc.dram_tensor("kvbW", [KVB_SH, LAT], F32, kind="ExternalInput")
    kvnorm_in = nc.dram_tensor("kvnorm", [128, LC], F32, kind="ExternalInput")
    cosq_in = nc.dram_tensor("cosq", [ROPE, TC], F32, kind="ExternalInput")
    sinq_in = nc.dram_tensor("sinq", [ROPE, TC], F32, kind="ExternalInput")
    cosk_in = nc.dram_tensor("cosk", [ROPE, SEQ], F32, kind="ExternalInput")
    sink_in = nc.dram_tensor("sink", [ROPE, SEQ], F32, kind="ExternalInput")
    identb_in = nc.dram_tensor("identb", [128, 128], BF16, kind="ExternalInput")
    identf_in = nc.dram_tensor("identf", [128, 128], F32, kind="ExternalInput")
    y_out = nc.dram_tensor("y", [TC, D], F32, kind="ExternalOutput")

    bncA = nc.dram_tensor("bncA", [A_BLK, D], BF16)
    gathA = nc.dram_tensor("gathA", [NCORES * A_BLK, D], BF16, addr_space="Shared")
    bncC = nc.dram_tensor("bncC", [C_BLK, NMLP], BF16)
    gathC = nc.dram_tensor("gathC", [NCORES * C_BLK, NMLP], BF16, addr_space="Shared")
    dbg = "ExternalOutput" if _DEV_DUMP else "Internal"
    latn_my = nc.dram_tensor("latn_my", [LAT, TC], BF16)
    latn_all = nc.dram_tensor("latn_all", [GRP * LAT, TC], BF16)
    dbg_latn = nc.dram_tensor("dbg_latn", [GRP * LAT, TC], BF16, kind="ExternalOutput") if _DEV_DUMP else None
    qT_dram = nc.dram_tensor("qT_dram", [NH * 128, TC], BF16, kind=dbg)
    heads_dram = nc.dram_tensor("heads_dram", [NH * 128, TC], BF16, kind=dbg)
    dbg_stats = nc.dram_tensor("dbg_stats", [NH, 8], F32, kind=dbg) if _DEV_DUMP else None
    dbg_latf = nc.dram_tensor("dbg_latf", [128, LC * TC], F32, kind=dbg) if _DEV_DUMP else None
    dbg_row = nc.dram_tensor("dbg_row", [1, TC], F32, kind=dbg) if _DEV_DUMP else None
    dbg_chain = nc.dram_tensor("dbg_chain", [1, 4 * TC], F32, kind=dbg) if _DEV_DUMP else None
    dbg_kT = nc.dram_tensor("dbg_kT", [128, SEQ], BF16, kind=dbg) if _DEV_DUMP else None
    dbg_vB = nc.dram_tensor("dbg_vB", [128, SEQ], BF16, kind=dbg) if _DEV_DUMP else None
    dbg_vB2 = nc.dram_tensor("dbg_vB2", [128, SEQ], BF16, kind=dbg) if _DEV_DUMP else None
    dbg_kvb = nc.dram_tensor("dbg_kvb", [128, LC * 256], BF16, kind=dbg) if _DEV_DUMP else None
    dbg_bncC = nc.dram_tensor("dbg_bncC", [16, NMLP], BF16, kind=dbg) if _DEV_DUMP else None
    dbg_gathC = nc.dram_tensor("dbg_gathC", [32, NMLP], BF16, kind=dbg) if _DEV_DUMP else None
    dbg_xtok = nc.dram_tensor("dbg_xtok", [128, 512], F32, kind=dbg) if _DEV_DUMP else None
    dbg_deq = nc.dram_tensor("dbg_deq", [128, 512], BF16, kind=dbg) if _DEV_DUMP else None
    dbg_bnc2 = nc.dram_tensor("dbg_bnc2", [4, NMLP], BF16, kind=dbg) if _DEV_DUMP else None
    dbg_pT = nc.dram_tensor("dbg_pT", [128, TC], BF16, kind=dbg) if _DEV_DUMP else None

    qd = P.sem("qdma")
    ad = P.sem("adma")
    gd = P.sem("gdma")
    pe = P.sem("pe")
    ve = P.sem("ve")
    se = P.sem("se")
    po = P.sem("po")
    cc = P.sem("cc")

    xT = P.sb("xT", [128, KC * TC], F32)
    identb = P.sb("identb_sb", [128, 128], BF16)
    identf = P.sb("identf_sb", [128, 128], F32)
    ones_col_f = P.sb("ones_col_f", [128, 1], F32)
    ones_row_f = P.sb("ones_row_f", [1, 128], F32)
    ones_col_b = P.sb("ones_col_b", [128, 1], BF16)
    kvn_sb = P.sb("kvn_sb", [128, LC], F32)

    block = P.st.enter_context(nc.Block())

    def SYNC(f):
        block.sync(f)

    def TEN(f):
        block.tensor(f)

    def VEC(f):
        block.vector(f)

    def SCA(f):
        block.scalar(f)

    def GPS(f):
        block.gpsimd(f)

    phA = ExitStack()
    x_tok = [P.sb(f"x_tok{i}", [128, D], F32, phA) for i in range(2)]
    ps_tr = [P.ps(f"ps_tr{i}", [128, 512], F32, phA) for i in range(2)]

    def _(q):
        q.dma_start(out=identb[:], in_=identb_in.ap()).then_inc(qd.h, 16)
        q.dma_start(out=identf[:], in_=identf_in.ap()).then_inc(qd.h, 16)
        q.dma_start(out=kvn_sb[:], in_=kvnorm_in.ap()).then_inc(qd.h, 16)
        qd.n += 48
    SYNC(_)

    def _(g):
        g.memset(ones_col_f[:], 1.0)
        g.memset(ones_row_f[:], 1.0)
        po.inc(g.memset(ones_col_b[:], 1.0), 3)
    GPS(_)

    xtok_ready = [None] * TT
    xtok_done = [None] * TT
    evac_cnt = {}

    def _emit_xdma(i):
        def _(q, i=i):
            if i >= 2:
                q.wait_ge(pe.h, xtok_done[i - 2])
            qd.inc(q.dma_start(out=x_tok[i % 2][:], in_=x_in.ap()[i * 128:(i + 1) * 128, :]), 16)
            xtok_ready[i] = qd.n
        SYNC(_)

    _emit_xdma(0)
    _emit_xdma(1)
    for i in range(TT):
        for kg in range(4):
            bb = (i * 4 + kg) % 2
            def _(t, i=i, kg=kg, bb=bb):
                if kg == 0:
                    t.wait_ge(qd.h, xtok_ready[i])
                    t.wait_ge(po.h, po.n)
                key = i * 4 + kg
                if key >= 2:
                    t.wait_ge(ve.h, evac_cnt[key - 2])
                last = None
                for j in range(4):
                    kc = kg * 4 + j
                    last = t.transpose(ps_tr[bb][:, j * 128:(j + 1) * 128],
                                       x_tok[i % 2][:, kc * 128:(kc + 1) * 128], identf[:])
                pe.inc(last)
            TEN(_)
            def _(v, i=i, kg=kg, bb=bb):
                v.wait_ge(pe.h, pe.n)
                src = ps_tr[bb][:].rearrange("p (j c) -> p j c", c=128)
                dst = APr(xT, (kg * 4) * TC + i * 128, [[KC * TC, 128], [TC, 4], [1, 128]])
                ve.inc(v.tensor_copy(dst, src))
                evac_cnt[i * 4 + kg] = ve.n
            VEC(_)
        xtok_done[i] = pe.n
        if i + 2 < TT:
            _emit_xdma(i + 2)

    dq = ExitStack()
    w_sb = P.sb("w_sb", [128, 2048], I32, dq)
    sc_sb = P.sb("sc_sb", [128, 128], F32, dq)
    bc_ = P.sb("dq_c", [128, 2048], I32, dq)
    bc8 = P.sb("dq_c8", [128, 2048], I32, dq)
    bs31 = P.sb("dq_s31", [128, 2048], I32, dq)
    bt1 = P.sb("dq_t1", [128, 2048], I32, dq)
    bu = P.sb("dq_u", [128, 2048], I32, dq)
    deq_sb = P.sb("deq_sb", [128, 4096], BF16, dq)
    wT_sb = P.sb("wT_sb", [128, 4096], BF16, dq)
    ps_wq = [P.ps(f"ps_wq{i}", [128, 512], BF16, dq) for i in range(2)]

    def emit_dq_unit(wait_dma_cnt, Fk, sc_cols, out_planes, fold05):
        NB = Fk // 16
        for plane in range(2):
            def _(v, plane=plane):
                c = bc_[:, 0:Fk]
                c8 = bc8[:, 0:Fk]
                s31 = bs31[:, 0:Fk]
                t1 = bt1[:, 0:Fk]
                u = bu[:, 0:Fk]
                v.wait_ge(qd.h, wait_dma_cnt)
                v.wait_ge(ve.h, ve.n)
                if plane == 0:
                    ve.inc(v.tensor_scalar(c, w_sb[:, 0:Fk], 15, None, ALU.bitwise_and))
                else:
                    ve.inc(v.tensor_scalar(c, w_sb[:, 0:Fk], 4, None, ALU.logical_shift_right))
                v.wait_ge(ve.h, ve.n)
                ve.inc(v.tensor_scalar(c8, c, 7, None, ALU.bitwise_and))
                v.wait_ge(ve.h, ve.n)
                ve.inc(v.tensor_scalar(s31, c, 3, 31, ALU.logical_shift_right, ALU.logical_shift_left))
                v.wait_ge(ve.h, ve.n)
                ve.inc(v.tensor_scalar(t1, c8, -251, 252, ALU.mult, ALU.add))
                v.wait_ge(ve.h, ve.n)
                ve.inc(v.tensor_scalar(t1, t1, 0, None, ALU.max))
                v.wait_ge(ve.h, ve.n)
                ve.inc(v.tensor_scalar(u, c8, 1 << 22, 0x3F000000, ALU.mult, ALU.add))
                v.wait_ge(ve.h, ve.n)
                ve.inc(v.scalar_tensor_tensor(c8, t1, -(1 << 22), u, ALU.mult, ALU.add))
                v.wait_ge(ve.h, ve.n)
                ve.inc(v.tensor_tensor(c8, c8, s31, ALU.bitwise_or))
                v.wait_ge(ve.h, ve.n)
                bits_f = c8.bitcast(F32).rearrange("p (b j) -> p b j", j=16)
                sc_view = sc_sb[:, 0:NB, None].broadcast_to([128, NB, 16])
                out = out_planes[plane]
                if fold05:
                    ve.inc(v.scalar_tensor_tensor(out, bits_f, 0.5, sc_view, ALU.mult, ALU.mult))
                else:
                    ve.inc(v.tensor_tensor(out, bits_f, sc_view, ALU.mult))
            VEC(_)

    def emit_shard_transpose_and_store(n_pieces, dst_fn):
        for grp4 in range((n_pieces + 3) // 4):
            bb = grp4 % 2
            def _(t, grp4=grp4, bb=bb):
                t.wait_ge(ve.h, ve.n)
                last = None
                for j in range(4):
                    pc = grp4 * 4 + j
                    last = t.transpose(ps_wq[bb][:, j * 128:(j + 1) * 128],
                                       deq_sb[:, pc * 128:(pc + 1) * 128], identb[:])
                pe.inc(last)
            TEN(_)
            def _(v, grp4=grp4, bb=bb):
                v.wait_ge(pe.h, pe.n)
                v.wait_ge(gd.h, gd.n)
                ve.inc(v.tensor_copy(wT_sb[:, grp4 * 512:(grp4 + 1) * 512], ps_wq[bb][:]))
            VEC(_)
        for gi in range(n_pieces // 16):
            def _(g, gi=gi):
                g.wait_ge(ve.h, ve.n)
                dst, src = dst_fn(gi)
                gd.inc(g.dma_start(out=dst, in_=src), 16)
            GPS(_)

    att_tiles = [(0, 128), (128, 128), (256, 128), (384, 128), (512, 64)]
    for (r0, nr) in att_tiles:
        def _(q, r0=r0, nr=nr):
            q.wait_ge(ve.h, ve.n)
            qd.inc(q.dma_start(out=x_tok[0][0:nr, :], in_=attnA_in.ap()[r0:r0 + nr, :]), 16)
        SYNC(_)
        def _(v, r0=r0, nr=nr):
            v.wait_ge(qd.h, qd.n)
            v.wait_ge(gd.h, gd.n)
            ve.inc(v.tensor_copy(deq_sb[0:nr, 0:2048], x_tok[0][0:nr, :]))
        VEC(_)
        def _(g, r0=r0, nr=nr):
            g.wait_ge(ve.h, ve.n)
            gd.inc(g.dma_start(out=bncA.ap()[A_ATTN_OFF + r0:A_ATTN_OFF + r0 + nr, :],
                               in_=deq_sb[0:nr, 0:2048]), 16)
        GPS(_)
    for r in range(KVB_SH // 128):
        def _(q, r=r):
            q.wait_ge(ve.h, ve.n)
            qd.inc(q.dma_start(out=x_tok[0][:, 0:512], in_=kvbW_in.ap()[r * 128:(r + 1) * 128, :]), 16)
        SYNC(_)
        def _(v, r=r):
            v.wait_ge(qd.h, qd.n)
            v.wait_ge(gd.h, gd.n)
            ve.inc(v.tensor_copy(deq_sb[:, 0:512], x_tok[0][:, 0:512]))
        VEC(_)
        def _(g, r=r):
            g.wait_ge(ve.h, ve.n)
            gd.inc(g.dma_start(out=APr(bncC, C_DOWN_ROWS * NMLP + r * 128 * 512,
                                       [[512, 128], [1, 512]]),
                               in_=deq_sb[:, 0:512]), 16)
            if _DEV_DUMP and r == 3:
                gd.inc(g.dma_start(out=dbg_xtok.ap(), in_=x_tok[0][:, 0:512]), 16)
                gd.inc(g.dma_start(out=dbg_deq.ap(), in_=deq_sb[:, 0:512]), 16)
                g.wait_ge(gd.h, gd.n)
                gd.inc(g.dma_start(out=dbg_bnc2.ap(),
                                   in_=bncC.ap()[C_DOWN_ROWS + 12:C_DOWN_ROWS + 16, :]), 16)
        GPS(_)

    for hb in range(DOWN_SH // 128):
        for kq in range(4):
            def _(q, hb=hb, kq=kq):
                q.wait_ge(ve.h, ve.n)
                q.dma_start(out=w_sb[:, 0:2048],
                            in_=dpk.ap()[hb * 128:(hb + 1) * 128, kq * 2048:(kq + 1) * 2048]).then_inc(qd.h, 16)
                q.dma_start(out=sc_sb[:, 0:128],
                            in_=dsc.ap()[hb * 128:(hb + 1) * 128, kq * 128:(kq + 1) * 128]).then_inc(qd.h, 16)
                qd.n += 32
            SYNC(_)
            lo = deq_sb[:, 0:2048].rearrange("p (b j) -> p b j", j=16)
            hi = deq_sb[:, 2048:4096].rearrange("p (b j) -> p b j", j=16)
            def _(v):
                v.wait_ge(pe.h, pe.n)
            VEC(_)
            emit_dq_unit(qd.n, 2048, 128, (lo, hi), True)

            def dst_fn(gi, hb=hb, kq=kq):
                n0 = kq * 2048 + (0 if gi == 0 else NMLP // 2)
                dst = APr(bncC, n0 * DOWN_SH + hb * 128,
                          [[DOWN_SH, 128], [128 * DOWN_SH, 16], [1, 128]])
                src = wT_sb[:, gi * 2048:(gi + 1) * 2048].rearrange("p (kc n) -> p kc n", n=128)
                return dst, src
            emit_shard_transpose_and_store(32, dst_fn)

    def _(g):
        g.wait_ge(gd.h, gd.n)
        g.wait_ge(ve.h, ve.n)
        cc.inc(g.collective_compute(
            "AllGather", ALU.bypass,
            replica_groups=[list(range(NCORES))],
            ins=[bncC.ap().opt()], outs=[gathC.ap().opt()]))
    GPS(_)

    for seg, (pk_t, sc_t, blob_row0) in enumerate([(gpk, gsc, 0), (upk, usc, GATE_SH)]):
        for r in range(GATE_SH // 128):
            def _(q, pk_t=pk_t, sc_t=sc_t, r=r):
                q.wait_ge(ve.h, ve.n)
                q.dma_start(out=w_sb[:, 0:1024], in_=pk_t.ap()[r * 128:(r + 1) * 128, :]).then_inc(qd.h, 16)
                q.dma_start(out=sc_sb[:, 0:64], in_=sc_t.ap()[r * 128:(r + 1) * 128, :]).then_inc(qd.h, 16)
                qd.n += 32
            SYNC(_)
            lo = deq_sb[:, 0:1024].rearrange("p (b j) -> p b j", j=16)
            hi = deq_sb[:, 1024:2048].rearrange("p (b j) -> p b j", j=16)
            def _(v):
                v.wait_ge(pe.h, pe.n)
            VEC(_)
            emit_dq_unit(qd.n, 1024, 64, (lo, hi), False)

            def dst_fn(gi, r=r, blob_row0=blob_row0):
                dst = APr(bncA, blob_row0 * D + r * 128,
                          [[D, 128], [128 * D, 16], [1, 128]])
                src = wT_sb[:, 0:2048].rearrange("p (kc n) -> p kc n", n=128)
                return dst, src
            emit_shard_transpose_and_store(16, dst_fn)

    def _(g):
        g.wait_ge(gd.h, gd.n)
        g.wait_ge(ve.h, ve.n)
        cc.inc(g.collective_compute(
            "AllGather", ALU.bypass,
            replica_groups=[list(range(NCORES))],
            ins=[bncA.ap().opt()], outs=[gathA.ap().opt()]))
    GPS(_)

    dq.close()
    phA.close()

    def emit_ln(dst, scr, rstd_t, ps_st):
        ps_sum, ps_sq, ps_mb, ps_rb, rows = ps_st
        for kc in range(KC):
            def _(v, kc=kc):
                if kc == 0:
                    v.wait_ge(ve.h, ve.n)
                    v.wait_ge(pe.h, pe.n)
                    v.wait_ge(gd.h, gd.n)
                    v.wait_ge(ad.h, ad.n)
                else:
                    v.wait_ge(pe.h, pe.n)
                ve.inc(v.tensor_tensor(scr[kc % 2][:], xT[:, kc * TC:kc * TC + TC],
                                       xT[:, kc * TC:kc * TC + TC], ALU.mult))
            VEC(_)
            def _(t, kc=kc):
                t.wait_ge(ve.h, ve.n)
                for h in range(2):
                    t.matmul(ps_sum[0:1, h * 512:(h + 1) * 512], ones_col_f[:],
                             xT[:, kc * TC + h * 512:kc * TC + (h + 1) * 512],
                             start=(kc == 0), stop=(kc == KC - 1))
                last = None
                for h in range(2):
                    last = t.matmul(ps_sq[0:1, h * 512:(h + 1) * 512], ones_col_f[:],
                                    scr[kc % 2][:, h * 512:(h + 1) * 512],
                                    start=(kc == 0), stop=(kc == KC - 1))
                pe.inc(last)
            TEN(_)
        def _(v):
            v.wait_ge(pe.h, pe.n)
            ve.inc(v.tensor_scalar(rows[0][:], ps_sum[0:1, :], 1.0 / D, None, ALU.mult))
            v.wait_ge(ve.h, ve.n)
            ve.inc(v.tensor_scalar(rows[1][:], ps_sq[0:1, :], 1.0 / D, None, ALU.mult))
            v.wait_ge(ve.h, ve.n)
            ve.inc(v.tensor_tensor(rows[2][:], rows[0][:], rows[0][:], ALU.mult))
            v.wait_ge(ve.h, ve.n)
            ve.inc(v.scalar_tensor_tensor(rows[1][:], rows[1][:], 1e-5, rows[2][:], ALU.add, ALU.subtract))
        VEC(_)
        def _(t):
            t.wait_ge(ve.h, ve.n)
            for h in range(2):
                t.matmul(ps_mb[:, h * 512:(h + 1) * 512], ones_row_f[:],
                         rows[0][0:1, h * 512:(h + 1) * 512], start=True, stop=True)
            last = None
            for h in range(2):
                last = t.matmul(ps_rb[:, h * 512:(h + 1) * 512], ones_row_f[:],
                                rows[1][0:1, h * 512:(h + 1) * 512], start=True, stop=True)
            pe.inc(last)
        TEN(_)
        def _(s):
            s.wait_ge(pe.h, pe.n)
            s.wait_ge(ve.h, ve.n)
            se.inc(s.activation(scr[0][:], ps_rb[:], AF.Sqrt))
        SCA(_)
        def _(v):
            v.wait_ge(se.h, se.n)
            ve.inc(v.reciprocal(rstd_t[:], scr[0][:]))
        VEC(_)
        for kc in range(KC):
            def _(v, kc=kc):
                v.wait_ge(pe.h, pe.n)
                v.wait_ge(ve.h, ve.n)
                ve.inc(v.tensor_tensor(scr[kc % 2][:], xT[:, kc * TC:kc * TC + TC], ps_mb[:], ALU.subtract))
                v.wait_ge(ve.h, ve.n)
                ve.inc(v.tensor_tensor(dst[:, kc * TC:kc * TC + TC], scr[kc % 2][:], rstd_t[:], ALU.mult))
            VEC(_)

    def emit_ffn(tag):
        ph = ExitStack()
        xnT = P.sb(f"{tag}_xnT", [128, KC * TC], BF16, ph)
        interT = P.sb(f"{tag}_inter", [128, 16 * TC], BF16, ph)
        scr = [P.sb(f"{tag}_scr{i}", [128, TC], F32, ph) for i in range(2)]
        rows = [P.sb(f"{tag}_row{i}", [1, TC], F32, ph) for i in range(3)]
        wg_sl = [P.sb(f"{tag}_wg{i}", [128, 2048], BF16, ph) for i in range(2)]
        wu_sl = [P.sb(f"{tag}_wu{i}", [128, 2048], BF16, ph) for i in range(2)]
        wd_sl = [P.sb(f"{tag}_wd{i}", [128, 8192], BF16, ph) for i in range(2)]
        rstd_t = P.sb(f"{tag}_rstd", [128, TC], F32, ph)

        lnps = ExitStack()
        ps_st = (P.ps(f"{tag}_ln_sum", [1, TC], F32, lnps),
                 P.ps(f"{tag}_ln_sq", [1, TC], F32, lnps),
                 P.ps(f"{tag}_ln_mb", [128, TC], F32, lnps),
                 P.ps(f"{tag}_ln_rb", [128, TC], F32, lnps),
                 rows)
        emit_ln(xnT, scr, rstd_t, ps_st)
        lnps.close()

        ps_g = P.ps(f"{tag}_ps_g", [128, TC], F32, ph)
        ps_u = [P.ps(f"{tag}_ps_u{i}", [128, TC], F32, ph) for i in range(2)]
        ps_d = [P.ps(f"{tag}_ps_d{i}", [128, 512], F32, ph) for i in range(2)]

        ln_done = ve.n
        g_mm = {}
        gu_mm = {}
        gelu_done = {}
        mult_done = {}
        fetch_cnt = {}
        fetchd_cnt = {}
        d_read = {}
        slabd_mm = {}

        for g in range(8):
            for nci in range(16):
                nabs = g * 16 + nci
                bw = nabs % 2

                def _(q, g=g, nci=nci, nabs=nabs, bw=bw):
                    if nabs >= 2:
                        q.wait_ge(pe.h, gu_mm[nabs - 2])
                    if nabs == 0:
                        q.wait_ge(cc.h, cc.n)
                        q.wait_ge(gd.h, gd.n)
                        q.wait_ge(ad.h, ad.n)
                    if nabs < 2:
                        q.wait_ge(ve.h, ln_done)
                    src_g = APr(gathA, (g * A_BLK) * D + nci * 128,
                                [[D, 128], [128 * D, 16], [1, 128]])
                    src_u = APr(gathA, (g * A_BLK + GATE_SH) * D + nci * 128,
                                [[D, 128], [128 * D, 16], [1, 128]])
                    q.dma_start(out=wg_sl[bw][:].rearrange("p (kc n) -> p kc n", n=128),
                                in_=src_g).then_inc(qd.h, 16)
                    q.dma_start(out=wu_sl[bw][:].rearrange("p (kc n) -> p kc n", n=128),
                                in_=src_u).then_inc(qd.h, 16)
                    qd.n += 32
                    fetch_cnt[nabs] = qd.n
                SYNC(_)

                def _(t, nabs=nabs, bw=bw):
                    t.wait_ge(qd.h, fetch_cnt[nabs])
                    if nabs == 0:
                        t.wait_ge(ve.h, ln_done)
                    if nabs >= 1:
                        t.wait_ge(se.h, gelu_done[nabs - 1])
                    if nabs >= 2:
                        t.wait_ge(ve.h, mult_done[nabs - 2])
                    last = None
                    for kc in range(KC):
                        for h in range(2):
                            last = t.matmul(ps_g[:, h * 512:(h + 1) * 512],
                                            wg_sl[bw][:, kc * 128:(kc + 1) * 128],
                                            xnT[:, kc * TC + h * 512:kc * TC + (h + 1) * 512],
                                            start=(kc == 0), stop=(kc == KC - 1))
                    pe.inc(last)
                    g_mm[nabs] = pe.n
                    for kc in range(KC):
                        for h in range(2):
                            last = t.matmul(ps_u[bw][:, h * 512:(h + 1) * 512],
                                            wu_sl[bw][:, kc * 128:(kc + 1) * 128],
                                            xnT[:, kc * TC + h * 512:kc * TC + (h + 1) * 512],
                                            start=(kc == 0), stop=(kc == KC - 1))
                    pe.inc(last)
                    gu_mm[nabs] = pe.n
                TEN(_)

                def _(s, nabs=nabs, bw=bw):
                    s.wait_ge(pe.h, g_mm[nabs])
                    if nabs >= 2:
                        s.wait_ge(ve.h, mult_done[nabs - 2])
                    else:
                        s.wait_ge(ve.h, ln_done)
                    se.inc(s.activation(scr[bw][:], ps_g[:], AF.Gelu_apprx_tanh))
                    gelu_done[nabs] = se.n
                SCA(_)

                def _(v, nabs=nabs, bw=bw, nci=nci):
                    v.wait_ge(se.h, gelu_done[nabs])
                    v.wait_ge(pe.h, gu_mm[nabs])
                    ve.inc(v.tensor_tensor(interT[:, nci * TC:(nci + 1) * TC],
                                           scr[bw][:], ps_u[bw][:], ALU.mult))
                    mult_done[nabs] = ve.n
                VEC(_)

            for hq in range(4):
                dseq = g * 4 + hq
                bws = dseq % 2

                def _(q, g=g, hq=hq, dseq=dseq, bws=bws):
                    if dseq >= 2:
                        q.wait_ge(pe.h, slabd_mm[dseq - 2])
                    if dseq < 2:
                        q.wait_ge(ve.h, ln_done)
                    for r in range(2):
                        src = APr(gathC,
                                  (hq * 2 + r) * C_BLK * NMLP + (g * 2048) * DOWN_SH,
                                  [[DOWN_SH, 128], [128 * DOWN_SH, 16], [1, 256]])
                        dst = APr(wd_sl[bws], r * 256, [[8192, 128], [512, 16], [1, 256]])
                        q.dma_start(out=dst, in_=src).then_inc(qd.h, 16)
                        qd.n += 16
                    fetchd_cnt[dseq] = qd.n
                SYNC(_)

                for hc in range(4):
                    habs = hq * 4 + hc
                    for th in range(2):
                        sseq = (dseq * 8) + hc * 2 + th
                        bd = sseq % 2

                        def _(t, g=g, hq=hq, hc=hc, th=th, habs=habs, dseq=dseq, bws=bws, bd=bd, sseq=sseq):
                            t.wait_ge(qd.h, fetchd_cnt[dseq])
                            if hc == 0 and th == 0:
                                t.wait_ge(ve.h, mult_done[g * 16 + 15])
                            if sseq >= 2 and (sseq - 2) in d_read:
                                t.wait_ge(ve.h, d_read[sseq - 2])
                            last = None
                            for nci in range(16):
                                last = t.matmul(ps_d[bd][:],
                                                wd_sl[bws][:, nci * 512 + hc * 128:nci * 512 + (hc + 1) * 128],
                                                interT[:, nci * TC + th * 512:nci * TC + (th + 1) * 512],
                                                start=(nci == 0), stop=(nci == 15))
                            pe.inc(last)
                            if hc == 3 and th == 1:
                                slabd_mm[dseq] = pe.n
                        TEN(_)

                        def _(v, habs=habs, th=th, bd=bd, sseq=sseq):
                            v.wait_ge(pe.h, pe.n)
                            ve.inc(v.tensor_tensor(
                                xT[:, habs * TC + th * 512:habs * TC + (th + 1) * 512],
                                xT[:, habs * TC + th * 512:habs * TC + (th + 1) * 512],
                                ps_d[bd][:], ALU.add))
                            d_read[sseq] = ve.n
                        VEC(_)

        ph.close()

    def tdma_attnA(q, dst_tile, dst_col0, row0, nrows, kc):
        done = 0
        n = 0
        while done < nrows:
            r = row0 + done
            blk = r // ATTN_SH
            within = r % ATTN_SH
            glen = min(ATTN_SH - within, nrows - done)
            src = APr(gathA, (blk * A_BLK + A_ATTN_OFF + within) * D + kc * 128,
                      [[D, glen], [1, 128]])
            q.dma_start(out=dst_tile[:, dst_col0 + done:dst_col0 + done + glen],
                        in_=src, transpose=True).then_inc(qd.h, 16)
            qd.n += 16
            done += glen
            n += 1
        return n

    def emit_attn():
        ph = ExitStack()
        cosk = P.sb("at_cosk", [ROPE, SEQ], F32, ph)
        sink = P.sb("at_sink", [ROPE, SEQ], F32, ph)

        s1 = ExitStack()
        cosq = P.sb("at_cosq", [ROPE, TC], F32, s1)
        sinq = P.sb("at_sinq", [ROPE, TC], F32, s1)
        xn2 = P.sb("at_xn", [128, KC * TC], BF16, s1)
        scr = [P.sb(f"at_scr{i}", [128, TC], F32, s1) for i in range(2)]
        rows = [P.sb(f"at_row{i}", [1, TC], F32, s1) for i in range(3)]
        qsl = P.sb("at_qsl", [128, 16 * 512], BF16, s1, is_transpose_dest=True, align_bytes=4096)
        rstd_t = P.sb("at_rstd", [128, TC], F32, s1)
        qt_tile = P.sb("at_qt", [128, TC], BF16, s1)
        tmpa = P.sb("at_tmpa", [32, TC], F32, s1)
        tmpb = P.sb("at_tmpb", [32, TC], F32, s1)
        lat_f = P.sb("at_latf", [128, LC * TC], F32, s1)
        lat_b = qsl

        def _(q):
            q.wait_ge(pe.h, pe.n)
            q.wait_ge(ve.h, ve.n)
            q.wait_ge(se.h, se.n)
            q.wait_ge(gd.h, gd.n)
            q.wait_ge(ad.h, ad.n)
            for dst, srct in ((cosk, cosk_in), (sink, sink_in), (cosq, cosq_in), (sinq, sinq_in)):
                q.dma_start(out=dst[:], in_=srct.ap()).then_inc(qd.h, 16)
                qd.n += 16
        SYNC(_)
        def _(a):
            a.wait_ge(pe.h, pe.n)
            a.wait_ge(ve.h, ve.n)
            a.wait_ge(gd.h, gd.n)
        SCA(_)

        lnps = ExitStack()
        ps_st_ln = (P.ps("at_ln_sum", [1, TC], F32, lnps),
                    P.ps("at_ln_sq", [1, TC], F32, lnps),
                    P.ps("at_ln_mb", [128, TC], F32, lnps),
                    P.ps("at_ln_rb", [128, TC], F32, lnps),
                    rows)
        emit_ln(xn2, scr, rstd_t, ps_st_ln)
        lnps.close()

        qps = ExitStack()
        ps_q = [P.ps(f"at_ps_q{i}", [128, TC], F32, qps) for i in range(2)]
        qfetch = {}
        qmm = {}
        qev = {}
        for qg in range(4):
            def _(q, qg=qg):
                if qg >= 1:
                    q.wait_ge(pe.h, qmm[qg - 1])
                for kc in range(KC):
                    tdma_attnA(q, qsl, kc * 512, qg * 512, 512, kc)
                qfetch[qg] = qd.n
            SYNC(_)
            for qc4 in range(4):
                hidx = qg * 4 + qc4
                bq = hidx % 2
                def _(t, qg=qg, qc4=qc4, hidx=hidx, bq=bq):
                    t.wait_ge(qd.h, qfetch[qg])
                    if qc4 == 0 and qg == 0:
                        t.wait_ge(ve.h, ve.n)
                    if hidx >= 2:
                        t.wait_ge(ve.h, qev[hidx - 2])
                    last = None
                    for kc in range(KC):
                        for hh in range(2):
                            last = t.matmul(ps_q[bq][:, hh * 512:(hh + 1) * 512],
                                            qsl[:, kc * 512 + qc4 * 128:kc * 512 + qc4 * 128 + 128],
                                            xn2[:, kc * TC + hh * 512:kc * TC + (hh + 1) * 512],
                                            start=(kc == 0), stop=(kc == KC - 1))
                    pe.inc(last)
                    qmm[qg] = pe.n
                TEN(_)
                def _(v, hidx=hidx, bq=bq):
                    v.wait_ge(pe.h, pe.n)
                    v.wait_ge(ad.h, ad.n)
                    v.wait_ge(qd.h, qd.n)
                    ve.inc(v.tensor_tensor(tmpa[:], ps_q[bq][0:32, :], cosq[0:32, :], ALU.mult))
                    ve.inc(v.tensor_tensor(tmpb[:], ps_q[bq][32:64, :], sinq[0:32, :], ALU.mult))
                    v.wait_ge(ve.h, ve.n)
                    ve.inc(v.tensor_tensor(qt_tile[0:32, :], tmpa[:], tmpb[:], ALU.subtract))
                    ve.inc(v.tensor_tensor(tmpa[:], ps_q[bq][32:64, :], cosq[32:64, :], ALU.mult))
                    ve.inc(v.tensor_tensor(tmpb[:], ps_q[bq][0:32, :], sinq[32:64, :], ALU.mult))
                    v.wait_ge(ve.h, ve.n)
                    ve.inc(v.tensor_tensor(qt_tile[32:64, :], tmpa[:], tmpb[:], ALU.add))
                    ve.inc(v.tensor_scalar(qt_tile[64:128, :], ps_q[bq][64:128, :], SCALE, None, ALU.mult))
                    qev[hidx] = ve.n
                VEC(_)
                def _(a, hidx=hidx):
                    a.wait_ge(ve.h, qev[hidx])
                    ad.inc(a.dma_start(out=qT_dram.ap()[hidx * 128:(hidx + 1) * 128, :],
                                       in_=qt_tile[:]), 16)
                SCA(_)
        qps.close()

        kps = ExitStack()
        ps_l = [P.ps(f"at_ps_l{i}", [128, TC], F32, kps) for i in range(2)]
        ps_ssq = P.ps("at_ps_ssq", [1, TC], F32, kps)
        ps_rsb = P.ps("at_ps_rsb", [128, TC], F32, kps)
        def _(q):
            q.wait_ge(pe.h, pe.n)
            for kc in range(KC):
                tdma_attnA(q, qsl, kc * 512, D + 0, 512, kc)
        SYNC(_)
        kvafetch = qd.n
        lev = {}
        for lc in range(LC):
            bl = lc % 2
            def _(t, lc=lc, bl=bl):
                t.wait_ge(qd.h, kvafetch)
                if lc >= 2:
                    t.wait_ge(ve.h, lev[lc - 2])
                last = None
                for kc in range(KC):
                    for hh in range(2):
                        last = t.matmul(ps_l[bl][:, hh * 512:(hh + 1) * 512],
                                        qsl[:, kc * 512 + lc * 128:kc * 512 + lc * 128 + 128],
                                        xn2[:, kc * TC + hh * 512:kc * TC + (hh + 1) * 512],
                                        start=(kc == 0), stop=(kc == KC - 1))
                pe.inc(last)
            TEN(_)
            def _(v, lc=lc, bl=bl):
                v.wait_ge(pe.h, pe.n)
                ve.inc(v.tensor_copy(lat_f[:, lc * TC:(lc + 1) * TC], ps_l[bl][:]))
                lev[lc] = ve.n
            VEC(_)
            def _(v, lc=lc):
                v.wait_ge(ve.h, lev[lc])
                ve.inc(v.tensor_tensor(scr[lc % 2][:], lat_f[:, lc * TC:(lc + 1) * TC],
                                       lat_f[:, lc * TC:(lc + 1) * TC], ALU.mult))
            VEC(_)
            def _(t, lc=lc):
                t.wait_ge(ve.h, ve.n)
                last = None
                for hh in range(2):
                    last = t.matmul(ps_ssq[0:1, hh * 512:(hh + 1) * 512], ones_col_f[:],
                                    scr[lc % 2][:, hh * 512:(hh + 1) * 512],
                                    start=(lc == 0), stop=(lc == LC - 1))
                pe.inc(last)
            TEN(_)
        def _(v):
            v.wait_ge(pe.h, pe.n)
            if _DEV_DUMP:
                ve.inc(v.tensor_copy(rows[2][:], ps_ssq[0:1, :]))
            ve.inc(v.tensor_scalar(rows[0][:], ps_ssq[0:1, :], 1.0 / LAT, 1e-6, ALU.mult, ALU.add))
        VEC(_)
        if _DEV_DUMP:
            def _(g):
                g.wait_ge(ve.h, ve.n)
                gd.inc(g.dma_start(out=dbg_chain.ap()[0:1, 0:TC], in_=rows[2][:]), 16)
                gd.inc(g.dma_start(out=dbg_chain.ap()[0:1, TC:2 * TC], in_=rows[0][:]), 16)
            GPS(_)
        def _(t):
            t.wait_ge(ve.h, ve.n)
            last = None
            for hh in range(2):
                last = t.matmul(ps_rsb[:, hh * 512:(hh + 1) * 512], ones_row_f[:],
                                rows[0][0:1, hh * 512:(hh + 1) * 512], start=True, stop=True)
            pe.inc(last)
        TEN(_)
        def _(s):
            s.wait_ge(pe.h, pe.n)
            s.wait_ge(se.h, se.n)
            se.inc(s.activation(scr[0][:], ps_rsb[:], AF.Sqrt))
        SCA(_)
        def _(v):
            v.wait_ge(se.h, se.n)
            ve.inc(v.reciprocal(rstd_t[:], scr[0][:]))
        VEC(_)
        for lc in range(LC):
            def _(v, lc=lc):
                v.wait_ge(ve.h, ve.n)
                ve.inc(v.scalar_tensor_tensor(lat_b[:, lc * TC:(lc + 1) * TC],
                                              lat_f[:, lc * TC:(lc + 1) * TC],
                                              kvn_sb[:, lc:lc + 1], rstd_t[:],
                                              ALU.mult, ALU.mult))
            VEC(_)
        if _DEV_DUMP:
            def _(g):
                g.wait_ge(ve.h, ve.n)
                gd.inc(g.dma_start(out=dbg_latf.ap(), in_=lat_f[:]), 16)
                gd.inc(g.dma_start(out=dbg_row.ap(), in_=rows[0][:]), 16)
            GPS(_)
        def _(g):
            g.wait_ge(ve.h, ve.n)
            dst = latn_my.ap().rearrange("(lc p) t -> p lc t", p=128)
            gd.inc(g.dma_start(out=dst, in_=lat_b[:, 0:LC * TC].rearrange("p (lc t) -> p lc t", t=TC)), 16)
            g.wait_ge(gd.h, gd.n)
            cc.inc(g.collective_compute(
                "AllGather", ALU.bypass,
                replica_groups=[[0, 1, 2, 3], [4, 5, 6, 7]],
                ins=[latn_my.ap().opt()], outs=[latn_all.ap().opt()]))
        GPS(_)
        kps.close()
        s1.close()

        s2 = ExitStack()
        latn_sb = P.sb("at_latn", [128, LC * SEQ], BF16, s2)
        kT = P.sb("at_kT", [128, SEQ], BF16, s2)
        vB = P.sb("at_vB", [128, SEQ], BF16, s2)
        qh = [P.sb(f"at_qh{i}", [128, TC], BF16, s2) for i in range(2)]
        pT = [P.sb(f"at_pT{i}", [128, TC], BF16, s2) for i in range(2)]
        kvbsl = [P.sb(f"at_kvb{i}", [128, LC * 256], BF16, s2) for i in range(2)]
        kvnat = [P.sb(f"at_kvn{i}", [128, 1024], BF16, s2) for i in range(2)]
        ksq = P.sb("at_ksq", [128, SEQ], BF16, s2)
        qsq = P.sb("at_qsq", [128, TC], BF16, s2)
        recip_sb = P.sb("at_recip", [128, TC], F32, s2)
        ht = P.sb("at_ht", [128, TC], BF16, s2)
        ot_sb = P.sb("at_ot", [128, TC], F32, s2)
        biasC = P.sb("at_biasC", [128, 1], F32, s2)
        tmpa2 = P.sb("at_tmpa2", [32, TC], F32, s2)
        tmpb2 = P.sb("at_tmpb2", [32, TC], F32, s2)
        srow = P.sb("at_srow", [1, 8], F32, s2)
        nrow = P.sb("at_nrow", [1, TC], F32, s2)

        def _(g):
            g.wait_ge(cc.h, cc.n)
            if _DEV_DUMP:
                gd.inc(g.dma_start(out=dbg_latn.ap(), in_=latn_all.ap()), 16)
                gd.inc(g.dma_start(out=dbg_bncC.ap(), in_=bncC.ap()[C_DOWN_ROWS:C_DOWN_ROWS + 16, :]), 16)
                gd.inc(g.dma_start(out=dbg_gathC.ap()[0:16, :], in_=gathC.ap()[C_DOWN_ROWS:C_DOWN_ROWS + 16, :]), 16)
                gd.inc(g.dma_start(out=dbg_gathC.ap()[16:32, :], in_=gathC.ap()[C_BLK + C_DOWN_ROWS:C_BLK + C_DOWN_ROWS + 16, :]), 16)
            for lc in range(LC):
                src = APr(latn_all, lc * 128 * TC, [[TC, 128], [LAT * TC, KB], [1, TC]])
                dst = latn_sb[:, lc * SEQ:(lc + 1) * SEQ].rearrange("p (r t) -> p r t", t=TC)
                gd.inc(g.dma_start(out=dst, in_=src), 16)
        GPS(_)

        head_end_pe = {}
        head_exp = {}
        qh_cnt = {}
        kvb_evac = {}
        ot_evac = {}
        rs_read = {}
        for h in range(NH):
            blk = h // 2
            def _(a, h=h):
                if h >= 2:
                    a.wait_ge(pe.h, head_end_pe[h - 2])
                else:
                    a.wait_ge(ve.h, ve.n)
                    a.wait_ge(gd.h, gd.n)
                ad.inc(a.dma_start(out=qh[h % 2][:],
                                   in_=qT_dram.ap()[h * 128:(h + 1) * 128, :]), 16)
                qh_cnt[h] = ad.n
            SCA(_)
            def _(q, h=h, blk=blk):
                if h >= 2:
                    q.wait_ge(pe.h, head_end_pe[h - 2])
                else:
                    q.wait_ge(ve.h, ve.n)
                    q.wait_ge(gd.h, gd.n)
                for t2 in range(2):
                    src = APr(gathC, (blk * C_BLK + C_DOWN_ROWS) * NMLP
                              + ((h % 2) * 256 + t2 * 128) * LAT,
                              [[LAT, 128], [1, 512]])
                    q.dma_start(out=kvnat[h % 2][:, t2 * 512:(t2 + 1) * 512],
                                in_=src).then_inc(qd.h, 16)
                    qd.n += 16
            SYNC(_)
            kvb_cnt = qd.n

            sc1 = ExitStack()
            ps_kv = [P.ps(f"at_kv{h}_{i}", [128, TC], F32, sc1) for i in range(2)]
            ps_v = P.ps(f"at_v{h}", [128, 512], F32, sc1)
            ps_st = P.ps(f"at_st{h}", [1, TC], F32, sc1)
            ps_tp = P.ps(f"at_tp{h}", [128, 512], BF16, sc1)

            for t2 in range(2):
                def _(t, h=h, t2=t2):
                    t.wait_ge(qd.h, kvb_cnt)
                    if h == 0 and t2 == 0:
                        t.wait_ge(gd.h, gd.n)
                        t.wait_ge(ad.h, ad.n)
                    if h >= 1 and t2 == 0:
                        t.wait_ge(se.h, head_exp[h - 1])
                        t.wait_ge(ve.h, rs_read[h - 1])
                    if t2 == 1:
                        t.wait_ge(ve.h, ve.n)
                    last = None
                    for lc in range(LC):
                        last = t.transpose(ps_tp[:, lc * 128:(lc + 1) * 128],
                                           kvnat[h % 2][:, t2 * 512 + lc * 128:t2 * 512 + (lc + 1) * 128],
                                           identb[:])
                    pe.inc(last)
                TEN(_)
                def _(v, h=h, t2=t2):
                    v.wait_ge(pe.h, pe.n)
                    dst = APr(kvbsl[h % 2], t2 * 128, [[1024, 128], [256, 4], [1, 128]])
                    ve.inc(v.tensor_copy(dst, ps_tp[:].rearrange("p (lc n) -> p lc n", n=128)))
                    kvb_evac[h] = ve.n
                VEC(_)

            krope = {}
            for r in range(KB):
                br = r % 2
                def _(t, h=h, r=r, br=br):
                    t.wait_ge(ve.h, kvb_evac[h])
                    if r >= 2:
                        t.wait_ge(ve.h, krope[r - 2])
                    if h >= 1 and r < 2:
                        t.wait_ge(se.h, head_exp[h - 1])
                    last = None
                    for lc in range(LC):
                        for hh in range(2):
                            last = t.matmul(ps_kv[br][:, hh * 512:(hh + 1) * 512],
                                            kvbsl[h % 2][:, lc * 256:lc * 256 + 128],
                                            latn_sb[:, lc * SEQ + r * TC + hh * 512:lc * SEQ + r * TC + (hh + 1) * 512],
                                            start=(lc == 0), stop=(lc == LC - 1))
                    pe.inc(last)
                TEN(_)
                def _(v, r=r, br=br):
                    v.wait_ge(pe.h, pe.n)
                    cols = slice(r * TC, (r + 1) * TC)
                    ve.inc(v.tensor_tensor(tmpa2[:], ps_kv[br][0:32, :], cosk[0:32, cols], ALU.mult))
                    ve.inc(v.tensor_tensor(tmpb2[:], ps_kv[br][32:64, :], sink[0:32, cols], ALU.mult))
                    v.wait_ge(ve.h, ve.n)
                    ve.inc(v.tensor_tensor(kT[0:32, cols], tmpa2[:], tmpb2[:], ALU.subtract))
                    ve.inc(v.tensor_tensor(tmpa2[:], ps_kv[br][32:64, :], cosk[32:64, cols], ALU.mult))
                    ve.inc(v.tensor_tensor(tmpb2[:], ps_kv[br][0:32, :], sink[32:64, cols], ALU.mult))
                    v.wait_ge(ve.h, ve.n)
                    ve.inc(v.tensor_tensor(kT[32:64, cols], tmpa2[:], tmpb2[:], ALU.add))
                    ve.inc(v.tensor_copy(kT[64:128, cols], ps_kv[br][64:128, :]))
                    krope[r] = ve.n
                VEC(_)

            vev = {}
            for kt4 in range(8):
                def _(t, h=h, kt4=kt4):
                    if kt4 == 0 and h >= 1:
                        t.wait_ge(ve.h, ot_evac[h - 1])
                    if kt4 >= 1:
                        t.wait_ge(ve.h, vev[kt4 - 1])
                    last = None
                    for j in range(4):
                        kt = kt4 * 4 + j
                        for lc in range(LC):
                            last = t.matmul(ps_v[:, j * 128:(j + 1) * 128],
                                            latn_sb[:, lc * SEQ + kt * 128:lc * SEQ + (kt + 1) * 128],
                                            kvbsl[h % 2][:, lc * 256 + 128:lc * 256 + 256],
                                            start=(lc == 0), stop=(lc == LC - 1))
                    pe.inc(last)
                TEN(_)
                def _(v, kt4=kt4):
                    v.wait_ge(pe.h, pe.n)
                    ve.inc(v.tensor_copy(vB[:, kt4 * 512:(kt4 + 1) * 512], ps_v[:]))
                    vev[kt4] = ve.n
                VEC(_)

            if _DEV_DUMP and h == 3:
                def _(g):
                    g.wait_ge(ve.h, ve.n)
                    gd.inc(g.dma_start(out=dbg_vB2.ap(), in_=vB[:]), 16)
                    gd.inc(g.dma_start(out=dbg_kvb.ap(), in_=kvbsl[1][:]), 16)
                GPS(_)
                def _(s):
                    s.wait_ge(gd.h, gd.n)
                SCA(_)
            def _(v, h=h):
                v.wait_ge(ve.h, ve.n)
                v.wait_ge(ad.h, qh_cnt[h])
                ve.inc(v.tensor_tensor(ksq[:], kT[:], kT[:], ALU.mult))
                ve.inc(v.tensor_tensor(qsq[:], qh[h % 2][:], qh[h % 2][:], ALU.mult))
            VEC(_)
            def _(v):
                v.wait_ge(ve.h, ve.n)
                ve.inc(v.memset(srow[:], -1e30))
            VEC(_)
            for rr in range(5):
                def _(t, rr=rr, h=h):
                    t.wait_ge(ve.h, ve.n)
                    if rr == 0 and h >= 1:
                        t.wait_ge(ve.h, ot_evac[h - 1])
                    last = None
                    for jj in range(2):
                        i = rr * 2 + jj
                        srccol = (ksq[:, i * 512:(i + 1) * 512] if i < 8
                                  else qsq[:, (i - 8) * 512:(i - 7) * 512])
                        last = t.matmul(ps_st[0:1, jj * 512:(jj + 1) * 512],
                                        ones_col_b[:], srccol, start=True, stop=True)
                    pe.inc(last)
                TEN(_)
                def _(v, rr=rr):
                    v.wait_ge(pe.h, pe.n)
                    for jj in range(2):
                        i = rr * 2 + jj
                        j = 0 if i < 8 else 1
                        ve.inc(v.reduce_max(nrow[0:1, 0:1], ps_st[0:1, jj * 512:(jj + 1) * 512], axis=AX))
                        v.wait_ge(ve.h, ve.n)
                        ve.inc(v.tensor_tensor(srow[0:1, j:j + 1], srow[0:1, j:j + 1],
                                               nrow[0:1, 0:1], ALU.max))
                        v.wait_ge(ve.h, ve.n)
                VEC(_)
            def _(v):
                v.wait_ge(ve.h, ve.n)
                ve.inc(v.tensor_scalar(srow[0:1, 2:3], srow[0:1, 0:1], -0.5, None, ALU.mult))
                v.wait_ge(ve.h, ve.n)
                ve.inc(v.scalar_tensor_tensor(srow[0:1, 3:4], srow[0:1, 1:2], -0.5,
                                              srow[0:1, 2:3], ALU.mult, ALU.add))
            VEC(_)
            def _(t):
                t.wait_ge(ve.h, ve.n)
                pe.inc(t.matmul(ps_v[:, 0:1], ones_row_f[:], srow[0:1, 3:4], start=True, stop=True))
            TEN(_)
            def _(v):
                v.wait_ge(pe.h, pe.n)
                ve.inc(v.tensor_copy(biasC[:], ps_v[:, 0:1]))
            VEC(_)
            sc1.close()

            if _DEV_DUMP and h == 3:
                def _(g):
                    g.wait_ge(ve.h, ve.n)
                    gd.inc(g.dma_start(out=dbg_kT.ap(), in_=kT[:]), 16)
                    gd.inc(g.dma_start(out=dbg_vB.ap(), in_=vB[:]), 16)
                GPS(_)
            sc2 = ExitStack()
            ps_sT = [P.ps(f"at_sT{h}_{i}", [128, TC], F32, sc2) for i in range(2)]
            ps_OT = P.ps(f"at_OT{h}", [128, TC], F32, sc2)
            ps_rs = P.ps(f"at_rs{h}", [1, TC], F32, sc2)
            sT_mm = {}
            exp_done = {}
            av_mm = {}
            _peb = pe.n
            _c = _peb
            for kt in range(32):
                _c += 1
                sT_mm[kt] = _c
                if kt >= 2:
                    _c += 1
                    av_mm[kt - 2] = _c
            av_mm[30] = _c + 1
            av_mm[31] = _c + 2
            _seb = se.n
            for kt in range(32):
                exp_done[kt] = _seb + kt + 1

            def _(t, h=h):
                for kt in range(32):
                    bk = kt % 2
                    if kt == 0:
                        t.wait_ge(ve.h, ve.n)
                        t.wait_ge(ad.h, qh_cnt[h])
                    if kt >= 2:
                        t.wait_ge(se.h, exp_done[kt - 2])
                    last = None
                    for hh in range(2):
                        last = t.matmul(ps_sT[bk][:, hh * 512:(hh + 1) * 512],
                                        kT[:, kt * 128:(kt + 1) * 128],
                                        qh[h % 2][:, hh * 512:(hh + 1) * 512],
                                        start=True, stop=True)
                    pe.inc(last)
                    assert pe.n == sT_mm[kt]
                    if kt >= 2:
                        kk = kt - 2
                        bb2 = kk % 2
                        t.wait_ge(se.h, exp_done[kk])
                        for hh in range(2):
                            t.matmul(ps_OT[:, hh * 512:(hh + 1) * 512],
                                     vB[:, kk * 128:(kk + 1) * 128],
                                     pT[bb2][:, hh * 512:(hh + 1) * 512],
                                     start=(kk == 0), stop=(kk == 31))
                        last = None
                        for hh in range(2):
                            last = t.matmul(ps_rs[0:1, hh * 512:(hh + 1) * 512],
                                            ones_col_b[:],
                                            pT[bb2][:, hh * 512:(hh + 1) * 512],
                                            start=(kk == 0), stop=(kk == 31))
                        pe.inc(last)
                        assert pe.n == av_mm[kk]
                for kk in (30, 31):
                    bb2 = kk % 2
                    t.wait_ge(se.h, exp_done[kk])
                    for hh in range(2):
                        t.matmul(ps_OT[:, hh * 512:(hh + 1) * 512],
                                 vB[:, kk * 128:(kk + 1) * 128],
                                 pT[bb2][:, hh * 512:(hh + 1) * 512],
                                 start=(kk == 0), stop=(kk == 31))
                    last = None
                    for hh in range(2):
                        last = t.matmul(ps_rs[0:1, hh * 512:(hh + 1) * 512],
                                        ones_col_b[:],
                                        pT[bb2][:, hh * 512:(hh + 1) * 512],
                                        start=(kk == 0), stop=(kk == 31))
                    pe.inc(last)
                    assert pe.n == av_mm[kk]
            TEN(_)

            def _(s, h=h):
                for kt in range(32):
                    bk = kt % 2
                    s.wait_ge(pe.h, sT_mm[kt])
                    if kt >= 2:
                        s.wait_ge(pe.h, av_mm.get(kt - 2, 0))
                    se.inc(s.activation(pT[bk][:], ps_sT[bk][:], AF.Exp, bias=biasC[:, 0:1]))
                    assert se.n == exp_done[kt]
            SCA(_)
            head_exp[h] = se.n

            def _(v, h=h):
                v.wait_ge(pe.h, av_mm[31])
                ve.inc(v.tensor_copy(ot_sb[:], ps_OT[:]))
                ot_evac[h] = ve.n
                v.wait_ge(ve.h, ve.n)
                ve.inc(v.reciprocal(nrow[:], ps_rs[0:1, :]))
                rs_read[h] = ve.n
            VEC(_)
            def _(t, h=h):
                t.wait_ge(ve.h, rs_read[h])
                t.wait_ge(se.h, head_exp[h])
                last = None
                for hh in range(2):
                    last = t.matmul(ps_sT[0][:, hh * 512:(hh + 1) * 512], ones_row_f[:],
                                    nrow[0:1, hh * 512:(hh + 1) * 512], start=True, stop=True)
                pe.inc(last)
            TEN(_)
            def _(v, h=h):
                v.wait_ge(pe.h, pe.n)
                v.wait_ge(gd.h, gd.n)
                ve.inc(v.tensor_copy(recip_sb[:], ps_sT[0][:]))
                v.wait_ge(ve.h, ve.n)
                ve.inc(v.tensor_tensor(ht[:], ot_sb[:], recip_sb[:], ALU.mult))
            VEC(_)
            if _DEV_DUMP:
                def _(g, h=h):
                    g.wait_ge(ve.h, ve.n)
                    gd.inc(g.dma_start(out=dbg_stats.ap()[h:h + 1, :], in_=srow[:]), 16)
                GPS(_)
            def _(g, h=h):
                g.wait_ge(ve.h, ve.n)
                gd.inc(g.dma_start(out=heads_dram.ap()[h * 128:(h + 1) * 128, :], in_=ht[:]), 16)
            GPS(_)
            sc2.close()
            head_end_pe[h] = pe.n
        s2.close()

        s3 = ExitStack()
        heads_sb = P.sb("at_heads", [128, NH * TC], BF16, s3)
        osl = [P.sb(f"at_osl{i}", [128, 16 * 512], BF16, s3, is_transpose_dest=True, align_bytes=4096) for i in range(2)]
        ps_o = [P.ps(f"at_ps_o{i}", [128, TC], F32, s3) for i in range(2)]
        def _(g):
            g.wait_ge(gd.h, gd.n)
            g.wait_ge(pe.h, pe.n)
            g.wait_ge(ve.h, ve.n)
            g.wait_ge(se.h, se.n)
            src = heads_dram.ap().rearrange("(hh p) t -> p hh t", p=128)
            gd.inc(g.dma_start(out=heads_sb[:].rearrange("p (hh t) -> p hh t", t=TC), in_=src), 16)
        GPS(_)
        ofetch = {}
        omm = {}
        oev = {}
        for og in range(4):
            def _(q, og=og):
                if og >= 2:
                    q.wait_ge(pe.h, omm[og - 2])
                else:
                    q.wait_ge(pe.h, pe.n)
                    q.wait_ge(ve.h, ve.n)
                    q.wait_ge(se.h, se.n)
                    q.wait_ge(gd.h, gd.n)
                for odc in range(KC):
                    tdma_attnA(q, osl[og % 2], odc * 512, D + LAT + og * 512, 512, odc)
                ofetch[og] = qd.n
            SYNC(_)
            for oc4 in range(4):
                oc = og * 4 + oc4
                bo = oc % 2
                def _(t, og=og, oc4=oc4, oc=oc, bo=bo):
                    t.wait_ge(qd.h, ofetch[og])
                    if oc == 0:
                        t.wait_ge(gd.h, gd.n)
                    if oc >= 2:
                        t.wait_ge(ve.h, oev[oc - 2])
                    last = None
                    for odc in range(KC):
                        for hh in range(2):
                            last = t.matmul(ps_o[bo][:, hh * 512:(hh + 1) * 512],
                                            osl[og % 2][:, odc * 512 + oc4 * 128:odc * 512 + oc4 * 128 + 128],
                                            heads_sb[:, odc * TC + hh * 512:odc * TC + (hh + 1) * 512],
                                            start=(odc == 0), stop=(odc == KC - 1))
                    pe.inc(last)
                    omm[og] = pe.n
                TEN(_)
                def _(v, oc=oc, bo=bo):
                    v.wait_ge(pe.h, pe.n)
                    ve.inc(v.tensor_tensor(xT[:, oc * TC:(oc + 1) * TC],
                                           xT[:, oc * TC:(oc + 1) * TC],
                                           ps_o[bo][:], ALU.add))
                    oev[oc] = ve.n
                VEC(_)
        s3.close()
        ph.close()

    if stop in ("ffn1", "attn", "full"):
        emit_ffn("f1")
    if stop in ("attn", "full"):
        emit_attn()
    if stop == "full":
        emit_ffn("f2")

    phE = ExitStack()
    xo_sb = P.sb("xo_sb", [128, D], F32, phE)
    ps_e = [P.ps(f"ps_e{i}", [128, 512], F32, phE) for i in range(2)]
    for tt in range(TT):
        for kg in range(4):
            bb = (tt * 4 + kg) % 2
            def _(t, tt=tt, kg=kg, bb=bb):
                if tt == 0 and kg == 0:
                    t.wait_ge(ve.h, ve.n)
                    t.wait_ge(cc.h, cc.n)
                t.wait_ge(ve.h, ve.n)
                last = None
                for j in range(4):
                    kc = kg * 4 + j
                    last = t.transpose(ps_e[bb][:, j * 128:(j + 1) * 128],
                                       xT[:, kc * TC + tt * 128:kc * TC + (tt + 1) * 128],
                                       identf[:])
                pe.inc(last)
            TEN(_)
            def _(v, tt=tt, kg=kg, bb=bb):
                v.wait_ge(pe.h, pe.n)
                if kg == 0:
                    v.wait_ge(ad.h, ad.n)
                    v.wait_ge(gd.h, gd.n)
                ve.inc(v.tensor_copy(xo_sb[:, kg * 512:(kg + 1) * 512], ps_e[bb][:]))
            VEC(_)
        def _(q, tt=tt):
            q.wait_ge(ve.h, ve.n)
            ad.inc(q.dma_start(out=y_out.ap()[tt * 128:(tt + 1) * 128, :], in_=xo_sb[:]), 16)
        SCA(_)
    def _(q):
        q.wait_ge(ad.h, ad.n)
    SCA(_)
    phE.close()

    P.st.close()
    return nc


def kernel(**inputs):
    x = np.asarray(inputs["L0_attn_hidden_states"], np.float32)
    q_w = np.asarray(inputs["L0_attn_q_w"], np.float32)
    kv_a_w = np.asarray(inputs["L0_attn_kv_a_w"], np.float32)
    kv_b_w = np.asarray(inputs["L0_attn_kv_b_w"], np.float32)
    o_w = np.asarray(inputs["L0_attn_o_w"], np.float32)
    kv_norm_w = np.asarray(inputs["L0_attn_kv_norm_w"], np.float32)
    g_pk = np.asarray(inputs["L0_ffn_gate_W_packed"], np.int32)
    g_sc = np.asarray(inputs["L0_ffn_gate_scales"], np.float32)
    u_pk = np.asarray(inputs["L0_ffn_up_W_packed"], np.int32)
    u_sc = np.asarray(inputs["L0_ffn_up_scales"], np.float32)
    d_pk = np.asarray(inputs["L0_ffn_down_W_packed"], np.int32)
    d_sc = np.asarray(inputs["L0_ffn_down_scales"], np.float32)

    Pd = _perm(D)
    Pn = _perm(NMLP)

    xf = np.ascontiguousarray(x.reshape(T, D)[:, Pd])
    q_wp = np.ascontiguousarray(q_w[:, Pd])
    kva_p = np.ascontiguousarray(kv_a_w[ROPE:, Pd])
    o_wp = np.ascontiguousarray(o_w[Pd, :])
    attn_stack = np.concatenate([q_wp, kva_p, o_wp], axis=0)
    g_pkp = np.ascontiguousarray(g_pk[Pn]); g_scp = np.ascontiguousarray(g_sc[Pn])
    u_pkp = np.ascontiguousarray(u_pk[Pn]); u_scp = np.ascontiguousarray(u_sc[Pn])
    d_pkp = np.ascontiguousarray(d_pk[Pd]); d_scp = np.ascontiguousarray(d_sc[Pd])
    kvn_dev = np.ascontiguousarray(kv_norm_w.reshape(LC, 128).T)

    cos, sin = _yarn_cos_sin()
    cosk = np.ascontiguousarray(cos.T)
    sink = np.ascontiguousarray(sin.T)
    cosq_s = (cos.T * np.float32(SCALE)).astype(np.float32)
    sinq_s = (sin.T * np.float32(SCALE)).astype(np.float32)

    import ml_dtypes
    identb = np.eye(128, dtype=ml_dtypes.bfloat16)
    identf = np.eye(128, dtype=np.float32)

    in_maps = []
    for c in range(NCORES):
        sq = (c % GRP) * TC
        in_maps.append({
            "x_in": np.ascontiguousarray(xf[c * TC:(c + 1) * TC]),
            "gpk": np.ascontiguousarray(g_pkp[c * GATE_SH:(c + 1) * GATE_SH]),
            "gsc": np.ascontiguousarray(g_scp[c * GATE_SH:(c + 1) * GATE_SH]),
            "upk": np.ascontiguousarray(u_pkp[c * GATE_SH:(c + 1) * GATE_SH]),
            "usc": np.ascontiguousarray(u_scp[c * GATE_SH:(c + 1) * GATE_SH]),
            "dpk": np.ascontiguousarray(d_pkp[c * DOWN_SH:(c + 1) * DOWN_SH]),
            "dsc": np.ascontiguousarray(d_scp[c * DOWN_SH:(c + 1) * DOWN_SH]),
            "attnA": np.ascontiguousarray(attn_stack[c * ATTN_SH:(c + 1) * ATTN_SH]),
            "kvbW": np.ascontiguousarray(kv_b_w[c * KVB_SH:(c + 1) * KVB_SH]),
            "kvnorm": kvn_dev,
            "cosq": np.ascontiguousarray(cosq_s[:, sq:sq + TC]),
            "sinq": np.ascontiguousarray(sinq_s[:, sq:sq + TC]),
            "cosk": cosk,
            "sink": sink,
            "identb": identb,
            "identf": identf,
        })

    nc = _build()
    res = run_bass_kernel_spmd(nc, in_maps, core_ids=list(range(NCORES)))

    out = np.empty((T, D), np.float32)
    for c in range(NCORES):
        out[c * TC:(c + 1) * TC] = res.results[c]["y"]
    inv = np.empty((T, D), np.float32)
    inv[:, Pd] = out
    return inv.reshape(B, S, D)
